# version 17
# baseline (speedup 1.0000x reference)
import numpy as np
from contextlib import ExitStack
import concourse.bass as bass
import concourse.mybir as mybir

F32 = mybir.dt.float32
BF16 = mybir.dt.bfloat16
I32 = mybir.dt.int32
AF = mybir.ActivationFunctionType
ALU = mybir.AluOpType
AX = mybir.AxisListType

T, C, H, D, L = 4096, 1024, 16, 64, 128
NCH = T // L
KC = C // 128
NT = T // 128


def host_consts():
    import ml_dtypes
    c = {}
    c["ident"] = np.eye(128, dtype=ml_dtypes.bfloat16)
    c["identf64"] = np.eye(64, dtype=np.float32)
    f = np.arange(128, dtype=np.float32)
    c["iota_pf"] = (f[:, None] - f[None, :]).astype(np.float32)
    c["iota_f"] = np.broadcast_to(f[None, :], (128, 128)).astype(np.float32).copy()
    c["m_lowstrict"] = (f[None, :] < f[:, None]).astype(np.float32)
    c["m_upstrict"] = (f[None, :] > f[:, None]).astype(np.float32)
    c["m_upincl"] = (f[None, :] >= f[:, None]).astype(np.float32)
    c["ones16f"] = np.ones((16, 128), np.float32)
    c["ones"] = np.ones((128, 128), dtype=ml_dtypes.bfloat16)
    c["hsel2"] = np.zeros((16, 128), dtype=np.float32)
    for h in range(16):
        c["hsel2"][h, (h % 2) * 64:(h % 2) * 64 + 64] = 1.0
    c["halfsel"] = np.zeros((16, 8), dtype=np.float32)
    for h in range(16):
        c["halfsel"][h, h // 2] = 1.0
    half = D // 2
    i = np.arange(half, dtype=np.float32)
    alpha = 1.0 / 10000.0 ** (2.0 * i / D)
    ang = np.arange(T, dtype=np.float32)[None, :] * alpha[:, None]
    c["ropecos"] = np.tile(np.cos(ang), (4, 1)).astype(np.float32)
    c["ropesin"] = np.tile(np.sin(ang), (4, 1)).astype(np.float32)
    s = np.arange(128)[:, None]
    q = np.arange(512)[None, :]
    mt = np.concatenate([(q >= s + 128 * j).astype(np.float32) for j in range(4)], axis=1)
    c["maskT4"] = mt.astype(ml_dtypes.bfloat16)
    fc = np.zeros((128, 8), np.float32)
    fc[:, 0] = 1e-5
    fc[:, 1] = -10.0
    fc[:, 2] = 1.0
    fc[:, 3] = 128.0
    c["fconst"] = fc
    hs = np.zeros((128, 2), np.float32)
    hs[0:64, 0] = 1.0
    hs[64:128, 1] = 1.0
    c["hselcol"] = hs.astype(ml_dtypes.bfloat16)
    bc = np.zeros((2, 128), np.float32)
    bc[0, 0:64] = 1.0
    bc[1, 64:128] = 1.0
    c["bc2"] = bc.astype(ml_dtypes.bfloat16)
    c["onesrow"] = np.ones((1, 512), dtype=ml_dtypes.bfloat16)
    return c

VEC_LAYOUT = {}

def host_vecs(inp):
    VEC_LAYOUT.clear()
    cols = []
    def add(name, v, ncols):
        v = np.asarray(v, dtype=np.float32).reshape(-1)
        col0 = sum(x.shape[1] for x in cols)
        VEC_LAYOUT[name] = (col0, ncols)
        if v.size == 16:
            pad = np.zeros((128, 1), np.float32)
            pad[:16, 0] = v
            cols.append(pad)
        else:
            assert v.size == ncols * 128
            cols.append(v.reshape(ncols, 128).T.copy())
    add("b_r", inp["L0_attn_b_r"], 8); add("b_k", inp["L0_attn_b_k"], 8)
    add("b_v", inp["L0_attn_b_v"], 8); add("b_g", inp["L0_attn_b_g"], 8)
    add("b_o", inp["L0_attn_b_o"], 8)
    add("gnw", inp["L0_attn_gn_weight"], 8); add("gnb", inp["L0_attn_gn_bias"], 8)
    add("tmr", inp["L0_attn_time_mix_r"], 8); add("tmk", inp["L0_attn_time_mix_k"], 8)
    add("tmv", inp["L0_attn_time_mix_v"], 8); add("tmg", inp["L0_attn_time_mix_g"], 8)
    add("s1", inp["L0_ffn_w1_s"], 32); add("s2", inp["L0_ffn_w2_s"], 32)
    add("sh1", inp["L0_ffn_h1_s"], 32); add("sh2", inp["L0_ffn_h2_s"], 32)
    add("s3", inp["L0_ffn_w3_s"], 8)
    add("bq", inp["L1_attn_bq"], 8); add("bk1", inp["L1_attn_bk"], 8)
    add("bv1", inp["L1_attn_bv"], 8); add("bp", inp["L1_attn_bp"], 8)
    add("decay", inp["L0_attn_decay"], 1); add("eta", inp["L0_attn_eta"], 1)
    return np.concatenate(cols, axis=1)

def host_weights(inp):
    w = {}
    for nm, src in [("wrT", "L0_attn_W_r"), ("wkT", "L0_attn_W_k"), ("wvT", "L0_attn_W_v"),
                    ("wgT", "L0_attn_W_g"), ("woT", "L0_attn_W_o"),
                    ("w1T", "L0_ffn_w1_w"), ("w2T", "L0_ffn_w2_w"),
                    ("h1T", "L0_ffn_h1_w"), ("h2T", "L0_ffn_h2_w"), ("w3T", "L0_ffn_w3_w"),
                    ("wqT", "L1_attn_wq"), ("wk1T", "L1_attn_wk"), ("wv1T", "L1_attn_wv"),
                    ("wpT", "L1_attn_wp")]:
        w[nm] = np.ascontiguousarray(np.asarray(inp[src]).T)
    import ml_dtypes
    brow = np.concatenate([np.asarray(inp[k], np.float32).reshape(1, -1) for k in
                           ("L0_attn_b_r", "L0_attn_b_k", "L0_attn_b_v", "L0_attn_b_g",
                            "L0_attn_b_o", "L1_attn_bq", "L1_attn_bk", "L1_attn_bv",
                            "L1_attn_bp")], axis=1)
    w["brow"] = brow.astype(ml_dtypes.bfloat16)
    w["gateT"] = np.ascontiguousarray(np.asarray(inp["L1_ffn_w_gate"]).T)
    w["upT"] = np.ascontiguousarray(np.asarray(inp["L1_ffn_w_up"]).T)
    w["downT"] = np.ascontiguousarray(np.asarray(inp["L1_ffn_w_down"]).T)
    return w

CONST_SPECS = {
    "ident": ([128, 128], BF16), "identf64": ([64, 64], F32),
    "iota_pf": ([128, 128], F32), "iota_f": ([128, 128], F32),
    "m_lowstrict": ([128, 128], F32), "m_upincl": ([128, 128], F32),
    "m_upstrict": ([128, 128], F32), "ones16f": ([16, 128], F32),
    "ones": ([128, 128], BF16), "hsel2": ([16, 128], F32), "halfsel": ([16, 8], F32),
    "ropecos": ([128, T], F32), "ropesin": ([128, T], F32), "maskT4": ([128, 2048], BF16),
    "fconst": ([128, 8], F32), "hselcol": ([128, 2], BF16), "bc2": ([2, 128], BF16),
    "onesrow": ([1, 512], BF16),
}
WEIGHT_SPECS = {
    "wrT": ([C, C], F32), "wkT": ([C, C], F32), "wvT": ([C, C], F32),
    "wgT": ([C, C], F32), "woT": ([C, C], F32),
    "w1T": ([C, 4096], F32), "w2T": ([C, 4096], F32),
    "h1T": ([4096, 4096], F32), "h2T": ([4096, 4096], F32), "w3T": ([4096, C], F32),
    "wqT": ([C, C], F32), "wk1T": ([C, C], F32), "wv1T": ([C, C], F32), "wpT": ([C, C], F32),
    "gateT": ([C, 3072], I32), "upT": ([C, 3072], I32), "downT": ([3072, C], I32),
    "brow": ([1, 9216], BF16),
}
SCRATCH_SPECS = {
    "xn_fm": ([C, T], BF16), "r_fm": ([C, T], BF16), "k_fm": ([C, T], BF16),
    "v_fm": ([C, T], BF16), "g0_fm": ([C, T], BF16), "wkv_fm": ([C, T], BF16),
    "stats": ([2, 16], F32), "s_fm": ([C, T], BF16), "gf_fm": ([C, T], BF16),
    "x1_tm": ([T, C], F32), "xn2_fm": ([C, T], BF16), "s2_fm": ([C, T], BF16),
    "o_scr": ([C, T], BF16),
}
PHASE_IO = {
    "A": (["x"], ["xn_fm"]),
    "B": (["xn_fm", "wrT", "wkT", "wvT", "wgT", "brow"], ["r_fm", "k_fm", "v_fm", "g0_fm"]),
    "C": (["r_fm", "k_fm", "v_fm"], ["wkv_fm", "stats"]),
    "D": (["wkv_fm", "stats", "g0_fm", "woT", "brow"], ["s_fm"]),
    "E": (["xn_fm", "w1T", "w2T", "h1T", "h2T", "w3T"], ["gf_fm"]),
    "F": (["x", "s_fm", "gf_fm"], ["x1_tm", "xn2_fm"]),
    "G": (["xn2_fm", "wqT", "wk1T", "wv1T", "wpT", "brow"], ["s2_fm", "o_scr"]),
    "H": (["xn2_fm", "x1_tm", "s2_fm", "gateT", "upT", "downT"], ["y"]),
}

class Sem:
    __slots__ = ("h", "n")
    def __init__(self, h):
        self.h = h
        self.n = 0
    def inc(self, ins, k=1):
        ins.then_inc(self.h, k)
        self.n += k
        return ins

class Ctx:
    def __init__(self, nc, blk, stack, d, nvec):
        self.nc, self.blk, self.stack, self.d, self.nvec = nc, blk, stack, d, nvec
        self.sems = {}
        for name in ("dml", "dmg", "dms", "mm", "ve", "ac", "ev", "gp", "pb", "aux", "aux2", "aux3"):
            self.sems[name] = Sem(stack.enter_context(nc.semaphore(name)))
        self.pb_target = 0
        self.sb = {}

    def s(self, name):
        return self.sems[name]

def vcol(name, j=0):
    c0, n = VEC_LAYOUT[name]
    assert j < n, (name, j, n)
    return c0 + j

LAST_INPUTS = []

def build_nc(phases, standalone=True, expose=()):
    nvec = sum(n for _, n in VEC_LAYOUT.values())
    nc = bass.Bass("TRN2", target_bir_lowering=False)
    used_in, used_out = set(), set()
    for p in phases:
        i, o = PHASE_IO[p]
        used_in |= set(i)
        used_out |= set(o)
    d = {}
    LAST_INPUTS.clear()
    def din(name, shape, dt):
        LAST_INPUTS.append(name)
        return nc.dram_tensor(name, shape, dt, kind="ExternalInput").ap()
    if "x" in used_in:
        d["x"] = din("x", [T, C], F32)
    if "y" in used_out:
        d["y"] = nc.dram_tensor("y", [T, C], F32, kind="ExternalOutput").ap()
    for name, (shape, dt) in WEIGHT_SPECS.items():
        if name in used_in:
            d[name] = din(name, shape, dt)
    for name, (shape, dt) in CONST_SPECS.items():
        d[name] = din(name, shape, dt)
    d["vecs"] = din("vecs", [128, nvec], F32)
    for name, (shape, dt) in SCRATCH_SPECS.items():
        if name in (used_in | used_out):
            produced = name in used_out
            consumed = name in used_in
            if standalone or name in expose:
                kind = "ExternalInput" if (consumed and not produced) else "ExternalOutput"
            else:
                kind = "Internal"
            if kind == "ExternalInput":
                LAST_INPUTS.append(name)
            d[name] = nc.dram_tensor(name, shape, dt, kind=kind).ap()
    stack = ExitStack()
    with stack:
        with nc.Block() as blk:
            ctx = Ctx(nc, blk, stack, d, nvec)
            prelude(ctx)
            for p in phases:
                globals()[f"phase_{p}"](ctx)
            finale(ctx)
    return nc

PHASE_LOCAL_CONSTS = ("ropecos", "ropesin", "maskT4", "iota_pf", "iota_f",
                      "m_lowstrict", "m_upstrict", "m_upincl", "identf64")

def prelude(ctx):
    nc, blk, d = ctx.nc, ctx.blk, ctx.d
    sb = ctx.sb
    names = [n for n in CONST_SPECS if n not in PHASE_LOCAL_CONSTS]
    for name in names:
        shape, dt = CONST_SPECS[name]
        sb[name] = ctx.stack.enter_context(nc.sbuf_tensor("c_" + name, shape, dt))
    sb["vecs"] = ctx.stack.enter_context(nc.sbuf_tensor("c_vecs", [128, ctx.nvec], F32))
    if "brow" in d:
        sb["brow_sb"] = ctx.stack.enter_context(nc.sbuf_tensor("c_brow", [1, 9216], BF16))
    dml = ctx.s("dml")
    sb["acwarm"] = ctx.stack.enter_context(nc.sbuf_tensor("c_acwarm", [128, 8], F32))
    sb["vwarm"] = ctx.stack.enter_context(nc.sbuf_tensor("c_vwarm", [128, 512], F32))
    vwb = ctx.stack.enter_context(nc.sbuf_tensor("c_vwb", [128, 512], BF16))
    vwb2 = ctx.stack.enter_context(nc.sbuf_tensor("c_vwb2", [128, 512], BF16))
    _warm_es = ExitStack()
    warm_ps = _warm_es.enter_context(nc.psum_tensor("c_warmps", [128, 512], F32))
    warm_psb = _warm_es.enter_context(nc.psum_tensor("c_warmpsb", [128, 128], BF16))
    wsem = ctx.stack.enter_context(nc.semaphore("warmsem"))

    @blk.tensor
    def _(tensor):
        tensor.wait_ge(wsem, 1)
        for i in range(8):
            tensor.matmul(out=warm_ps[:, 0:128], lhsT=vwb[:, 0:128], rhs=vwb[:, 0:128],
                          start=True, stop=True)
        tensor.matmul(out=warm_ps[:, 0:128], lhsT=sb["vwarm"][:, 0:128], rhs=sb["vwarm"][:, 0:128],
                      start=True, stop=True)
        for _ in range(3):
            tensor.matmul(out=warm_ps[0:16, 0:1], lhsT=sb["ones16f"][0:1, 0:16], rhs=sb["ones16f"][0:1, 0:1],
                          start=True, stop=True)
            tensor.matmul(out=warm_ps[:, 0:8], lhsT=sb["ones16f"][:, :], rhs=sb["halfsel"][:, 0:8],
                          start=True, stop=True)
            tensor.matmul(out=warm_ps[:, 0:8], lhsT=sb["hsel2"][:, :], rhs=sb["halfsel"][:, :],
                          start=True, stop=True)
        tensor.transpose(out=warm_psb[:, :], in_=vwb[:, 0:128], identity=vwb2[:, 0:128])
        tensor.nop().then_inc(wsem, 1)
        ctx.s("pb").inc(tensor.nop(), 1)

    @blk.vector
    def _(vector):
        v = vector
        for _warmrep in range(2):
            v.memset(sb["vwarm"][:, :], 1.0)
            v.memset(vwb[:, :], 1.0)
            v.memset(vwb2[:, :], 1.0)
            v.tensor_copy(out=vwb[:, :], in_=sb["vwarm"][:, :])
            v.tensor_copy(out=sb["vwarm"][:, :], in_=vwb[:, :])
            v.tensor_copy(out=vwb2[:, :], in_=vwb[:, :])
            v.tensor_add(out=sb["vwarm"][:, :], in0=sb["vwarm"][:, :], in1=sb["vwarm"][:, 0:512])
            v.tensor_mul(out=vwb2[:, :], in0=vwb[:, :], in1=vwb2[:, :])
            v.tensor_sub(out=vwb2[:, :], in0=vwb[:, :], in1=vwb2[:, :])
            v.tensor_mul(out=sb["vwarm"][:, :], in0=sb["vwarm"][:, :], in1=vwb[:, :])
            v.tensor_scalar(out=vwb2[:, :], in0=vwb[:, :], scalar1=1.0, scalar2=0.0, op0=ALU.mult, op1=ALU.add)
            v.tensor_scalar(out=vwb2[:, :], in0=vwb[:, :], scalar1=sb["vwarm"][:, 0:1], scalar2=sb["vwarm"][:, 1:2], op0=ALU.mult, op1=ALU.add)
            v.tensor_scalar(out=vwb2[:, :], in0=vwb[:, :], scalar1=sb["vwarm"][:, 0:1], scalar2=sb["vwarm"][:, 1:2], op0=ALU.subtract, op1=ALU.mult)
            v.tensor_scalar(out=sb["vwarm"][:, :], in0=sb["vwarm"][:, :], scalar1=-1.0, scalar2=sb["vwarm"][:, 1:2], op0=ALU.mult, op1=ALU.add)
            v.tensor_scalar_mul(out=sb["vwarm"][:, :], in0=sb["vwarm"][:, :], scalar1=0.5)
            v.tensor_scalar_mul(out=vwb2[:, :], in0=vwb[:, :], scalar1=sb["vwarm"][:, 0:1])
            v.tensor_scalar_max(out=sb["vwarm"][:, :], in0=sb["vwarm"][:, :], scalar1=1e-12)
            v.scalar_tensor_tensor(out=vwb2[:, :], in0=vwb[:, :], scalar=sb["vwarm"][:, 0:1], in1=vwb2[:, :], op0=ALU.mult, op1=ALU.add)
            v.scalar_tensor_tensor(out=sb["vwarm"][:, :], in0=sb["vwarm"][:, :], scalar=-1.0, in1=sb["vwarm"][:, 0:512], op0=ALU.mult, op1=ALU.mult)
            v.reduce_sum(out=sb["vwarm"][:, 0:1], in_=sb["vwarm"][:, :], axis=AX.X)
            v.reduce_max(out=sb["vwarm"][:, 1:2], in_=sb["vwarm"][:, :], axis=AX.X)
            v.reciprocal(out=sb["vwarm"][:, 2:3], in_=sb["vwarm"][:, 0:1])
            with nc.allow_low_precision(reason="warm"):
                v.reciprocal(out=vwb2[:, 0:1], in_=sb["vwarm"][:, 0:1])
        v.nop().then_inc(wsem, 1)
        v.wait_ge(wsem, 2)
        v.tensor_copy(out=sb["vwarm"][:, 0:128], in_=warm_ps[:, 0:128])
        v.tensor_copy(out=vwb[:, 0:128], in_=warm_ps[:, 0:128])
        v.tensor_add(out=sb["vwarm"][:, 0:128], in0=warm_ps[:, 0:128], in1=sb["vwarm"][:, 0:128])
        v.tensor_mul(out=vwb2[:, 0:128], in0=warm_ps[:, 0:128], in1=vwb[:, 0:128])
        v.tensor_mul(out=vwb2[:, 0:128], in0=vwb[:, 0:128], in1=warm_ps[:, 0:128])
        v.tensor_scalar(out=vwb2[:, 0:128], in0=warm_ps[:, 0:128], scalar1=1.0, scalar2=0.0, op0=ALU.mult, op1=ALU.add)
        v.scalar_tensor_tensor(out=vwb2[:, 0:128], in0=warm_ps[:, 0:128], scalar=sb["vwarm"][:, 0:1], in1=vwb2[:, 0:128], op0=ALU.mult, op1=ALU.add)
        v.tensor_copy(out=vwb2[:, 0:128], in_=warm_psb[:, :])
        v.wait_ge(wsem, 3)
        v.reduce_sum(out=sb["vwarm"][:, 5:6], in_=vwb2[:, 0:256], axis=AX.X)
        v.nop().then_inc(wsem, 1)
        v.wait_ge(wsem, 5)
        v.reduce_sum(out=sb["vwarm"][:, 5:6], in_=vwb2[:, 0:256], axis=AX.X)
        ctx.s("pb").inc(v.nop(), 1)

    @blk.gpsimd
    def _(gpsimd):
        gpsimd.wait_ge(wsem, 1)
        for _ in range(4):
            gpsimd.tensor_mul(out=vwb2[:, 256:384], in0=vwb[:, 256:384], in1=vwb2[:, 256:384])
            gpsimd.tensor_copy(out=vwb2[:, 384:512], in_=vwb[:, 384:512])
        ctx.s("pb").inc(gpsimd.nop(), 1)

    @blk.scalar
    def _(scalar):
        import os as _os
        scalar.wait_ge(wsem, 2)
        scalar.copy(out=sb["vwarm"][:, 3:4], in_=sb["vwarm"][:, 0:1])
        scalar.copy(out=vwb[:, 3:4], in_=warm_ps[:, 0:1])
        scalar.activation(out=sb["vwarm"][:, 4:5], in_=warm_ps[:, 0:1], func=AF.Silu, scale=sb["vwarm"][:, 0:1])
        scalar.activation(out=sb["vwarm"][:, 4:5], in_=warm_ps[:, 0:1], func=AF.Exp, scale=0.125, bias=sb["fconst"][:, 1:2])
        for rep in range(4):
            for f in (AF.Sigmoid, AF.Silu, AF.Exp, AF.Ln, AF.Gelu, AF.Sqrt, AF.Square):
                scalar.activation(out=sb["acwarm"][:, 0:1], in_=sb["acwarm"][:, 1:2], func=f)
        for _ in range(int(_os.environ.get("KB_ACT_STALL_NOPS", "0"))):
            scalar.nop(cycle_cnt=60000)
        for f in (AF.Sigmoid, AF.Silu, AF.Exp, AF.Ln, AF.Gelu, AF.Sqrt):
            scalar.activation(out=sb["acwarm"][:, 0:1], in_=sb["acwarm"][:, 1:2], func=f)
        for _ in range(int(_os.environ.get("KB_ACT_STALL2", "0"))):
            scalar.nop(cycle_cnt=60000)
        scalar.activation(out=vwb2[:, 0:256], in_=sb["vwarm"][:, 0:256], func=AF.Square)
        scalar.nop().then_inc(wsem, 1)
        scalar.wait_ge(wsem, 4)
        scalar.activation(out=vwb2[:, 0:256], in_=sb["vwarm"][:, 0:256], func=AF.Square)
        scalar.nop().then_inc(wsem, 1)
        scalar.wait_ge(dml.h, dml.n + 16 * (len(names) + 1 + (1 if "brow" in d else 0)))
        ctx.s("pb").inc(scalar.nop(), 1)

    @blk.sync
    def _(sync):
        import os
        for _ in range(int(os.environ.get("KB_WARMUP_NOPS", "0"))):
            sync.nop(cycle_cnt=60000)
        for name in names:
            dml.inc(sync.dma_start(out=sb[name][:, :], in_=d[name]), 16)
        dml.inc(sync.dma_start(out=sb["vecs"][:, :], in_=d["vecs"]), 16)
        if "brow" in d:
            dml.inc(sync.dma_start(out=sb["brow_sb"][:, :], in_=d["brow"]), 16)
        sync.wait_ge(dml.h, dml.n)
        ctx.s("pb").inc(sync.nop(), 1)
    ctx.pb_target += 5
    _warm_es.close()

def finale(ctx):
    blk = ctx.blk
    dms, dmg, dml = ctx.s("dms"), ctx.s("dmg"), ctx.s("dml")
    pb = ctx.s("pb")
    tgt = ctx.pb_target

    @blk.sync
    def _(sync):
        sync.wait_ge(pb.h, tgt)
        sync.wait_ge(dms.h, dms.n)
        sync.wait_ge(dmg.h, dmg.n)
        sync.wait_ge(dml.h, dml.n)

def _barrier(ctx, engines):
    ctx.pb_target += len(engines)

def _entry(eng, ctx):
    eng.wait_ge(ctx.s("pb").h, ctx.pb_target)


def phase_A(ctx):
    import os
    dbg = bool(int(os.environ.get("KB_DBG_A", "0")))
    nc, blk, d = ctx.nc, ctx.blk, ctx.d
    if dbg:
        d["dbg_xin0"] = nc.dram_tensor("dbg_xin0", [128, C], F32, kind="ExternalOutput").ap()
        d["dbg_st0"] = nc.dram_tensor("dbg_st0", [128, 8], F32, kind="ExternalOutput").ap()
        d["dbg_fc"] = nc.dram_tensor("dbg_fc", [128, 8], F32, kind="ExternalOutput").ap()
        d["dbg_xsq0"] = nc.dram_tensor("dbg_xsq0", [128, C], F32, kind="ExternalOutput").ap()
        d["dbg_xnbf0"] = nc.dram_tensor("dbg_xnbf0", [128, C], BF16, kind="ExternalOutput").ap()
    dml, dms, mm, ve, ac, ev = (ctx.s(n) for n in ("dml", "dms", "mm", "ve", "ac", "ev"))
    b_dml, b_dms, b_mm, b_ve, b_ac, b_ev = dml.n, dms.n, mm.n, ve.n, ac.n, ev.n
    pbt = ctx.pb_target
    pb = ctx.s("pb")
    ident = ctx.sb["ident"]
    redo = bool(int(os.environ.get("KB_A_REDO", "1")))
    tiles = list(range(NT)) + ([0] if redo else [])
    NPOS = len(tiles)
    es = ExitStack()
    xin = [es.enter_context(nc.sbuf_tensor(f"pa_xin{j}", [128, C], F32)) for j in range(2)]
    xsq = [es.enter_context(nc.sbuf_tensor(f"pa_xsq{j}", [128, C], F32)) for j in range(2)]
    st = es.enter_context(nc.sbuf_tensor("pa_st", [128, 8], F32))
    S1, S2, MEAN, EX2, M2, VAR, SD, RSTD = range(8)
    nmr = es.enter_context(nc.sbuf_tensor("pa_nmr", [128, 1], F32))
    touch = es.enter_context(nc.sbuf_tensor("pa_touch", [128, 1], F32))
    vtouch = es.enter_context(nc.sbuf_tensor("pa_vtouch", [128, 1], BF16))
    xnbf = [es.enter_context(nc.sbuf_tensor(f"pa_xnbf{j}", [128, C], BF16)) for j in range(2)]
    stage = [es.enter_context(nc.sbuf_tensor(f"pa_stage{j}", [128, C], BF16)) for j in range(2)]
    pst = [es.enter_context(nc.psum_tensor(f"pa_pst{j}", [128, 128], BF16)) for j in range(2)]
    with es:
        @blk.sync
        def _(sync):
            _entry(sync, ctx)
            for i in range(NPOS):
                ti = tiles[i]
                if i >= 2:
                    sync.wait_ge(ve.h, b_ve + 2 * (i - 2) + 2)
                    sync.wait_ge(ac.h, b_ac + 2 * (i - 2) + 1)
                dml.inc(sync.dma_start(out=xin[i % 2][:, :],
                                       in_=d["x"][ti * 128:(ti + 1) * 128, :]), 16)
                if dbg and i == 0:
                    sync.wait_ge(dml.h, b_dml + 16)
                    dms.inc(sync.dma_start(out=d["dbg_xin0"], in_=xin[0][:, :]), 16)
                    dms.inc(sync.dma_start(out=d["dbg_fc"], in_=ctx.sb["fconst"][:, :]), 16)
                    sync.wait_ge(ve.h, b_ve + 2)
                    dms.inc(sync.dma_start(out=d["dbg_st0"], in_=st[:, :]), 16)
                    dms.inc(sync.dma_start(out=d["dbg_xsq0"], in_=xsq[0][:, :]), 16)
                    dms.inc(sync.dma_start(out=d["dbg_xnbf0"], in_=xnbf[0][:, :]), 16)
                if i >= 1:
                    j = i - 1
                    tj = tiles[j]
                    sync.wait_ge(ev.h, b_ev + 8 * (j + 1))
                    dms.inc(sync.dma_start(
                        out=d["xn_fm"].rearrange("(k p) t -> p k t", p=128)[:, :, tj * 128:(tj + 1) * 128],
                        in_=stage[j % 2].rearrange("p (k n) -> p k n", n=128)), 16)
            j = NPOS - 1
            tj = tiles[j]
            sync.wait_ge(ev.h, b_ev + 8 * (j + 1))
            dms.inc(sync.dma_start(
                out=d["xn_fm"].rearrange("(k p) t -> p k t", p=128)[:, :, tj * 128:(tj + 1) * 128],
                in_=stage[j % 2].rearrange("p (k n) -> p k n", n=128)), 16)
            sync.wait_ge(dms.h, b_dms + 16 * NPOS)
            pb.inc(sync.nop(), 1)

        @blk.vector
        def _(vector):
            _entry(vector, ctx)
            for i in range(NPOS):
                p = i % 2
                if dbg and i == 1:
                    vector.wait_ge(dms.h, b_dms + 16 * 5)
                vector.wait_ge(dml.h, b_dml + 16 * (i + 1))
                vector.reduce_sum(out=st[:, S1:S1 + 1], in_=xin[p][:, :], axis=AX.X)
                vector.wait_ge(ac.h, b_ac + 2 * i + 1)
                vector.reduce_sum(out=st[:, S2:S2 + 1], in_=xsq[p][:, :], axis=AX.X)
                vector.tensor_scalar_mul(out=st[:, MEAN:MEAN + 1], in0=st[:, S1:S1 + 1], scalar1=1.0 / C)
                vector.tensor_scalar_mul(out=st[:, EX2:EX2 + 1], in0=st[:, S2:S2 + 1], scalar1=1.0 / C)
                vector.tensor_mul(out=st[:, M2:M2 + 1], in0=st[:, MEAN:MEAN + 1], in1=st[:, MEAN:MEAN + 1])
                ve.inc(vector.tensor_sub(out=st[:, VAR:VAR + 1], in0=st[:, EX2:EX2 + 1], in1=st[:, M2:M2 + 1]), 1)
                vector.wait_ge(ac.h, b_ac + 2 * i + 2)
                vector.reciprocal(out=st[:, RSTD:RSTD + 1], in_=st[:, SD:SD + 1])
                if i >= 2:
                    vector.wait_ge(mm.h, b_mm + 8 * (i - 2) + 8)
                vector.tensor_scalar(out=xnbf[p][:, :], in0=xin[p][:, :],
                                     scalar1=st[:, MEAN:MEAN + 1], scalar2=st[:, RSTD:RSTD + 1],
                                     op0=ALU.subtract, op1=ALU.mult)
                ve.inc(vector.tensor_copy(out=vtouch[:, :], in_=xnbf[p][:, 0:1]), 1)
                if i >= 1:
                    jj = i - 1
                    if jj >= 2:
                        vector.wait_ge(dms.h, b_dms + 16 * (jj - 1))
                    for k in range(KC):
                        vector.wait_ge(mm.h, b_mm + 8 * jj + k + 1)
                        ev.inc(vector.tensor_copy(out=stage[jj % 2][:, k * 128:(k + 1) * 128],
                                                  in_=pst[(8 * jj + k) % 2][:, :]), 1)
            jj = NPOS - 1
            vector.wait_ge(dms.h, b_dms + 16 * (jj - 1))
            for k in range(KC):
                vector.wait_ge(mm.h, b_mm + 8 * jj + k + 1)
                ev.inc(vector.tensor_copy(out=stage[jj % 2][:, k * 128:(k + 1) * 128],
                                          in_=pst[(8 * jj + k) % 2][:, :]), 1)
            pb.inc(vector.nop(), 1)

        @blk.scalar
        def _(scalar):
            _entry(scalar, ctx)
            for i in range(NPOS):
                p = i % 2
                scalar.wait_ge(dml.h, b_dml + 16 * (i + 1))
                if i >= 2:
                    scalar.wait_ge(ve.h, b_ve + 2 * (i - 2) + 1)
                scalar.activation(out=xsq[p][:, :], in_=xin[p][:, :], func=AF.Square)
                ac.inc(scalar.activation(out=touch[:, :], in_=xsq[p][:, 0:1], func=AF.Square), 1)
                scalar.wait_ge(ve.h, b_ve + 2 * i + 1)
                scalar.activation(out=st[:, SD:SD + 1], in_=st[:, VAR:VAR + 1],
                                  func=AF.Sqrt, bias=ctx.sb["fconst"][:, 0:1])
                ac.inc(scalar.activation(out=touch[:, :], in_=st[:, SD:SD + 1], func=AF.Square), 1)
            pb.inc(scalar.nop(), 1)

        @blk.tensor
        def _(tensor):
            _entry(tensor, ctx)
            for i in range(NPOS):
                p = i % 2
                tensor.wait_ge(ve.h, b_ve + 2 * i + 2)
                for k in range(KC):
                    t = 8 * i + k
                    if t >= 2:
                        tensor.wait_ge(ev.h, b_ev + t - 1)
                    mm.inc(tensor.transpose(out=pst[t % 2][:, :],
                                            in_=xnbf[p][:, k * 128:(k + 1) * 128],
                                            identity=ident[:, :]), 1)
            pb.inc(tensor.nop(), 1)

    _barrier(ctx, ["sync", "vector", "scalar", "tensor"])
    assert dml.n == b_dml + 16 * NPOS
    assert ac.n == b_ac + 2 * NPOS
    assert ve.n == b_ve + 2 * NPOS
    assert mm.n == b_mm + 8 * NPOS
    assert ev.n == b_ev + 8 * NPOS



BROW = {"b_r": 0, "b_k": 1, "b_v": 2, "b_g": 3, "b_o": 4, "bq": 5, "bk1": 6, "bv1": 7, "bp": 8}

def phase_B(ctx):
    nc, blk, d, sb = ctx.nc, ctx.blk, ctx.d, ctx.sb
    dml, dmg, dms, mm, ve, ac, ev = (ctx.s(n) for n in ("dml", "dmg", "dms", "mm", "ve", "ac", "ev"))
    b = {s: ctx.s(s).n for s in ("dml", "dmg", "dms", "mm", "ve", "ac", "ev")}
    pbt = ctx.pb_target
    pb = ctx.s("pb")
    TT8 = 8
    MM_T = 4 * 8 * 1 + 16 + 16
    VE_T = 1 + 16 + 16
    AC_T = 32 + 16
    DMS_T = 4 * 16
    units = [(p, m) for p in ("r", "k", "v", "g") for m in range(8)]
    WNAME = {"r": "wrT", "k": "wkT", "v": "wvT", "g": "wgT"}
    BIDX = {"r": BROW["b_r"], "k": BROW["b_k"], "v": BROW["b_v"], "g": BROW["b_g"]}
    kv_units = [j for j, (p, _) in enumerate(units) if p in ("k", "v")]
    kvpos = {j: idx for idx, j in enumerate(kv_units)}

    es = ExitStack()
    wsb = {p: es.enter_context(nc.sbuf_tensor(f"pb_w{p}", [128, 8 * 1024], BF16))
           for p in ("r", "k", "v", "g")}
    xnt = [es.enter_context(nc.sbuf_tensor(f"pb_xnt{j}", [128, 8 * 513], BF16)) for j in range(2)]
    dt1 = es.enter_context(nc.sbuf_tensor("pb_dt1", [128, 512], BF16))
    xp = {p: es.enter_context(nc.sbuf_tensor(f"pb_xp{p}", [128, 8 * 512], BF16))
          for p in ("r", "k", "v", "g")}
    obuf = {p: es.enter_context(nc.sbuf_tensor(f"pb_o{p}", [128, 8 * 512], BF16))
            for p in ("r", "k", "v", "g")}
    raw = {p: es.enter_context(nc.sbuf_tensor(f"pb_raw{p}", [128, 8 * 512], BF16)) for p in ("k", "v")}
    sqb = {p: es.enter_context(nc.sbuf_tensor(f"pb_sq{p}", [128, 8 * 512], BF16)) for p in ("k", "v")}
    sqn = es.enter_context(nc.sbuf_tensor("pb_sqn", [2, 512], F32))
    rnb = es.enter_context(nc.sbuf_tensor("pb_rnb", [2, 512], BF16))
    psA = [es.enter_context(nc.psum_tensor(f"pb_psA{j}", [128, 512], F32)) for j in range(4)]
    psN = [es.enter_context(nc.psum_tensor(f"pb_psN{j}", [2, 512], F32)) for j in range(2)]
    psB = [es.enter_context(nc.psum_tensor(f"pb_psB{j}", [128, 512], F32)) for j in range(2)]
    SCR = {"r": "r_fm", "k": "k_fm", "v": "v_fm", "g": "g0_fm"}

    def unit_mm_done(i, j):
        return b["mm"] + MM_T * i + (j + 1)
    def sumsq_done(i, jj):
        return b["mm"] + MM_T * i + 32 + 2 * jj + 1
    def bcast_done(i, jj):
        return b["mm"] + MM_T * i + 32 + 2 * jj + 2
    def evac_done(i, j):
        return b["ac"] + AC_T * i + (j + 1)
    def sqrt_done(i, jj):
        return b["ac"] + AC_T * i + 32 + jj + 1
    def tmix_done(i):
        return b["ve"] + VE_T * i + 1
    def recip_done(i, jj):
        return b["ve"] + VE_T * i + 1 + 2 * jj + 1
    def knorm_done(i, jj):
        return b["ve"] + VE_T * i + 1 + 2 * jj + 2

    with es:
        @blk.gpsimd
        def _(gpsimd):
            _entry(gpsimd, ctx)
            for p in ("r", "k", "v", "g"):
                dmg.inc(gpsimd.dma_start(
                    out=wsb[p].rearrange("p (k n) -> p k n", n=1024),
                    in_=d[WNAME[p]].rearrange("(k p) n -> p k n", p=128)), 16)
            gpsimd.wait_ge(dmg.h, b["dmg"] + 64)
            pb.inc(gpsimd.nop(), 1)

        @blk.sync
        def _(sync):
            _entry(sync, ctx)
            for i in range(TT8):
                if i >= 2:
                    sync.wait_ge(ve.h, tmix_done(i - 2))
                t0 = i * 512
                if i == 0:
                    dml.inc(sync.dma_start(
                        out=xnt[0].rearrange("p (k n) -> p k n", n=513)[:, :, 1:513],
                        in_=d["xn_fm"].rearrange("(k p) t -> p k t", p=128)[:, :, 0:512]), 16)
                else:
                    dml.inc(sync.dma_start(
                        out=xnt[i % 2].rearrange("p (k n) -> p k n", n=513),
                        in_=d["xn_fm"].rearrange("(k p) t -> p k t", p=128)[:, :, t0 - 1:t0 + 512]), 16)
                for p in ("r", "g"):
                    j = units.index((p, 7))
                    sync.wait_ge(ac.h, evac_done(i, j))
                    dms.inc(sync.dma_start(
                        out=d[SCR[p]].rearrange("(k p) t -> p k t", p=128)[:, :, t0:t0 + 512],
                        in_=obuf[p].rearrange("p (k n) -> p k n", n=512)), 16)
                for p in ("k", "v"):
                    jj = kvpos[units.index((p, 7))]
                    sync.wait_ge(ve.h, knorm_done(i, jj))
                    dms.inc(sync.dma_start(
                        out=d[SCR[p]].rearrange("(k p) t -> p k t", p=128)[:, :, t0:t0 + 512],
                        in_=obuf[p].rearrange("p (k n) -> p k n", n=512)), 16)
            sync.wait_ge(dms.h, b["dms"] + DMS_T * TT8)
            pb.inc(sync.nop(), 1)

        @blk.vector
        def _(vector):
            _entry(vector, ctx)
            for i in range(TT8):
                vector.wait_ge(dml.h, b["dml"] + 16 * (i + 1))
                if i >= 1:
                    vector.wait_ge(mm.h, b["mm"] + MM_T * (i - 1) + 32)
                xv = xnt[i % 2]
                if i == 0:
                    for k in range(KC):
                        vector.memset(xv[:, k * 513:k * 513 + 1], 0.0)
                last = None
                for k in range(KC):
                    vector.tensor_sub(out=dt1[:, :],
                                      in0=xv[:, k * 513 + 1:k * 513 + 513],
                                      in1=xv[:, k * 513:k * 513 + 512])
                    for p, tm in (("r", "tmr"), ("k", "tmk"), ("v", "tmv"), ("g", "tmg")):
                        last = vector.scalar_tensor_tensor(
                            out=xp[p][:, k * 512:(k + 1) * 512],
                            in0=dt1[:, :],
                            scalar=sb["vecs"][:, vcol(tm, k):vcol(tm, k) + 1],
                            in1=xv[:, k * 513:k * 513 + 512],
                            op0=ALU.mult, op1=ALU.add)
                ve.inc(last, 1)
                for jj in range(16):
                    j = kv_units[jj]
                    p, m = units[j]
                    if i >= 1 and m == 0:
                        vector.wait_ge(dms.h, b["dms"] + DMS_T * (i - 1) + 16 * (3 if p == "k" else 4))
                    vector.wait_ge(ac.h, sqrt_done(i, jj))
                    vector.tensor_scalar_max(out=sqn[:, :], in0=sqn[:, :], scalar1=1e-12)
                    with nc.allow_low_precision(reason="l2norm recip bf16 operand"):
                        ve.inc(vector.reciprocal(out=rnb[:, :], in_=sqn[:, :]), 1)
                    vector.wait_ge(mm.h, bcast_done(i, jj))
                    ve.inc(vector.tensor_tensor(
                        out=obuf[p][:, m * 512:(m + 1) * 512],
                        in0=raw[p][:, m * 512:(m + 1) * 512],
                        in1=psB[jj % 2][:, :], op=ALU.mult), 1)
            pb.inc(vector.nop(), 1)

        @blk.scalar
        def _(scalar):
            _entry(scalar, ctx)
            for i in range(TT8):
                for j, (p, m) in enumerate(units):
                    scalar.wait_ge(mm.h, unit_mm_done(i, j))
                    ps = psA[j % 4]
                    sl = slice(m * 512, (m + 1) * 512)
                    if i >= 1 and p in ("r", "g"):
                        scalar.wait_ge(dms.h, b["dms"] + DMS_T * (i - 1) + 16 * (2 if p == "g" else 1))
                    if p == "r":
                        ac.inc(scalar.activation(out=obuf["r"][:, sl], in_=ps[:, :], func=AF.Sigmoid), 1)
                    elif p == "g":
                        ac.inc(scalar.activation(out=obuf["g"][:, sl], in_=ps[:, :], func=AF.Silu), 1)
                    else:
                        scalar.activation(out=raw[p][:, sl], in_=ps[:, :], func=AF.Copy)
                        ac.inc(scalar.activation(out=sqb[p][:, sl], in_=ps[:, :], func=AF.Square), 1)
                for jj in range(16):
                    scalar.wait_ge(mm.h, sumsq_done(i, jj))
                    if jj >= 1:
                        scalar.wait_ge(ve.h, recip_done(i, jj - 1))
                    elif i >= 1:
                        scalar.wait_ge(ve.h, recip_done(i - 1, 15))
                    ac.inc(scalar.activation(out=sqn[:, :], in_=psN[jj % 2][:, :], func=AF.Sqrt), 1)
            pb.inc(scalar.nop(), 1)

        @blk.tensor
        def _(tensor):
            _entry(tensor, ctx)
            tensor.wait_ge(dmg.h, b["dmg"] + 64)
            for i in range(TT8):
                tensor.wait_ge(ve.h, tmix_done(i))
                for j, (p, m) in enumerate(units):
                    ps = psA[j % 4]
                    if i > 0 or j >= 4:
                        pi, pj = (i, j - 4) if j >= 4 else (i - 1, j + 28)
                        tensor.wait_ge(ac.h, evac_done(pi, pj))
                    for k in range(KC):
                        tensor.matmul(out=ps[:, :],
                                      lhsT=wsb[p][:, k * 1024 + m * 128:k * 1024 + (m + 1) * 128],
                                      rhs=xp[p][:, k * 512:(k + 1) * 512],
                                      start=(k == 0), stop=False)
                    mm.inc(tensor.matmul(out=ps[:, :],
                                         lhsT=sb["brow_sb"][0:1, BIDX[p] * 1024 + m * 128:BIDX[p] * 1024 + (m + 1) * 128],
                                         rhs=sb["onesrow"][0:1, :],
                                         start=False, stop=True), 1)
                for jj in range(16):
                    j = kv_units[jj]
                    p, m = units[j]
                    tensor.wait_ge(ac.h, evac_done(i, j))
                    if jj >= 2:
                        tensor.wait_ge(ac.h, sqrt_done(i, jj - 2))
                    mm.inc(tensor.matmul(out=psN[jj % 2][:, :], lhsT=sb["hselcol"][:, :],
                                         rhs=sqb[p][:, m * 512:(m + 1) * 512],
                                         start=True, stop=True), 1)
                    tensor.wait_ge(ve.h, recip_done(i, jj))
                    if jj >= 2:
                        tensor.wait_ge(ve.h, knorm_done(i, jj - 2))
                    mm.inc(tensor.matmul(out=psB[jj % 2][:, :], lhsT=sb["bc2"][:, :],
                                         rhs=rnb[:, :], start=True, stop=True), 1)
            pb.inc(tensor.nop(), 1)

    _barrier(ctx, ["sync", "vector", "scalar", "tensor", "gpsimd"])
    assert mm.n == b["mm"] + MM_T * TT8, (mm.n, b["mm"] + MM_T * TT8)
    assert ve.n == b["ve"] + VE_T * TT8
    assert ac.n == b["ac"] + AC_T * TT8
    assert dms.n == b["dms"] + DMS_T * TT8


ENG_SEM = {"tensor": "mm", "vector": "ve", "scalar": "ac", "sync": "dml", "gpsimd": "dmg"}
DMA_ENGS = ("sync", "gpsimd")

class Step:
    __slots__ = ("eng", "seq", "fn", "deps")
    def __init__(self, eng, seq, fn, deps):
        self.eng, self.seq, self.fn, self.deps = eng, seq, fn, deps


def _phase_par(name):
    import os as _o
    return name in _o.environ.get("KB_PAR_PHASES", "CDEFG")

class Script:
    def __init__(self, ctx, serial=None, serial_first=0):
        import os as _os
        self.ctx = ctx
        self.steps = {e: [] for e in ENG_SEM}
        self.all = []
        if serial is None:
            serial = not bool(int(_os.environ.get("KB_PARALLEL", "0")))
        self.serial = serial
        self.serial_first = serial_first

    def add(self, eng, fn, deps=()):
        deps = list(deps)
        if self.all and (self.serial is True or len(self.all) < self.serial_first
                         or (isinstance(self.serial, str) and eng in self.serial.split(","))):
            deps.append(self.all[-1])
        s = Step(eng, len(self.steps[eng]), fn, [d for d in deps if d is not None and d.eng != eng])
        self.steps[eng].append(s)
        self.all.append(s)
        return s

    def emit(self, pb_engines=("sync", "gpsimd", "vector", "scalar", "tensor")):
        ctx = self.ctx
        blk = ctx.blk
        base = {e: ctx.s(ENG_SEM[e]).n for e in ENG_SEM}
        stepv = {e: (16 if e in DMA_ENGS else 1) for e in ENG_SEM}
        needed = {e: set() for e in ENG_SEM}
        for e in DMA_ENGS:
            needed[e] = set(range(len(self.steps[e])))
        for e in ENG_SEM:
            for s in self.steps[e]:
                for dp in s.deps:
                    needed[dp.eng].add(dp.seq)
        incval = {}
        for e in ENG_SEM:
            incval[e] = {}
            for i, seq in enumerate(sorted(needed[e])):
                incval[e][seq] = (i + 1) * stepv[e]
        pbt = ctx.pb_target
        for eng in ENG_SEM:
            steps = self.steps[eng]
            if not steps and eng not in pb_engines:
                continue
            def body(engine, steps=steps, eng=eng):
                _entry(engine, ctx)
                waited = {}
                for s in steps:
                    fresh_mm_wait = False
                    for dp in s.deps:
                        tgt = base[dp.eng] + incval[dp.eng][dp.seq]
                        if waited.get(dp.eng, -1) < tgt:
                            engine.wait_ge(ctx.s(ENG_SEM[dp.eng]).h, tgt)
                            waited[dp.eng] = tgt
                            if dp.eng == "tensor":
                                fresh_mm_wait = True
                    if fresh_mm_wait and eng == "vector":
                        engine.tensor_copy(out=ctx.sb["vwarm"][:, 8:16], in_=ctx.sb["vwarm"][:, 0:8])
                    elif fresh_mm_wait and eng == "scalar":
                        engine.copy(out=ctx.sb["acwarm"][:, 2:3], in_=ctx.sb["acwarm"][:, 1:2])
                    ins = s.fn(engine)
                    if s.seq in incval[eng]:
                        ctx.s(ENG_SEM[eng]).inc(ins, stepv[eng])
                if eng in pb_engines:
                    if eng in DMA_ENGS and steps:
                        engine.wait_ge(ctx.s(ENG_SEM[eng]).h, base[eng] + len(steps) * 16)
                    ctx.s("pb").inc(engine.nop(), 1)
            getattr(blk, eng)(body)
        ctx.pb_target += len(pb_engines)

class BD:
    def __init__(self, S):
        self.S = S
        self.w = {}
        self.r = {}

    def step(self, eng, fn, reads=(), writes=()):
        deps = []
        for bname in reads:
            if bname in self.w:
                deps.append(self.w[bname])
        for bname in writes:
            deps.extend(self.r.get(bname, ()))
            if bname in self.w:
                deps.append(self.w[bname])
        s = self.S.add(eng, fn, deps)
        for bname in reads:
            self.r.setdefault(bname, []).append(s)
        for bname in writes:
            self.w[bname] = s
            self.r[bname] = []
        return s


def phase_C(ctx):
    nc, blk, d, sb = ctx.nc, ctx.blk, ctx.d, ctx.sb
    import os as _oc
    _serc = _oc.environ.get("KB_SER_C")
    S = Script(ctx, serial=(_serc if _serc is not None else not _phase_par("C")),
               serial_first=int(_oc.environ.get("KB_SERF_C", "230")))
    B = BD(S)
    es = ExitStack()
    sbt = lambda name, shape, dt: es.enter_context(nc.sbuf_tensor(name, shape, dt))
    pst = lambda name, shape, dt: es.enter_context(nc.psum_tensor(name, shape, dt))
    kh = [sbt(f"pc_kh{j}", [64, T], BF16) for j in range(2)]
    vh = [sbt(f"pc_vh{j}", [64, T], BF16) for j in range(2)]
    rh = [sbt(f"pc_rh{j}", [64, T], BF16) for j in range(2)]
    wkvh = [sbt(f"pc_wkvh{j}", [64, T], BF16) for j in range(2)]
    declow = [sbt(f"pc_dlo{h}", [128, 128], BF16) for h in range(H)]
    decup = [sbt(f"pc_dup{h}", [128, 128], BF16) for h in range(H)]
    psiup = [sbt(f"pc_psi{h}", [128, 128], BF16) for h in range(H)]
    dgrow = [sbt(f"pc_dgr{h}", [64, 128], BF16) for h in range(H)]
    dpow0 = [sbt(f"pc_dp0{h}", [128, 1], F32) for h in range(H)]
    dpowr = [sbt(f"pc_dpr{h}", [128, 1], F32) for h in range(H)]
    decLc = [sbt(f"pc_dL{h}", [128, 1], F32) for h in range(H)]
    idecl = [sbt(f"pc_idL{h}", [64, 64], BF16) for h in range(H)]
    sc16 = sbt("pc_sc16", [16, 8], F32)
    DEC16, ET16, LND16, LNET16, LNB16 = 0, 1, 2, 3, 4
    diag = sbt("pc_diag", [16, 16], F32)
    LND = sbt("pc_LND", [128, 16], F32)
    NLND = sbt("pc_NLND", [128, 16], F32)
    LNB = sbt("pc_LNB", [128, 16], F32)
    ET = sbt("pc_ET", [128, 16], F32)
    NET = sbt("pc_NET", [128, 16], F32)
    tmpf = sbt("pc_tmpf", [128, 128], F32)
    c_iota_pf = sbt("pc_iotapf", [128, 128], F32)
    c_iota_f = sbt("pc_iotaf", [128, 128], F32)
    c_mlow = sbt("pc_mlow", [128, 128], F32)
    c_mups = sbt("pc_mups", [128, 128], F32)
    c_mupi = sbt("pc_mupi", [128, 128], F32)
    c_idf64 = sbt("pc_idf64", [64, 64], F32)
    Cb = [sbt(f"pc_C{u}", [128, 128], BF16) for u in range(2)]
    Ctb = [sbt(f"pc_Ct{u}", [128, 128], BF16) for u in range(2)]
    PSIb = [sbt(f"pc_PSIb{u}", [128, 128], BF16) for u in range(2)]
    Mt = [[sbt(f"pc_M{u}_{j}", [128, 128], BF16) for j in range(2)] for u in range(2)]
    Pb = [[sbt(f"pc_P{u}_{j}", [128, 128], BF16) for j in range(2)] for u in range(2)]
    Qb = [[sbt(f"pc_Q{u}_{j}", [128, 128], BF16) for j in range(2)] for u in range(2)]
    Ktm = [sbt(f"pc_Ktm{u}", [128, 64], BF16) for u in range(2)]
    Vtm = [sbt(f"pc_Vtm{u}", [128, 64], BF16) for u in range(2)]
    Kd = [sbt(f"pc_Kd{u}", [128, 64], BF16) for u in range(2)]
    FT = [sbt(f"pc_FT{u}", [64, 64], BF16) for u in range(2)]
    Gs = [sbt(f"pc_G{u}", [64, 64], F32) for u in range(2)]
    WVT = [sbt(f"pc_WVT{u}", [64, 128], F32) for u in range(2)]
    RT = [sbt(f"pc_RT{u}", [64, 128], BF16) for u in range(2)]
    Rg = [sbt(f"pc_Rg{u}", [64, 128], BF16) for u in range(2)]
    wkvf = [sbt(f"pc_wkvf{u}", [64, 128], F32) for u in range(2)]
    wsq = [sbt(f"pc_wsq{u}", [64, 128], BF16) for u in range(2)]
    Hs = sbt("pc_H", [64, 64], F32)
    Hb = sbt("pc_Hb", [64, 64], BF16)
    stbf = sbt("pc_stbf", [128, 2], BF16)
    ztile = sbt("pc_ztile", [64, 128], BF16)
    statsum = sbt("pc_statsum", [1, 16], F32)
    statsq = sbt("pc_statsq", [1, 16], F32)
    PG = pst("pc_PG", [128, 128], F32)
    PR = pst("pc_PR", [128, 128], F32)
    PT = pst("pc_PT", [128, 64], BF16)
    PAu = [pst(f"pc_PA{u}", [128, 128], F32) for u in range(2)]
    PA = PAu[0]
    P4 = pst("pc_P4", [64, 128], F32)
    P5 = pst("pc_P5", [64, 128], F32)
    STB = pst("pc_STB", [128, 2], F32)
    ST1 = STB[:, 0:1]
    ST2 = STB[:, 1:2]
    ident = sb["ident"]
    dcol = vcol("decay")
    ecol = vcol("eta")

    with es:
        for nmc, tile in (("iota_pf", c_iota_pf), ("iota_f", c_iota_f), ("m_lowstrict", c_mlow),
                          ("m_upstrict", c_mups), ("m_upincl", c_mupi), ("identf64", c_idf64)):
            B.step("sync", lambda e, nmc=nmc, tile=tile: e.dma_start(out=tile[:, :], in_=d[nmc]), writes=(nmc + "L",))
        B.step("scalar", lambda e: e.activation(out=sc16[:, DEC16:DEC16+1], in_=sb["vecs"][0:16, dcol:dcol+1], func=AF.Sigmoid), writes=("sc16",))
        B.step("scalar", lambda e: e.activation(out=sc16[:, ET16:ET16+1], in_=sb["vecs"][0:16, ecol:ecol+1], func=AF.Sigmoid), writes=("sc16",), reads=("sc16",))
        B.step("scalar", lambda e: e.activation(out=sc16[:, LND16:LND16+1], in_=sc16[:, DEC16:DEC16+1], func=AF.Ln), reads=("sc16",), writes=("sc16",))
        B.step("scalar", lambda e: e.activation(out=sc16[:, LNET16:LNET16+1], in_=sc16[:, ET16:ET16+1], func=AF.Ln), reads=("sc16",), writes=("sc16",))
        B.step("vector", lambda e: e.tensor_sub(out=sc16[:, LNB16:LNB16+1], in0=sc16[:, LNET16:LNET16+1], in1=sc16[:, LND16:LND16+1]), reads=("sc16",), writes=("sc16",))
        for tab, col in ((LND, LND16), (LNB, LNB16), (ET, ET16)):
            B.step("vector", lambda e, col=col: e.tensor_scalar_mul(out=diag[:, :], in0=c_idf64[0:16, 0:16], scalar1=sc16[:, col:col+1]), reads=("sc16",), writes=("diag",))
            B.step("tensor", lambda e: e.matmul(out=PA[:, 0:16], lhsT=sb["ones16f"][:, :], rhs=diag[:, :], start=True, stop=True), reads=("diag",), writes=("PA",))
            B.step("vector", lambda e, tab=tab: e.tensor_copy(out=tab[:, :], in_=PA[:, 0:16]), reads=("PA",), writes=(f"tab{col}",))
        B.step("vector", lambda e: e.tensor_scalar_mul(out=NLND[:, :], in0=LND[:, :], scalar1=-1.0), reads=(f"tab{LND16}",), writes=("NLND",))
        B.step("vector", lambda e: e.tensor_scalar_mul(out=NET[:, :], in0=ET[:, :], scalar1=-1.0), reads=(f"tab{ET16}",), writes=("NET",))
        for h in range(H):
            lnd = LND[:, h:h+1]; nlnd = NLND[:, h:h+1]; lnb = LNB[:, h:h+1]
            B.step("scalar", lambda e, h=h, lnd=lnd, lnb=lnb: e.activation(out=tmpf[:, :], in_=c_iota_pf[:, :], func=AF.Exp, scale=lnd, bias=lnb), reads=(f"tab{LND16}", f"tab{LNB16}"), writes=("tmpf",))
            B.step("vector", lambda e, h=h: e.tensor_mul(out=declow[h][:, :], in0=tmpf[:, :], in1=c_mlow[:, :]), reads=("tmpf",), writes=(f"dlo{h}",))
            B.step("scalar", lambda e, h=h, nlnd=nlnd, lnb=lnb: e.activation(out=tmpf[:, :], in_=c_iota_pf[:, :], func=AF.Exp, scale=nlnd, bias=lnb), reads=("NLND", f"tab{LNB16}", f"dlo{h}"), writes=("tmpf",))
            B.step("vector", lambda e, h=h: e.tensor_mul(out=decup[h][:, :], in0=tmpf[:, :], in1=c_mups[:, :]), reads=("tmpf",), writes=(f"dup{h}",))
            B.step("scalar", lambda e, h=h, nlnd=nlnd: e.activation(out=tmpf[:, :], in_=c_iota_pf[:, :], func=AF.Exp, scale=nlnd), reads=(f"dup{h}",), writes=("tmpf",))
            B.step("vector", lambda e, h=h: e.tensor_mul(out=psiup[h][:, :], in0=tmpf[:, :], in1=c_mupi[:, :]), reads=("tmpf",), writes=(f"psi{h}",))
            B.step("scalar", lambda e, h=h, lnd=lnd: e.activation(out=dgrow[h][:, :], in_=c_iota_f[0:64, :], func=AF.Exp, scale=lnd[0:64, :], bias=lnd[0:64, :]), reads=(f"tab{LND16}",), writes=(f"dgr{h}",))
            B.step("scalar", lambda e, h=h, lnd=lnd: e.activation(out=dpow0[h][:, :], in_=c_iota_pf[:, 0:1], func=AF.Exp, scale=lnd), reads=(f"tab{LND16}",), writes=(f"dp0{h}",))
            B.step("scalar", lambda e, h=h, nlnd=nlnd: e.activation(out=dpowr[h][:, :], in_=c_iota_pf[:, 127:128], func=AF.Exp, scale=nlnd), reads=("NLND",), writes=(f"dpr{h}",))
            B.step("scalar", lambda e, h=h, lnd=lnd: e.activation(out=decLc[h][:, :], in_=sb["fconst"][:, 3:4], func=AF.Exp, scale=lnd), reads=(f"tab{LND16}",), writes=(f"dL{h}",))
            B.step("vector", lambda e, h=h: e.tensor_scalar_mul(out=idecl[h][:, :], in0=ident[0:64, 0:64], scalar1=decLc[h][0:64, :]), reads=(f"dL{h}",), writes=(f"idL{h}",))
        B.step("vector", lambda e: e.memset(ztile[:, :], 0.0), writes=("ztile",))
        for h in range(H):
            hb = h % 2
            B.step("sync", lambda e, h=h, hb=hb: e.dma_start(out=kh[hb][:, :], in_=d["k_fm"][64*h:64*h+64, :]), writes=(f"kh{hb}",))
            B.step("sync", lambda e, h=h, hb=hb: e.dma_start(out=vh[hb][:, :], in_=d["v_fm"][64*h:64*h+64, :]), writes=(f"vh{hb}",))
            B.step("sync", lambda e, h=h, hb=hb: e.dma_start(out=rh[hb][:, :], in_=d["r_fm"][64*h:64*h+64, :]), writes=(f"rh{hb}",))
            B.step("vector", lambda e: e.memset(Hs[:, :], 0.0), writes=("Hs",))
            B.step("vector", lambda e: e.memset(Hb[:, :], 0.0), writes=("Hb",))
            B.step("tensor", lambda e: e.matmul(out=STB[:, 0:2], lhsT=ztile[:, :], rhs=sb["ones"][0:64, 0:2],
                   start=True, stop=True), reads=("ztile",), writes=("ST1w", "ST2w"))
            for c in range(NCH):
                u = c % 2
                cs = slice(128 * c, 128 * (c + 1))
                Kf = kh[hb][:, cs]; Vf = vh[hb][:, cs]; Rf = rh[hb][:, cs]
                kread, vread, rread = f"kh{hb}", f"vh{hb}", f"rh{hb}"
                B.step("tensor", lambda e, Kf=Kf: e.matmul(out=PG[:, :], lhsT=Kf, rhs=Kf, start=True, stop=True), reads=(kread,), writes=("PG",))
                B.step("vector", lambda e, h=h, u=u: e.tensor_mul(out=Cb[u][:, :], in0=PG[:, :], in1=declow[h][:, :]), reads=("PG", f"dlo{h}"), writes=(f"C{u}",))
                B.step("vector", lambda e, h=h, u=u: e.tensor_mul(out=Ctb[u][:, :], in0=PG[:, :], in1=decup[h][:, :]), reads=("PG", f"dup{h}"), writes=(f"Ct{u}",))
                B.step("tensor", lambda e, Kf=Kf, Rf=Rf: e.matmul(out=PR[:, :], lhsT=Kf, rhs=Rf, start=True, stop=True), reads=(kread, rread, f"C{u}", f"Ct{u}"), writes=("PR",))
                B.step("vector", lambda e, h=h, u=u: e.tensor_mul(out=PSIb[u][:, :], in0=PR[:, :], in1=psiup[h][:, :]), reads=("PR", f"psi{h}"), writes=(f"PSIb{u}",))
                B.step("tensor", lambda e, Kf=Kf: e.transpose(out=PT[:, :], in_=Kf, identity=ident[0:64, 0:64]), reads=(kread,), writes=("PT",))
                B.step("vector", lambda e, u=u: e.tensor_copy(out=Ktm[u][:, :], in_=PT[:, :]), reads=("PT",), writes=(f"Ktm{u}",))
                B.step("tensor", lambda e, Vf=Vf: e.transpose(out=PT[:, :], in_=Vf, identity=ident[0:64, 0:64]), reads=(vread, f"Ktm{u}"), writes=("PT",))
                B.step("vector", lambda e, u=u: e.tensor_copy(out=Vtm[u][:, :], in_=PT[:, :]), reads=("PT",), writes=(f"Vtm{u}",))
                B.step("vector", lambda e, u=u, h=h: e.tensor_scalar_mul(out=Mt[u][0][:, 0:64], in0=Ktm[u][:, :], scalar1=dpow0[h][:, :]), reads=(f"Ktm{u}", f"dp0{h}"), writes=(f"M{u}_0",))
                B.step("vector", lambda e, u=u, h=h: e.tensor_scalar_mul(out=Mt[u][0][:, 64:128], in0=Vtm[u][:, :], scalar1=ET[:, h:h+1]), reads=(f"Vtm{u}", f"tab{ET16}"), writes=(f"M{u}_0",), )
                B.step("tensor", lambda e, u=u: e.matmul(out=PAu[u][:, :], lhsT=Ctb[u][:, :], rhs=Mt[u][0][:, :], start=True, stop=True), reads=(f"Ct{u}", f"M{u}_0"), writes=(f"PA{u}",))
                B.step("vector", lambda e, u=u: e.tensor_sub(out=Mt[u][1][:, :], in0=Mt[u][0][:, :], in1=PAu[u][:, :]), reads=(f"PA{u}", f"M{u}_0"), writes=(f"M{u}_1",))
                pprev, qprev = f"C{u}", f"Ct{u}"
                pptile, qptile = Cb[u], Ctb[u]
                for lvl in range(1, 7):
                    pcur, qcur = f"P{u}_{lvl%2}", f"Q{u}_{lvl%2}"
                    pt, qt = Pb[u][lvl % 2], Qb[u][lvl % 2]
                    B.step("tensor", lambda e, u=u, qptile=qptile, pptile=pptile: e.matmul(out=PAu[u][:, :], lhsT=qptile[:, :], rhs=pptile[:, :], start=True, stop=True), reads=(pprev, qprev), writes=(f"PA{u}",))
                    B.step("vector", lambda e, u=u, pt=pt: e.tensor_copy(out=pt[:, :], in_=PAu[u][:, :]), reads=(f"PA{u}",), writes=(pcur,))
                    B.step("tensor", lambda e, u=u, qptile=qptile, pptile=pptile: e.matmul(out=PAu[u][:, :], lhsT=pptile[:, :], rhs=qptile[:, :], start=True, stop=True), reads=(pprev, qprev), writes=(f"PA{u}",))
                    B.step("vector", lambda e, u=u, qt=qt: e.tensor_copy(out=qt[:, :], in_=PAu[u][:, :]), reads=(f"PA{u}",), writes=(qcur,))
                    mprev, mcur = f"M{u}_{lvl%2}", f"M{u}_{(lvl+1)%2}"
                    mpt, mct = Mt[u][lvl % 2], Mt[u][(lvl + 1) % 2]
                    B.step("tensor", lambda e, u=u, qt=qt, mpt=mpt: e.matmul(out=PAu[u][:, :], lhsT=qt[:, :], rhs=mpt[:, :], start=True, stop=True), reads=(qcur, mprev), writes=(f"PA{u}",))
                    B.step("vector", lambda e, u=u, mct=mct, mpt=mpt: e.tensor_add(out=mct[:, :], in0=mpt[:, :], in1=PAu[u][:, :]), reads=(f"PA{u}", mprev), writes=(mcur,))
                    pprev, qprev = pcur, qcur
                    pptile, qptile = pt, qt
                MF = Mt[u][1]
                mfk = f"M{u}_1"
                B.step("vector", lambda e, u=u, h=h: e.tensor_scalar_mul(out=Kd[u][:, :], in0=Ktm[u][:, :], scalar1=dpowr[h][:, :]), reads=(f"Ktm{u}", f"dpr{h}"), writes=(f"Kd{u}",))
                B.step("tensor", lambda e, MF=MF, u=u: e.matmul(out=P4[0:64, :64], lhsT=MF[:, 0:64], rhs=Kd[u][:, :], start=True, stop=True), reads=(mfk, f"Kd{u}"), writes=("P4",))
                B.step("vector", lambda e, u=u, h=h: e.scalar_tensor_tensor(out=FT[u][:, :], in0=P4[0:64, :64], scalar=NET[0:64, h:h+1], in1=idecl[h][:, :], op0=ALU.mult, op1=ALU.add), reads=("P4", "NET", f"idL{h}"), writes=(f"FT{u}",))
                B.step("tensor", lambda e, MF=MF, u=u: e.matmul(out=P5[0:64, :64], lhsT=Kd[u][:, :], rhs=MF[:, 64:128], start=True, stop=True), reads=(mfk, f"Kd{u}"), writes=("P5",))
                B.step("vector", lambda e, u=u: e.tensor_copy(out=Gs[u][:, :], in_=P5[0:64, :64]), reads=("P5",), writes=(f"G{u}",))
                B.step("tensor", lambda e, MF=MF, u=u: e.matmul(out=P4[0:64, :], lhsT=MF[:, 64:128], rhs=PSIb[u][:, :], start=True, stop=True), reads=(mfk, f"PSIb{u}", f"FT{u}"), writes=("P4",))
                B.step("vector", lambda e, u=u: e.tensor_copy(out=WVT[u][:, :], in_=P4[0:64, :]), reads=("P4",), writes=(f"WVT{u}",))
                B.step("tensor", lambda e, MF=MF, u=u: e.matmul(out=P5[0:64, :], lhsT=MF[:, 0:64], rhs=PSIb[u][:, :], start=True, stop=True), reads=(mfk, f"PSIb{u}", f"G{u}"), writes=("P5",))
                B.step("vector", lambda e, u=u, h=h, Rf=Rf: e.tensor_mul(out=Rg[u][:, :], in0=Rf, in1=dgrow[h][:, :]), reads=(rread, f"dgr{h}"), writes=(f"Rg{u}",))
                B.step("vector", lambda e, u=u, h=h: e.scalar_tensor_tensor(out=RT[u][:, :], in0=P5[0:64, :], scalar=NET[0:64, h:h+1], in1=Rg[u][:, :], op0=ALU.mult, op1=ALU.add), reads=("P5", "NET", f"Rg{u}"), writes=(f"RT{u}",))
                B.step("tensor", lambda e, u=u: e.matmul(out=P4[0:64, :], lhsT=Hb[:, :], rhs=RT[u][:, :], start=True, stop=True), reads=("Hb", f"RT{u}", f"WVT{u}"), writes=("P4",))
                B.step("vector", lambda e, u=u: e.tensor_add(out=wkvf[u][:, :], in0=P4[0:64, :], in1=WVT[u][:, :]), reads=("P4", f"WVT{u}"), writes=(f"wkvf{u}",))
                B.step("scalar", lambda e, u=u, hb=hb, cs=cs: e.copy(out=wkvh[hb][:, cs], in_=wkvf[u][:, :]), reads=(f"wkvf{u}",), writes=(f"wkvh{hb}", f"wkvbc{u}"))
                B.step("scalar", lambda e, u=u: e.activation(out=wsq[u][:, :], in_=wkvf[u][:, :], func=AF.Square), reads=(f"wkvf{u}",), writes=(f"wsq{u}",))
                B.step("tensor", lambda e, hb=hb, cs=cs, c=c: e.matmul(out=ST1, lhsT=wkvh[hb][:, cs], rhs=sb["ones"][0:64, 0:1], start=False, stop=(c == NCH - 1)), reads=(f"wkvbc{u}",), writes=("ST1w",))
                B.step("tensor", lambda e, u=u, c=c: e.matmul(out=ST2, lhsT=wsq[u][:, :], rhs=sb["ones"][0:64, 0:1], start=False, stop=(c == NCH - 1)), reads=(f"wsq{u}",), writes=("ST2w",))
                B.step("tensor", lambda e, u=u: e.matmul(out=P5[0:64, :64], lhsT=FT[u][:, :], rhs=Hb[:, :], start=True, stop=True), reads=(f"FT{u}", "Hb", f"G{u}"), writes=("P5",))
                B.step("vector", lambda e, u=u: e.tensor_add(out=Hs[:, :], in0=P5[0:64, :64], in1=Gs[u][:, :]), reads=("P5", f"G{u}", "Hb"), writes=("Hs",))
                B.step("scalar", lambda e: e.copy(out=Hb[:, :], in_=Hs[:, :]), reads=("Hs",), writes=("Hb",))
            B.step("vector", lambda e: e.tensor_copy(out=stbf[:, 0:1], in_=ST1), reads=("ST1w",), writes=("stbf", "ST1"))
            B.step("vector", lambda e: e.tensor_copy(out=stbf[:, 1:2], in_=ST2), reads=("ST2w",), writes=("stbf", "ST2"))
            B.step("tensor", lambda e: e.matmul(out=P4[0:1, 0:1], lhsT=stbf[:, 0:1], rhs=sb["ones"][:, 0:1], start=True, stop=True), reads=("stbf",), writes=("P4",))
            B.step("vector", lambda e, h=h: e.tensor_copy(out=statsum[:, h:h+1], in_=P4[0:1, 0:1]), reads=("P4",), writes=("statsum",))
            B.step("tensor", lambda e: e.matmul(out=P4[0:1, 0:1], lhsT=stbf[:, 1:2], rhs=sb["ones"][:, 0:1], start=True, stop=True), reads=("stbf", "statsum"), writes=("P4",))
            B.step("vector", lambda e, h=h: e.tensor_copy(out=statsq[:, h:h+1], in_=P4[0:1, 0:1]), reads=("P4",), writes=("statsq",))
            B.step("sync", lambda e, h=h, hb=hb: e.dma_start(out=d["wkv_fm"][64*h:64*h+64, :], in_=wkvh[hb][:, :]), reads=(f"wkvh{hb}",), writes=(f"wkvst{hb}",))
        B.step("sync", lambda e: e.dma_start(out=d["stats"][0:1, :], in_=statsum[:, :]), reads=("statsum",))
        B.step("sync", lambda e: e.dma_start(out=d["stats"][1:2, :], in_=statsq[:, :]), reads=("statsq",))
        S.emit()


def phase_D(ctx):
    nc, blk, d, sb = ctx.nc, ctx.blk, ctx.d, ctx.sb
    S = Script(ctx, serial=not _phase_par("D"))
    B = BD(S)
    es = ExitStack()
    sbt = lambda name, shape, dt: es.enter_context(nc.sbuf_tensor(name, shape, dt))
    pst = lambda name, shape, dt: es.enter_context(nc.psum_tensor(name, shape, dt))
    wo = sbt("pd_wo", [128, 8 * 1024], BF16)
    woi = sbt("pd_woi", [128, 8 * 1024], F32)
    ssum = sbt("pd_ssum", [1, 16], F32)
    ssq = sbt("pd_ssq", [1, 16], F32)
    sm = sbt("pd_sm", [1, 16], F32)
    sq2 = sbt("pd_sq2", [1, 16], F32)
    m2t = sbt("pd_m2t", [1, 16], F32)
    var = sbt("pd_var", [1, 16], F32)
    rstd = sbt("pd_rstd", [1, 16], F32)
    mt = sbt("pd_mt", [16, 1], F32)
    rt = sbt("pd_rt", [16, 1], F32)
    rhsm = sbt("pd_rhsm", [16, 8], F32)
    MB = sbt("pd_MB", [128, 8], F32)
    RB = sbt("pd_RB", [128, 8], F32)
    wn1 = sbt("pd_wn1", [128, 512], BF16)
    wkvt = [sbt(f"pd_wkvt{j}", [128, 8 * 512], BF16) for j in range(2)]
    g0t = [sbt(f"pd_g0t{j}", [128, 8 * 512], BF16) for j in range(2)]
    wnt = [sbt(f"pd_wnt{j}", [128, 8 * 512], BF16) for j in range(2)]
    sft = [sbt(f"pd_sft{j}", [128, 8 * 512], BF16) for j in range(2)]
    PS = [pst(f"pd_PS{j}", [128, 512], F32) for j in range(4)]
    PQ = pst("pd_PQ", [128, 16], F32)
    NRM = 1.0 / (T * D)

    import os
    dbg = bool(int(os.environ.get("KB_DBG_D", "0")))
    if dbg:
        d["dbg_d"] = nc.dram_tensor("dbg_d", [128, 40], F32, kind="ExternalOutput").ap()
        d["dbg_wnt"] = nc.dram_tensor("dbg_wnt", [128, 4096], BF16, kind="ExternalOutput").ap()
        d["dbg_wo"] = nc.dram_tensor("dbg_wo", [128, 8192], BF16, kind="ExternalOutput").ap()
    with es:
        B.step("sync", lambda e: e.dma_start(out=woi.rearrange("p (k n) -> p k n", n=1024),
                                             in_=d["woT"].rearrange("(k p) n -> p k n", p=128)), writes=("woi",))
        B.step("vector", lambda e: e.tensor_copy(out=wo[:, :], in_=woi[:, :]), reads=("woi",), writes=("wo",))
        B.step("sync", lambda e: e.dma_start(out=ssum[:, :], in_=d["stats"][0:1, :]), writes=("ssum",))
        B.step("sync", lambda e: e.dma_start(out=ssq[:, :], in_=d["stats"][1:2, :]), writes=("ssq",))
        B.step("vector", lambda e: e.tensor_scalar_mul(out=sm[:, :], in0=ssum[:, :], scalar1=NRM), reads=("ssum",), writes=("sm",))
        B.step("vector", lambda e: e.tensor_scalar_mul(out=sq2[:, :], in0=ssq[:, :], scalar1=NRM), reads=("ssq",), writes=("sq2",))
        B.step("scalar", lambda e: e.activation(out=m2t[:, :], in_=sm[:, :], func=AF.Square), reads=("sm",), writes=("m2t",))
        B.step("vector", lambda e: e.tensor_sub(out=var[:, :], in0=sq2[:, :], in1=m2t[:, :]), reads=("sq2", "m2t"), writes=("var",))
        B.step("scalar", lambda e: e.activation(out=rstd[:, :], in_=var[:, :], func=AF.Sqrt, bias=sb["fconst"][0:1, 0:1]), reads=("var",), writes=("rstd",))
        B.step("vector", lambda e: e.reciprocal(out=rstd[:, :], in_=rstd[:, :]), reads=("rstd",), writes=("rstd",))
        B.step("tensor", lambda e: e.matmul(out=PQ[0:16, 0:1], lhsT=sm[0:1, :], rhs=sb["ones16f"][0:1, 0:1], start=True, stop=True), reads=("sm",), writes=("PQ",))
        B.step("vector", lambda e: e.tensor_copy(out=mt[:, :], in_=PQ[0:16, 0:1]), reads=("PQ",), writes=("mt",))
        B.step("tensor", lambda e: e.matmul(out=PQ[0:16, 0:1], lhsT=rstd[0:1, :], rhs=sb["ones16f"][0:1, 0:1], start=True, stop=True), reads=("rstd", "mt"), writes=("PQ",))
        B.step("vector", lambda e: e.tensor_copy(out=rt[:, :], in_=PQ[0:16, 0:1]), reads=("PQ",), writes=("rt",))
        for nm, src, dst in (("m", mt, MB), ("r", rt, RB)):
            B.step("vector", lambda e, src=src: e.tensor_scalar_mul(out=rhsm[:, :], in0=sb["halfsel"][:, :], scalar1=src[:, :]), reads=(nm + "tkey", "mt" if nm == "m" else "rt"), writes=("rhsm",))
            B.step("tensor", lambda e: e.matmul(out=PQ[:, 0:8], lhsT=sb["hsel2"][:, :], rhs=rhsm[:, :], start=True, stop=True), reads=("rhsm",), writes=("PQ",))
            B.step("vector", lambda e, dst=dst: e.tensor_copy(out=dst[:, :], in_=PQ[:, 0:8]), reads=("PQ",), writes=(nm + "B",))

        if dbg:
            B.step("sync", lambda e: e.dma_start(out=d["dbg_wo"], in_=wo[:, :]), reads=("wo",))
            dd = es.enter_context(nc.sbuf_tensor("pd_dbg", [128, 40], F32))
            def dumpit(e):
                e.memset(dd[:, :], 0.0)
                e.tensor_copy(out=dd[0:1, 0:16], in_=var[:, :])
                e.tensor_copy(out=dd[0:1, 16:32], in_=rstd[:, :])
                e.tensor_copy(out=dd[:, 32:40], in_=MB[:, :])
                ins = e.tensor_copy(out=dd[:, 24:32], in_=RB[:, :])
                return ins
            B.step("vector", dumpit, reads=("var", "rstd", "mB", "rB"), writes=("dd",))
            B.step("sync", lambda e: e.dma_start(out=d["dbg_d"], in_=dd[:, :]), reads=("dd",))
        for i in range(8):
            j = i % 2
            t0 = i * 512
            B.step("sync", lambda e, j=j, t0=t0: e.dma_start(
                out=wkvt[j].rearrange("p (k n) -> p k n", n=512),
                in_=d["wkv_fm"].rearrange("(k p) t -> p k t", p=128)[:, :, t0:t0 + 512]), writes=(f"wkvt{j}",))
            B.step("sync", lambda e, j=j, t0=t0: e.dma_start(
                out=g0t[j].rearrange("p (k n) -> p k n", n=512),
                in_=d["g0_fm"].rearrange("(k p) t -> p k t", p=128)[:, :, t0:t0 + 512]), writes=(f"g0t{j}",))
            for k in range(KC):
                gw = sb["vecs"][:, vcol("gnw", k):vcol("gnw", k) + 1]
                gb = sb["vecs"][:, vcol("gnb", k):vcol("gnb", k) + 1]
                B.step("vector", lambda e, j=j, k=k: e.tensor_scalar(
                    out=wn1[:, :], in0=wkvt[j][:, k*512:(k+1)*512],
                    scalar1=MB[:, k:k+1], scalar2=RB[:, k:k+1],
                    op0=ALU.subtract, op1=ALU.mult), reads=(f"wkvt{j}", "mB", "rB"), writes=("wn1",))
                B.step("vector", lambda e, j=j, k=k, gw=gw, gb=gb: e.tensor_scalar(
                    out=wnt[j][:, k*512:(k+1)*512], in0=wn1[:, :],
                    scalar1=gw, scalar2=gb,
                    op0=ALU.mult, op1=ALU.add), reads=("wn1",), writes=(f"wnt{j}",))
            if dbg and i == 0:
                B.step("sync", lambda e: e.dma_start(out=d["dbg_wnt"], in_=wnt[0][:, :]), reads=("wnt0",))
            for m in range(KC):
                bank = f"PS{m % 4}"
                pso = PS[m % 4]
                for k in range(KC):
                    B.step("tensor", lambda e, j=j, k=k, m=m, pso=pso: e.matmul(
                        out=pso[:, :], lhsT=wo[:, k*1024 + m*128:k*1024 + (m+1)*128],
                        rhs=wnt[j][:, k*512:(k+1)*512], start=(k == 0), stop=False),
                        reads=(f"wnt{j}", "wo"), writes=(bank,))
                B.step("tensor", lambda e, m=m, pso=pso: e.matmul(
                    out=pso[:, :], lhsT=sb["brow_sb"][0:1, BROW["b_o"]*1024 + m*128:BROW["b_o"]*1024 + (m+1)*128],
                    rhs=sb["onesrow"][0:1, :], start=False, stop=True), writes=(bank,))
                B.step("vector", lambda e, j=j, m=m, pso=pso: e.tensor_mul(
                    out=sft[j][:, m*512:(m+1)*512], in0=pso[:, :], in1=g0t[j][:, m*512:(m+1)*512]),
                    reads=(bank, f"g0t{j}"), writes=(f"sft{j}",))
            B.step("sync", lambda e, j=j, t0=t0: e.dma_start(
                out=d["s_fm"].rearrange("(k p) t -> p k t", p=128)[:, :, t0:t0 + 512],
                in_=sft[j].rearrange("p (k n) -> p k n", n=512)), reads=(f"sft{j}",), writes=(f"sst{j}",))
        S.emit()


def phase_E(ctx):
    nc, blk, d, sb = ctx.nc, ctx.blk, ctx.d, ctx.sb
    S = Script(ctx, serial=not _phase_par("E"))
    B = BD(S)
    es = ExitStack()
    sbt = lambda name, shape, dt: es.enter_context(nc.sbuf_tensor(name, shape, dt))
    pst = lambda name, shape, dt: es.enter_context(nc.psum_tensor(name, shape, dt))
    xnt = sbt("pe_xnt", [128, 8 * 1024], BF16)
    BAm = [sbt(f"pe_BA{m}", [128, 1024], BF16) for m in range(32)]
    BBm = [sbt(f"pe_BB{m}", [128, 1024], BF16) for m in range(32)]
    wA = [sbt(f"pe_wA{j}", [128, 8 * 128], BF16) for j in range(2)]
    wB = [sbt(f"pe_wB{j}", [128, 8 * 128], BF16) for j in range(2)]
    wH = [sbt(f"pe_wH{j}", [128, 512], BF16) for j in range(4)]
    t1b = sbt("pe_t1", [128, 1024], BF16)
    t2b = sbt("pe_t2", [128, 1024], BF16)
    gft = sbt("pe_gft", [128, 8 * 1024], BF16)
    PS = [pst(f"pe_PS{j}", [128, 512], F32) for j in range(8)]

    def wslice(name, m):
        return d[name].rearrange("(k p) n -> p k n", p=128)[:, :, m*128:(m+1)*128]

    with es:
        for it in range(4):
            t0 = it * 1024
            B.step("sync", lambda e, t0=t0: e.dma_start(
                out=xnt.rearrange("p (k n) -> p k n", n=1024),
                in_=d["xn_fm"].rearrange("(k p) t -> p k t", p=128)[:, :, t0:t0 + 1024]), writes=("xnt",))
            for m in range(32):
                j = m % 2
                B.step("gpsimd", lambda e, j=j, m=m: e.dma_start(
                    out=wA[j].rearrange("p (k n) -> p k n", n=128), in_=wslice("w1T", m)), writes=(f"wA{j}",))
                B.step("gpsimd", lambda e, j=j, m=m: e.dma_start(
                    out=wB[j].rearrange("p (k n) -> p k n", n=128), in_=wslice("w2T", m)), writes=(f"wB{j}",))
                for half in range(2):
                    pa, pb = PS[half], PS[2 + half]
                    for k in range(KC):
                        B.step("tensor", lambda e, j=j, k=k, half=half, pa=pa: e.matmul(
                            out=pa[:, :], lhsT=wA[j][:, k*128:(k+1)*128],
                            rhs=xnt[:, k*1024 + half*512:k*1024 + half*512 + 512],
                            start=(k == 0), stop=(k == KC - 1)), reads=(f"wA{j}", "xnt"), writes=(f"bank{half}",))
                    for k in range(KC):
                        B.step("tensor", lambda e, j=j, k=k, half=half, pb=pb: e.matmul(
                            out=pb[:, :], lhsT=wB[j][:, k*128:(k+1)*128],
                            rhs=xnt[:, k*1024 + half*512:k*1024 + half*512 + 512],
                            start=(k == 0), stop=(k == KC - 1)), reads=(f"wB{j}", "xnt"), writes=(f"bank{2+half}",))
                    s1c = sb["vecs"][:, vcol("s1", m):vcol("s1", m) + 1]
                    s2c = sb["vecs"][:, vcol("s2", m):vcol("s2", m) + 1]
                    B.step("scalar", lambda e, half=half, pa=pa, s1c=s1c: e.activation(
                        out=t1b[:, half*512:half*512+512], in_=pa[:, :], func=AF.Silu, scale=s1c),
                        reads=(f"bank{half}",), writes=("t1b",))
                    B.step("vector", lambda e, half=half, pb=pb, s2c=s2c: e.tensor_scalar_mul(
                        out=t2b[:, half*512:half*512+512], in0=pb[:, :], scalar1=s2c),
                        reads=(f"bank{2+half}",), writes=("t2b",))
                    B.step("vector", lambda e, m=m, half=half: e.tensor_mul(
                        out=BBm[m][:, half*512:half*512+512], in0=t1b[:, half*512:half*512+512],
                        in1=t2b[:, half*512:half*512+512]), reads=("t1b", "t2b"), writes=(f"BB{m}",))
            for stepname, wname, scol, SRC, DST, srck, dstk in (
                    ("B", "h1T", "sh1", BBm, BAm, "BB", "BA"),
                    ("C", "h2T", "sh2", BAm, BBm, "BA", "BB")):
                for mg in range(8):
                    for k in range(32):
                        j = k % 4
                        B.step("gpsimd", lambda e, j=j, k=k, mg=mg, wname=wname: e.dma_start(
                            out=wH[j][:, :],
                            in_=d[wname][k*128:(k+1)*128, mg*512:(mg+1)*512]), writes=(f"wH{j}",))
                        for mi in range(4):
                            m = mg * 4 + mi
                            for half in range(2):
                                ps = PS[mi * 2 + half]
                                B.step("tensor", lambda e, j=j, k=k, mi=mi, half=half, ps=ps, m=m, SRC=SRC: e.matmul(
                                    out=ps[:, :], lhsT=wH[j][:, mi*128:(mi+1)*128],
                                    rhs=SRC[k][:, half*512:half*512+512],
                                    start=(k == 0), stop=(k == 31)),
                                    reads=(f"wH{j}", f"{srck}{k}"), writes=(f"bank{mi*2+half}",))
                    for mi in range(4):
                        m = mg * 4 + mi
                        sc = sb["vecs"][:, vcol(scol, m):vcol(scol, m) + 1]
                        for half in range(2):
                            ps = PS[mi * 2 + half]
                            B.step("scalar", lambda e, ps=ps, m=m, half=half, sc=sc, DST=DST: e.activation(
                                out=DST[m][:, half*512:half*512+512], in_=ps[:, :], func=AF.Gelu, scale=sc),
                                reads=(f"bank{mi*2+half}",), writes=(f"{dstk}{m}",))
            for mg in range(2):
                for k in range(32):
                    j = k % 4
                    B.step("gpsimd", lambda e, j=j, k=k, mg=mg: e.dma_start(
                        out=wH[j][:, :],
                        in_=d["w3T"][k*128:(k+1)*128, mg*512:(mg+1)*512]), writes=(f"wH{j}",))
                    for mi in range(4):
                        m = mg * 4 + mi
                        for half in range(2):
                            ps = PS[mi * 2 + half]
                            B.step("tensor", lambda e, j=j, k=k, mi=mi, half=half, ps=ps: e.matmul(
                                out=ps[:, :], lhsT=wH[j][:, mi*128:(mi+1)*128],
                                rhs=BBm[k][:, half*512:half*512+512],
                                start=(k == 0), stop=(k == 31)),
                                reads=(f"wH{j}", f"BB{k}"), writes=(f"bank{mi*2+half}",))
                for mi in range(4):
                    m = mg * 4 + mi
                    sc = sb["vecs"][:, vcol("s3", m):vcol("s3", m) + 1]
                    for half in range(2):
                        ps = PS[mi * 2 + half]
                        B.step("scalar", lambda e, ps=ps, m=m, half=half, sc=sc: e.activation(
                            out=gft[:, m*1024 + half*512:m*1024 + half*512 + 512],
                            in_=ps[:, :], func=AF.Silu, scale=sc),
                            reads=(f"bank{mi*2+half}",), writes=("gft",))
            B.step("sync", lambda e, t0=t0: e.dma_start(
                out=d["gf_fm"].rearrange("(k p) t -> p k t", p=128)[:, :, t0:t0 + 1024],
                in_=gft.rearrange("p (k n) -> p k n", n=1024)), reads=("gft",), writes=("gfst",))
        S.emit()


def phase_F(ctx):
    nc, blk, d, sb = ctx.nc, ctx.blk, ctx.d, ctx.sb
    S = Script(ctx, serial=not _phase_par("F"))
    B = BD(S)
    es = ExitStack()
    sbt = lambda name, shape, dt: es.enter_context(nc.sbuf_tensor(name, shape, dt))
    pst = lambda name, shape, dt: es.enter_context(nc.psum_tensor(name, shape, dt))
    xin = [sbt(f"pf_x{j}", [128, C], F32) for j in range(2)]
    sft = [sbt(f"pf_s{j}", [128, C], BF16) for j in range(2)]
    gft = [sbt(f"pf_g{j}", [128, C], BF16) for j in range(2)]
    prod = [sbt(f"pf_p{j}", [128, C], BF16) for j in range(2)]
    x1t = [sbt(f"pf_x1{j}", [128, C], F32) for j in range(2)]
    xsq = [sbt(f"pf_xsq{j}", [128, C], F32) for j in range(2)]
    xnbf = [sbt(f"pf_xnbf{j}", [128, C], BF16) for j in range(2)]
    stage = [sbt(f"pf_st{j}", [128, C], BF16) for j in range(2)]
    st = sbt("pf_stat", [128, 8], F32)
    S1c, S2c, MEAN, EX2, M2, VAR, SD, RSTD = range(8)
    pstr = [pst(f"pf_ps{j}", [128, 128], BF16) for j in range(2)]
    ident = sb["ident"]

    with es:
        for pos, i in enumerate(list(range(NT)) + [0]):
            j = pos % 2
            r0 = i * 128
            B.step("sync", lambda e, j=j, r0=r0: e.dma_start(out=xin[j][:, :], in_=d["x"][r0:r0+128, :]), writes=(f"x{j}",))
            B.step("sync", lambda e, j=j, r0=r0: e.dma_start(
                out=sft[j].rearrange("p (k n) -> p k n", n=128),
                in_=d["s_fm"].rearrange("(k p) t -> p k t", p=128)[:, :, r0:r0+128]), writes=(f"s{j}",))
            B.step("sync", lambda e, j=j, r0=r0: e.dma_start(
                out=gft[j].rearrange("p (k n) -> p k n", n=128),
                in_=d["gf_fm"].rearrange("(k p) t -> p k t", p=128)[:, :, r0:r0+128]), writes=(f"g{j}",))
            B.step("vector", lambda e, j=j: e.tensor_mul(out=prod[j][:, :], in0=sft[j][:, :], in1=gft[j][:, :]),
                   reads=(f"s{j}", f"g{j}"), writes=(f"p{j}",))
            for k in range(KC):
                pj = (pos * KC + k) % 2
                B.step("tensor", lambda e, j=j, k=k, pj=pj: e.transpose(
                    out=pstr[pj][:, :], in_=prod[j][:, k*128:(k+1)*128], identity=ident[:, :]),
                    reads=(f"p{j}",), writes=(f"ptr{pj}",))
                B.step("vector", lambda e, j=j, k=k, pj=pj: e.tensor_add(
                    out=x1t[j][:, k*128:(k+1)*128], in0=pstr[pj][:, :], in1=xin[j][:, k*128:(k+1)*128]),
                    reads=(f"ptr{pj}", f"x{j}"), writes=(f"x1{j}",))
            B.step("sync", lambda e, j=j, r0=r0: e.dma_start(out=d["x1_tm"][r0:r0+128, :], in_=x1t[j][:, :]),
                   reads=(f"x1{j}",), writes=(f"x1st{j}",))
            B.step("vector", lambda e, j=j: e.reduce_sum(out=st[:, S1c:S1c+1], in_=x1t[j][:, :], axis=AX.X),
                   reads=(f"x1{j}",), writes=("st",))
            B.step("scalar", lambda e, j=j: e.activation(out=xsq[j][:, :], in_=x1t[j][:, :], func=AF.Square),
                   reads=(f"x1{j}",), writes=(f"xsq{j}",))
            B.step("vector", lambda e, j=j: e.reduce_sum(out=st[:, S2c:S2c+1], in_=xsq[j][:, :], axis=AX.X),
                   reads=(f"xsq{j}",), writes=("st",))
            B.step("vector", lambda e: e.tensor_scalar_mul(out=st[:, MEAN:MEAN+1], in0=st[:, S1c:S1c+1], scalar1=1.0/C), reads=("st",), writes=("st",))
            B.step("vector", lambda e: e.tensor_scalar_mul(out=st[:, EX2:EX2+1], in0=st[:, S2c:S2c+1], scalar1=1.0/C), reads=("st",), writes=("st",))
            B.step("vector", lambda e: e.tensor_scalar_mul(out=st[:, M2:M2+1], in0=st[:, MEAN:MEAN+1], scalar1=st[:, MEAN:MEAN+1]), reads=("st",), writes=("st",))
            B.step("vector", lambda e: e.tensor_sub(out=st[:, VAR:VAR+1], in0=st[:, EX2:EX2+1], in1=st[:, M2:M2+1]), reads=("st",), writes=("st",))
            B.step("scalar", lambda e: e.activation(out=st[:, SD:SD+1], in_=st[:, VAR:VAR+1], func=AF.Sqrt, bias=sb["fconst"][:, 0:1]), reads=("st",), writes=("st2",))
            B.step("vector", lambda e: e.reciprocal(out=st[:, RSTD:RSTD+1], in_=st[:, SD:SD+1]), reads=("st2",), writes=("st",))
            B.step("vector", lambda e, j=j: e.tensor_scalar(out=xnbf[j][:, :], in0=x1t[j][:, :],
                   scalar1=st[:, MEAN:MEAN+1], scalar2=st[:, RSTD:RSTD+1], op0=ALU.subtract, op1=ALU.mult),
                   reads=("st", f"x1{j}"), writes=(f"xnbf{j}",))
            for k in range(KC):
                pj = (pos * KC + k) % 2
                B.step("tensor", lambda e, j=j, k=k, pj=pj: e.transpose(
                    out=pstr[pj][:, :], in_=xnbf[j][:, k*128:(k+1)*128], identity=ident[:, :]),
                    reads=(f"xnbf{j}",), writes=(f"ptr{pj}",))
                B.step("vector", lambda e, j=j, k=k, pj=pj: e.tensor_copy(
                    out=stage[j][:, k*128:(k+1)*128], in_=pstr[pj][:, :]),
                    reads=(f"ptr{pj}",), writes=(f"stg{j}",))
            B.step("sync", lambda e, j=j, r0=r0: e.dma_start(
                out=d["xn2_fm"].rearrange("(k p) t -> p k t", p=128)[:, :, r0:r0+128],
                in_=stage[j].rearrange("p (k n) -> p k n", n=128)),
                reads=(f"stg{j}",), writes=(f"xnst{j}",))
        S.emit()


def phase_G(ctx):
    nc, blk, d, sb = ctx.nc, ctx.blk, ctx.d, ctx.sb
    S = Script(ctx, serial=not _phase_par("G"))
    B = BD(S)
    es = ExitStack()
    sbt = lambda name, shape, dt: es.enter_context(nc.sbuf_tensor(name, shape, dt))
    pst = lambda name, shape, dt: es.enter_context(nc.psum_tensor(name, shape, dt))
    xnt = sbt("pg_xnt", [128, 8 * T], BF16)
    ropec = sbt("pg_ropec", [128, T], F32)
    ropes = sbt("pg_ropes", [128, T], F32)
    maskT4 = sbt("pg_maskT4", [128, 2048], BF16)
    wsb = {nm: sbt(f"pg_w{nm}", [128, 8 * 128], BF16) for nm in ("q", "k", "v")}
    qf = sbt("pg_qf", [128, T], F32)
    kf = qf
    t1 = sbt("pg_t1", [128, T], BF16)
    t2 = sbt("pg_t2", [128, T], BF16)
    Qb = sbt("pg_Qb", [128, T], BF16)
    Kb = sbt("pg_Kb", [128, T], BF16)
    Vb = sbt("pg_Vb", [128, T], BF16)
    v0 = sbt("pg_v0", [64, 128], BF16)
    vtm = sbt("pg_vtm", [128, 64 * 65], BF16)
    Pb = [sbt(f"pg_P{j}", [128, 512], BF16) for j in range(2)]
    rn = sbt("pg_rn", [1, 512], F32)
    osb = sbt("pg_osb", [64, 512], F32)
    ob2 = [sbt(f"pg_ob2{j}", [64, 512], BF16) for j in range(2)]
    PS = pst("pg_PS", [128, 512], F32)
    PO = pst("pg_PO", [128, 512], F32)
    PB2 = pst("pg_PB2", [64, 512], F32)
    PP = [pst(f"pg_PP{j}", [128, 512], F32) for j in range(4)]
    PTv = pst("pg_PTv", [128, 64], BF16)
    o_scr = d["o_scr"]

    if True:
        B.step("sync", lambda e: e.dma_start(out=maskT4[:, :], in_=d["maskT4"]), writes=("maskT4l",))
        B.step("sync", lambda e: e.dma_start(out=ropec[:, :], in_=d["ropecos"]), writes=("ropec",))
        B.step("sync", lambda e: e.dma_start(out=ropes[:, :], in_=d["ropes2"] if False else d["ropesin"]), writes=("ropes",))
        B.step("sync", lambda e: e.dma_start(
            out=xnt.rearrange("p (k n) -> p k n", n=T),
            in_=d["xn2_fm"].rearrange("(k p) t -> p k t", p=128)), writes=("xnt",))
        for hp in range(8):
            for nm, wname, bname in (("q", "wqT", "bq"), ("k", "wk1T", "bk1"), ("v", "wv1T", "bv1")):
                B.step("gpsimd", lambda e, nm=nm, wname=wname, hp=hp: e.dma_start(
                    out=wsb[nm].rearrange("p (k n) -> p k n", n=128),
                    in_=d[wname].rearrange("(k p) n -> p k n", p=128)[:, :, hp*128:(hp+1)*128]),
                    writes=(f"w{nm}",))
            for nm, bname, dstk, dst in (("q", "bq", "qf", qf), ("v", "bv1", None, None)):
                for tt in range(8):
                    for k in range(KC):
                        B.step("tensor", lambda e, nm=nm, tt=tt, k=k: e.matmul(
                            out=PS[:, :], lhsT=wsb[nm][:, k*128:(k+1)*128],
                            rhs=xnt[:, k*T + tt*512:k*T + tt*512 + 512],
                            start=(k == 0), stop=False), reads=(f"w{nm}", "xnt"), writes=("PS",))
                    B.step("tensor", lambda e, nm=nm, bname=bname, hp=hp: e.matmul(
                        out=PS[:, :], lhsT=sb["brow_sb"][0:1, BROW[bname]*1024 + hp*128:BROW[bname]*1024 + (hp+1)*128],
                        rhs=sb["onesrow"][0:1, :], start=False, stop=True), writes=("PS",))
                    if nm == "v":
                        B.step("vector", lambda e, tt=tt: e.tensor_copy(out=Vb[:, tt*512:(tt+1)*512], in_=PS[:, :]),
                               reads=("PS",), writes=("Vb",))
                    else:
                        B.step("vector", lambda e, dst=dst, tt=tt: e.tensor_copy(out=dst[:, tt*512:(tt+1)*512], in_=PS[:, :]),
                               reads=("PS",), writes=(dstk,))
            for src, dst, srck, dstk in ((qf, Qb, "qf", "Qb"),):
                for h2 in range(2):
                    re0 = h2 * 64
                    im0 = h2 * 64 + 32
                    eng = "vector" if h2 == 0 else "gpsimd"
                    B.step(eng, lambda e, src=src, re0=re0: e.tensor_mul(
                        out=t1[re0:re0+32, :], in0=src[re0:re0+32, :], in1=ropec[re0:re0+32, :]),
                        reads=(srck, "ropec"), writes=("t1",))
                    B.step(eng, lambda e, src=src, re0=re0, im0=im0: e.tensor_mul(
                        out=t2[re0:re0+32, :], in0=src[im0:im0+32, :], in1=ropes[im0:im0+32, :]),
                        reads=(srck, "ropes"), writes=("t2",))
                    B.step(eng, lambda e, dst=dst, re0=re0: e.tensor_sub(
                        out=dst[re0:re0+32, :], in0=t1[re0:re0+32, :], in1=t2[re0:re0+32, :]),
                        reads=("t1", "t2"), writes=(dstk,))
                    B.step(eng, lambda e, src=src, re0=re0, im0=im0: e.tensor_mul(
                        out=t1[im0:im0+32, :], in0=src[re0:re0+32, :], in1=ropes[re0:re0+32, :]),
                        reads=(srck, "ropes", dstk), writes=("t1",))
                    B.step(eng, lambda e, src=src, re0=re0, im0=im0: e.tensor_mul(
                        out=t2[im0:im0+32, :], in0=src[im0:im0+32, :], in1=ropec[im0:im0+32, :]),
                        reads=(srck, "ropec"), writes=("t2",))
                    B.step(eng, lambda e, dst=dst, im0=im0: e.tensor_add(
                        out=dst[im0:im0+32, :], in0=t1[im0:im0+32, :], in1=t2[im0:im0+32, :]),
                        reads=("t1", "t2"), writes=(dstk,))
            for nm, bname in (("k", "bk1"),):
                for tt in range(8):
                    for k in range(KC):
                        B.step("tensor", lambda e, nm=nm, tt=tt, k=k: e.matmul(
                            out=PS[:, :], lhsT=wsb[nm][:, k*128:(k+1)*128],
                            rhs=xnt[:, k*T + tt*512:k*T + tt*512 + 512],
                            start=(k == 0), stop=False), reads=(f"w{nm}", "xnt"), writes=("PS",))
                    B.step("tensor", lambda e, nm=nm, bname=bname, hp=hp: e.matmul(
                        out=PS[:, :], lhsT=sb["brow_sb"][0:1, BROW[bname]*1024 + hp*128:BROW[bname]*1024 + (hp+1)*128],
                        rhs=sb["onesrow"][0:1, :], start=False, stop=True), writes=("PS",))
                    B.step("vector", lambda e, tt=tt: e.tensor_copy(out=kf[:, tt*512:(tt+1)*512], in_=PS[:, :]),
                           reads=("PS",), writes=("qf",))
            for src, dst, srck, dstk in ((kf, Kb, "qf", "Kb"),):
                for h2 in range(2):
                    re0 = h2 * 64
                    im0 = h2 * 64 + 32
                    eng = "vector" if h2 == 0 else "gpsimd"
                    B.step(eng, lambda e, src=src, re0=re0: e.tensor_mul(
                        out=t1[re0:re0+32, :], in0=src[re0:re0+32, :], in1=ropec[re0:re0+32, :]),
                        reads=(srck, "ropec"), writes=("t1",))
                    B.step(eng, lambda e, src=src, re0=re0, im0=im0: e.tensor_mul(
                        out=t2[re0:re0+32, :], in0=src[im0:im0+32, :], in1=ropes[im0:im0+32, :]),
                        reads=(srck, "ropes"), writes=("t2",))
                    B.step(eng, lambda e, dst=dst, re0=re0: e.tensor_sub(
                        out=dst[re0:re0+32, :], in0=t1[re0:re0+32, :], in1=t2[re0:re0+32, :]),
                        reads=("t1", "t2"), writes=(dstk,))
                    B.step(eng, lambda e, src=src, re0=re0, im0=im0: e.tensor_mul(
                        out=t1[im0:im0+32, :], in0=src[re0:re0+32, :], in1=ropes[re0:re0+32, :]),
                        reads=(srck, "ropes", dstk), writes=("t1",))
                    B.step(eng, lambda e, src=src, re0=re0, im0=im0: e.tensor_mul(
                        out=t2[im0:im0+32, :], in0=src[im0:im0+32, :], in1=ropec[im0:im0+32, :]),
                        reads=(srck, "ropec"), writes=("t2",))
                    B.step(eng, lambda e, dst=dst, im0=im0: e.tensor_add(
                        out=dst[im0:im0+32, :], in0=t1[im0:im0+32, :], in1=t2[im0:im0+32, :]),
                        reads=("t1", "t2"), writes=(dstk,))
            B.step("vector", lambda e: e.memset(vtm[:, :], 1.0), writes=("vtm",))
            for sc in range(32):
                for h2 in range(2):
                    pj = (sc * 2 + h2) % 2
                    if h2 == 1:
                        B.step("vector", lambda e, sc=sc: e.tensor_copy(out=v0[:, :], in_=Vb[64:128, sc*128:(sc+1)*128]),
                               reads=("Vb",), writes=("v0",))
                    B.step("tensor", lambda e, sc=sc, h2=h2: e.transpose(
                        out=PTv[:, :],
                        in_=(Vb[0:64, sc*128:(sc+1)*128] if h2 == 0 else v0[:, :]),
                        identity=sb["ident"][0:64, 0:64]),
                        reads=("Vb" if h2 == 0 else "v0",), writes=("PTv",))
                    B.step("vector", lambda e, sc=sc, h2=h2: e.tensor_copy(
                        out=vtm[:, (sc*2+h2)*65:(sc*2+h2)*65+64], in_=PTv[:, :]),
                        reads=("PTv",), writes=("vtm",))
            for h2 in range(2):
                hb = h2 * 64
                for qt in range(8):
                    ns = (qt + 1) * 4
                    for stt in range(ns):
                        B.step("tensor", lambda e, h2=h2, hb=hb, qt=qt, stt=stt: e.matmul(
                            out=PS[:, :], lhsT=Kb[hb:hb+64, stt*128:(stt+1)*128],
                            rhs=Qb[hb:hb+64, qt*512:(qt+1)*512], start=True, stop=True),
                            reads=("Kb", "Qb"), writes=("PS",))
                        pj = stt % 2
                        B.step("scalar", lambda e, pj=pj: e.activation(
                            out=Pb[pj][:, :], in_=PS[:, :], func=AF.Exp, scale=0.125,
                            bias=sb["fconst"][:, 1:2]), reads=("PS",), writes=(f"Pb{pj}",))
                        if stt >= qt * 4:
                            jm = stt - qt * 4
                            B.step("vector", lambda e, pj=pj, jm=jm: e.tensor_mul(
                                out=Pb[pj][:, :], in0=Pb[pj][:, :], in1=maskT4[:, jm*512:(jm+1)*512]),
                                reads=(f"Pb{pj}", "maskT4l"), writes=(f"Pb{pj}",))
                        B.step("tensor", lambda e, pj=pj, stt=stt, h2=h2, ns=ns: e.matmul(
                            out=PO[0:65, :], lhsT=vtm[:, (stt*2+h2)*65:(stt*2+h2)*65+65],
                            rhs=Pb[pj][:, :], start=(stt == 0), stop=(stt == ns - 1)),
                            reads=(f"Pb{pj}", "vtm"), writes=("PO",))
                    B.step("vector", lambda e: e.reciprocal(out=rn[:, :], in_=PO[64:65, :]), reads=("PO",), writes=("rn",))
                    B.step("vector", lambda e: e.tensor_copy(out=osb[:, :], in_=PO[0:64, :]), reads=("PO",), writes=("osb",))
                    B.step("tensor", lambda e: e.matmul(out=PB2[:, :], lhsT=sb["ones16f"][0:1, 0:64], rhs=rn[:, :],
                                                        start=True, stop=True), reads=("rn", "osb"), writes=("PB2",))
                    oj = qt % 2
                    B.step("vector", lambda e, oj=oj: e.tensor_mul(
                        out=ob2[oj][:, :], in0=osb[:, :], in1=PB2[:, :]),
                        reads=("osb", "PB2"), writes=(f"ob2{oj}",))
                    r0 = hp * 128 + h2 * 64
                    B.step("sync", lambda e, oj=oj, r0=r0, qt=qt: e.dma_start(
                        out=o_scr[r0:r0+64, qt*512:(qt+1)*512], in_=ob2[oj][:, :]),
                        reads=(f"ob2{oj}",), writes=("ost",))
        es.close()
        es2 = ExitStack()
        globals()
        ot = [es2.enter_context(nc.sbuf_tensor(f"pg_ot{j}", [128, 8 * 512], BF16)) for j in range(2)]
        s2t = [es2.enter_context(nc.sbuf_tensor(f"pg_s2{j}", [128, 8 * 512], BF16)) for j in range(2)]
        wps = es2.enter_context(nc.sbuf_tensor("pg_wps", [128, 8 * 128], BF16))
        for i in range(8):
            j = i % 2
            t0 = i * 512
            B.step("sync", lambda e, j=j, t0=t0: e.dma_start(
                out=ot[j].rearrange("p (k n) -> p k n", n=512),
                in_=o_scr.rearrange("(k p) t -> p k t", p=128)[:, :, t0:t0+512]),
                reads=("ost", "obf", "vtm", "Qb", "Kb", "Vb", "xnt"), writes=(f"ot{j}",))
            for m in range(KC):
                B.step("gpsimd", lambda e, m=m: e.dma_start(
                    out=wps.rearrange("p (k n) -> p k n", n=128),
                    in_=d["wpT"].rearrange("(k p) n -> p k n", p=128)[:, :, m*128:(m+1)*128]),
                    reads=("ost",), writes=("wq",))
                pj = m % 4
                for k in range(KC):
                    B.step("tensor", lambda e, j=j, k=k, pj=pj: e.matmul(
                        out=PP[pj][:, :], lhsT=wps[:, k*128:(k+1)*128],
                        rhs=ot[j][:, k*512:(k+1)*512], start=(k == 0), stop=False),
                        reads=("wq", f"ot{j}"), writes=(f"PPt{pj}",))
                B.step("tensor", lambda e, m=m, pj=pj: e.matmul(
                    out=PP[pj][:, :], lhsT=sb["brow_sb"][0:1, BROW["bp"]*1024 + m*128:BROW["bp"]*1024 + (m+1)*128],
                    rhs=sb["onesrow"][0:1, :], start=False, stop=True), writes=(f"PPt{pj}",))
                B.step("vector", lambda e, j=j, m=m, pj=pj: e.tensor_copy(
                    out=s2t[j][:, m*512:(m+1)*512], in_=PP[pj][:, :]), reads=(f"PPt{pj}",), writes=(f"s2{j}",))
            B.step("sync", lambda e, j=j, t0=t0: e.dma_start(
                out=d["s2_fm"].rearrange("(k p) t -> p k t", p=128)[:, :, t0:t0+512],
                in_=s2t[j].rearrange("p (k n) -> p k n", n=512)), reads=(f"s2{j}",), writes=(f"s2st{j}",))
        S.emit()
        es2.close()


def phase_H(ctx):
    nc, blk, d, sb = ctx.nc, ctx.blk, ctx.d, ctx.sb
    import os as _os
    _serh = _os.environ.get("KB_SER_H")
    S = Script(ctx, serial=(_serh if _serh is not None else not _phase_par("H")),
               serial_first=int(_os.environ.get("KB_SERF_H", "40")))
    _sec = _os.environ.get("KB_SERIAL_SEC", "")
    B = BD(S)
    es = ExitStack()
    sbt = lambda name, shape, dt: es.enter_context(nc.sbuf_tensor(name, shape, dt))
    pst = lambda name, shape, dt: es.enter_context(nc.psum_tensor(name, shape, dt))
    _xnt0 = sbt("ph_xnt0", [128, 8 * 1024], BF16)
    xnt = [_xnt0, _xnt0]
    wg = [sbt(f"ph_wg{j}", [128, 8 * 128], BF16) for j in range(2)]
    wu = [sbt(f"ph_wu{j}", [128, 8 * 128], BF16) for j in range(2)]
    wgi = [sbt(f"ph_wgi{j}", [128, 8 * 128], I32) for j in range(2)]
    wui = [sbt(f"ph_wui{j}", [128, 8 * 128], I32) for j in range(2)]
    wd = [sbt(f"ph_wd{j}", [128, 512], BF16) for j in range(4)]
    wdi = [sbt(f"ph_wdi{j}", [128, 512], I32) for j in range(4)]
    hb = [sbt(f"ph_hb{m}", [128, 1024], BF16) for m in range(24)]
    t1b = sbt("ph_t1", [128, 1024], BF16)
    t2b = sbt("ph_t2", [128, 1024], BF16)
    g2t = sbt("ph_g2", [128, 8 * 1024], BF16)
    s2l = sbt("ph_s2l", [128, 8 * 1024], BF16)
    prod = sbt("ph_prod", [128, 8 * 1024], BF16)
    x1l = [sbt(f"ph_x1{j}", [128, 1024], F32) for j in range(2)]
    yt = [sbt(f"ph_y{j}", [128, 1024], F32) for j in range(2)]
    PS = [pst(f"ph_PS{j}", [128, 512], F32) for j in range(6)]
    ptr = [pst(f"ph_ptr{j}", [128, 128], BF16) for j in range(2)]
    SCL = 1.0 / 127.0
    ident = sb["ident"]

    with es:
        for it in range(4):
            t0 = it * 1024
            jx = it % 2
            B.step("sync", lambda e, jx=jx, t0=t0: e.dma_start(
                out=xnt[jx].rearrange("p (k n) -> p k n", n=1024),
                in_=d["xn2_fm"].rearrange("(k p) t -> p k t", p=128)[:, :, t0:t0 + 1024]), writes=(f"xnt{jx}",))
            pass
            for m in range(24):
                j = m % 2
                B.step("gpsimd", lambda e, j=j, m=m: e.dma_start(
                    out=wgi[j].rearrange("p (k n) -> p k n", n=128),
                    in_=d["gateT"].rearrange("(k p) n -> p k n", p=128)[:, :, m*128:(m+1)*128]), writes=(f"wgi{j}",))
                B.step("gpsimd", lambda e, j=j, m=m: e.dma_start(
                    out=wui[j].rearrange("p (k n) -> p k n", n=128),
                    in_=d["upT"].rearrange("(k p) n -> p k n", p=128)[:, :, m*128:(m+1)*128]), writes=(f"wui{j}",))
                B.step("vector", lambda e, j=j: e.tensor_copy(out=wg[j][:, :], in_=wgi[j][:, :]),
                       reads=(f"wgi{j}",), writes=(f"wg{j}",))
                B.step("vector", lambda e, j=j: e.tensor_copy(out=wu[j][:, :], in_=wui[j][:, :]),
                       reads=(f"wui{j}",), writes=(f"wu{j}",))
                for half in range(2):
                    pa, pb = PS[half], PS[2 + half]
                    for k in range(KC):
                        B.step("tensor", lambda e, j=j, jx=jx, k=k, half=half, pa=pa: e.matmul(
                            out=pa[:, :], lhsT=wg[j][:, k*128:(k+1)*128],
                            rhs=xnt[jx][:, k*1024 + half*512:k*1024 + half*512 + 512],
                            start=(k == 0), stop=(k == KC - 1)), reads=(f"wg{j}", f"xnt{jx}"), writes=(f"bank{half}",))
                    for k in range(KC):
                        B.step("tensor", lambda e, j=j, jx=jx, k=k, half=half, pb=pb: e.matmul(
                            out=pb[:, :], lhsT=wu[j][:, k*128:(k+1)*128],
                            rhs=xnt[jx][:, k*1024 + half*512:k*1024 + half*512 + 512],
                            start=(k == 0), stop=(k == KC - 1)), reads=(f"wu{j}", f"xnt{jx}"), writes=(f"bank{2+half}",))
                    B.step("scalar", lambda e, half=half, pa=pa: e.activation(
                        out=t1b[:, half*512:half*512+512], in_=pa[:, :], func=AF.Silu, scale=SCL),
                        reads=(f"bank{half}",), writes=("t1b",))
                    B.step("vector", lambda e, half=half, pb=pb: e.tensor_scalar_mul(
                        out=t2b[:, half*512:half*512+512], in0=pb[:, :], scalar1=SCL),
                        reads=(f"bank{2+half}",), writes=("t2b",))
                    B.step("vector", lambda e, m=m, half=half: e.tensor_mul(
                        out=hb[m][:, half*512:half*512+512], in0=t1b[:, half*512:half*512+512],
                        in1=t2b[:, half*512:half*512+512]), reads=("t1b", "t2b", "downdone"), writes=(f"hb{m}",))
            pass
            for m0, m1 in ((0, 3), (3, 6), (6, 8)):
                nmi = m1 - m0
                for k in range(24):
                    j = k % 4
                    B.step("sync", lambda e, j=j, k=k, m0=m0, m1=m1: e.dma_start(
                        out=wdi[j][:, 0:(m1-m0)*128],
                        in_=d["downT"][k*128:(k+1)*128, m0*128:m1*128]), writes=(f"wdi{j}",))
                    B.step("vector", lambda e, j=j, m0=m0, m1=m1: e.tensor_copy(
                        out=wd[j][:, 0:(m1-m0)*128], in_=wdi[j][:, 0:(m1-m0)*128]),
                        reads=(f"wdi{j}",), writes=(f"wd{j}",))
                    for mi in range(nmi):
                        for half in range(2):
                            ps = PS[mi * 2 + half]
                            B.step("tensor", lambda e, j=j, k=k, mi=mi, half=half, ps=ps: e.matmul(
                                out=ps[:, :], lhsT=wd[j][:, mi*128:(mi+1)*128],
                                rhs=hb[k][:, half*512:half*512+512],
                                start=(k == 0), stop=(k == 23)),
                                reads=(f"wd{j}", f"hb{k}"), writes=(f"bank{mi*2+half}",))
                for mi in range(nmi):
                    m = m0 + mi
                    for half in range(2):
                        ps = PS[mi * 2 + half]
                        wmark = ("g2t", "downdone") if (m0 == 6 and mi == nmi - 1 and half == 1) else ("g2t",)
                        B.step("scalar", lambda e, ps=ps, m=m, half=half: e.activation(
                            out=g2t[:, m*1024 + half*512:m*1024 + half*512 + 512],
                            in_=ps[:, :], func=AF.Silu, scale=SCL),
                            reads=(f"bank{mi*2+half}",), writes=wmark)
            pass
            B.step("sync", lambda e, jx=jx, t0=t0: e.dma_start(
                out=s2l.rearrange("p (k n) -> p k n", n=1024),
                in_=d["s2_fm"].rearrange("(k p) t -> p k t", p=128)[:, :, t0:t0 + 1024]), writes=("s2l",))
            for k in range(KC):
                B.step("vector", lambda e, k=k: e.tensor_mul(
                    out=prod[:, k*1024:(k+1)*1024], in0=s2l[:, k*1024:(k+1)*1024],
                    in1=g2t[:, k*1024:(k+1)*1024]), reads=("s2l", "g2t"), writes=("prod",))
            for sub in range(8):
                j = sub % 2
                r0 = t0 + sub * 128
                B.step("sync", lambda e, j=j, r0=r0: e.dma_start(out=x1l[j][:, :], in_=d["x1_tm"][r0:r0+128, :]),
                       writes=(f"x1l{j}",))
                for k in range(KC):
                    pj = (sub * KC + k) % 2
                    B.step("tensor", lambda e, k=k, sub=sub, pj=pj: e.transpose(
                        out=ptr[pj][:, :], in_=prod[:, k*1024 + sub*128:k*1024 + sub*128 + 128],
                        identity=ident[:, :]), reads=("prod",), writes=(f"ptr{pj}",))
                    B.step("vector", lambda e, j=j, k=k, pj=pj: e.tensor_add(
                        out=yt[j][:, k*128:(k+1)*128], in0=ptr[pj][:, :], in1=x1l[j][:, k*128:(k+1)*128]),
                        reads=(f"ptr{pj}", f"x1l{j}"), writes=(f"yt{j}",))
                B.step("sync", lambda e, j=j, r0=r0: e.dma_start(out=d["y"][r0:r0+128, :], in_=yt[j][:, :]),
                       reads=(f"yt{j}",), writes=(f"yst{j}",))
        S.emit()


def _run(phase_groups=None):
    pass

def kernel(**inputs):
    import os
    import numpy as np
    from concourse.bass_utils import run_bass_kernel_spmd
    groups = os.environ.get("KB_GROUPS", "ABCDEFGH").split(",")
    consts = host_consts()
    vecs = host_vecs(inputs)
    weights = host_weights(inputs)
    x = np.asarray(inputs["L0_attn_x"], np.float32)
    ncores = x.shape[0]
    common = dict(consts)
    common["vecs"] = vecs
    common.update(weights)
    state = [dict() for _ in range(ncores)]
    if int(os.environ.get("KB_WARM_LAUNCH", "0")):
        import sys
        import concourse.bass as _bass
        def _build_warm():
            nc = _bass.Bass("TRN2", target_bir_lowering=False)
            xw = nc.dram_tensor("xw", [128, 512], F32, kind="ExternalInput").ap()
            yw = nc.dram_tensor("yw", [128, 512], F32, kind="ExternalOutput").ap()
            with (nc.sbuf_tensor("w_x", [128, 512], F32) as w_x,
                  nc.sbuf_tensor("w_b", [128, 512], BF16) as w_b,
                  nc.sbuf_tensor("w_y", [128, 512], F32) as w_y,
                  nc.psum_tensor("w_ps", [128, 512], F32) as w_ps,
                  nc.semaphore("w_ld") as w_ld,
                  nc.semaphore("w_mm") as w_mm,
                  nc.semaphore("w_cp") as w_cp,
                  nc.semaphore("w_st") as w_st,
                  nc.Block() as blkw):
                @blkw.sync
                def _(sync):
                    sync.dma_start(out=w_x[:, :], in_=xw).then_inc(w_ld, 16)
                    sync.wait_ge(w_cp, 3)
                    sync.dma_start(out=yw, in_=w_y[:, :]).then_inc(w_st, 16)
                    sync.wait_ge(w_st, 16)
                @blkw.vector
                def _(v):
                    v.wait_ge(w_ld, 16)
                    v.tensor_copy(out=w_b[:, :], in_=w_x[:, :])
                    v.tensor_scalar(out=w_b[:, :], in0=w_b[:, :], scalar1=w_x[:, 0:1],
                                    scalar2=w_x[:, 1:2], op0=ALU.mult, op1=ALU.add).then_inc(w_cp, 1)
                    v.wait_ge(w_mm, 1)
                    v.tensor_add(out=w_y[:, :], in0=w_ps[:, :], in1=w_x[:, :]).then_inc(w_cp, 1)
                @blkw.scalar
                def _(s):
                    s.wait_ge(w_ld, 16)
                    for f in (AF.Sigmoid, AF.Silu, AF.Exp, AF.Ln, AF.Gelu, AF.Sqrt, AF.Square):
                        s.activation(out=w_y[:, 0:4], in_=w_x[:, 0:4], func=f)
                    s.nop().then_inc(w_cp, 1)
                @blkw.tensor
                def _(t):
                    t.wait_ge(w_cp, 1)
                    t.matmul(out=w_ps[:, :], lhsT=w_b[:, 0:128], rhs=w_b[:, :], start=True, stop=True).then_inc(w_mm, 1)
                @blkw.gpsimd
                def _(g):
                    g.wait_ge(w_ld, 16)
                    g.tensor_copy(out=w_b[:, 0:64], in_=w_b[:, 64:128])
            return nc
        xw = np.zeros((128, 512), np.float32) + 1.0
        for attempt in range(int(os.environ.get("KB_WARM_RUNS", "3"))):
            try:
                ncw = _build_warm()
                run_bass_kernel_spmd(ncw, [{"xw": xw}] * ncores, core_ids=list(range(ncores)))
                print(f"[kernel] warm {attempt} ok", file=sys.stderr, flush=True)
            except Exception as e:
                print(f"[kernel] warm {attempt} raised {type(e).__name__}", file=sys.stderr, flush=True)
    import sys, time as _t
    import ml_dtypes as _mld
    _np_dt = {BF16: _mld.bfloat16, F32: np.float32, I32: np.int32}
    for g in groups:
        print(f"[kernel] build {g} {_t.strftime('%H:%M:%S')}", file=sys.stderr, flush=True)
        nc_g = build_nc(g, standalone=True)
        names_g = list(LAST_INPUTS)
        in_maps = []
        for b in range(ncores):
            feed = {}
            for name in names_g:
                if name == "x":
                    feed[name] = np.ascontiguousarray(x[b])
                elif name in common:
                    feed[name] = common[name]
                else:
                    feed[name] = state[b][name]
            in_maps.append(feed)
        print(f"[kernel] group {g} run {_t.strftime('%H:%M:%S')}", file=sys.stderr, flush=True)
        if len(groups) > 1:
            try:
                run_bass_kernel_spmd(nc_g, in_maps, core_ids=list(range(ncores)))
            except Exception as e:
                print(f"[kernel] group {g} first run raised {type(e).__name__}", file=sys.stderr, flush=True)
        res = run_bass_kernel_spmd(nc_g, in_maps, core_ids=list(range(ncores)))
        for b in range(ncores):
            state[b].update(res.results[b])
    return np.stack([state[b]["y"] for b in range(ncores)], axis=0).astype(np.float32)


# version 18 (final)
# speedup vs baseline: 1.1945x; 1.1945x over previous
import numpy as np
from contextlib import ExitStack
import concourse.bass as bass
import concourse.mybir as mybir

F32 = mybir.dt.float32
BF16 = mybir.dt.bfloat16
I32 = mybir.dt.int32
AF = mybir.ActivationFunctionType
ALU = mybir.AluOpType
AX = mybir.AxisListType

T, C, H, D, L = 4096, 1024, 16, 64, 128
NCH = T // L
KC = C // 128
NT = T // 128


def host_consts():
    import ml_dtypes
    c = {}
    c["ident"] = np.eye(128, dtype=ml_dtypes.bfloat16)
    c["identf64"] = np.eye(64, dtype=np.float32)
    f = np.arange(128, dtype=np.float32)
    c["iota_pf"] = (f[:, None] - f[None, :]).astype(np.float32)
    c["iota_f"] = np.broadcast_to(f[None, :], (128, 128)).astype(np.float32).copy()
    c["m_lowstrict"] = (f[None, :] < f[:, None]).astype(np.float32)
    c["m_upstrict"] = (f[None, :] > f[:, None]).astype(np.float32)
    c["m_upincl"] = (f[None, :] >= f[:, None]).astype(np.float32)
    c["ones16f"] = np.ones((16, 128), np.float32)
    c["ones"] = np.ones((128, 128), dtype=ml_dtypes.bfloat16)
    c["hsel2"] = np.zeros((16, 128), dtype=np.float32)
    for h in range(16):
        c["hsel2"][h, (h % 2) * 64:(h % 2) * 64 + 64] = 1.0
    c["halfsel"] = np.zeros((16, 8), dtype=np.float32)
    for h in range(16):
        c["halfsel"][h, h // 2] = 1.0
    half = D // 2
    i = np.arange(half, dtype=np.float32)
    alpha = 1.0 / 10000.0 ** (2.0 * i / D)
    ang = np.arange(T, dtype=np.float32)[None, :] * alpha[:, None]
    c["ropecos"] = np.tile(np.cos(ang), (4, 1)).astype(np.float32)
    c["ropesin"] = np.tile(np.sin(ang), (4, 1)).astype(np.float32)
    s = np.arange(128)[:, None]
    q = np.arange(512)[None, :]
    mt = np.concatenate([(q >= s + 128 * j).astype(np.float32) for j in range(4)], axis=1)
    c["maskT4"] = mt.astype(ml_dtypes.bfloat16)
    fc = np.zeros((128, 8), np.float32)
    fc[:, 0] = 1e-5
    fc[:, 1] = -10.0
    fc[:, 2] = 1.0
    fc[:, 3] = 128.0
    c["fconst"] = fc
    hs = np.zeros((128, 2), np.float32)
    hs[0:64, 0] = 1.0
    hs[64:128, 1] = 1.0
    c["hselcol"] = hs.astype(ml_dtypes.bfloat16)
    bc = np.zeros((2, 128), np.float32)
    bc[0, 0:64] = 1.0
    bc[1, 64:128] = 1.0
    c["bc2"] = bc.astype(ml_dtypes.bfloat16)
    c["onesrow"] = np.ones((1, 512), dtype=ml_dtypes.bfloat16)
    return c

VEC_LAYOUT = {}

def host_vecs(inp):
    VEC_LAYOUT.clear()
    cols = []
    def add(name, v, ncols):
        v = np.asarray(v, dtype=np.float32).reshape(-1)
        col0 = sum(x.shape[1] for x in cols)
        VEC_LAYOUT[name] = (col0, ncols)
        if v.size == 16:
            pad = np.zeros((128, 1), np.float32)
            pad[:16, 0] = v
            cols.append(pad)
        else:
            assert v.size == ncols * 128
            cols.append(v.reshape(ncols, 128).T.copy())
    add("b_r", inp["L0_attn_b_r"], 8); add("b_k", inp["L0_attn_b_k"], 8)
    add("b_v", inp["L0_attn_b_v"], 8); add("b_g", inp["L0_attn_b_g"], 8)
    add("b_o", inp["L0_attn_b_o"], 8)
    add("gnw", inp["L0_attn_gn_weight"], 8); add("gnb", inp["L0_attn_gn_bias"], 8)
    add("tmr", inp["L0_attn_time_mix_r"], 8); add("tmk", inp["L0_attn_time_mix_k"], 8)
    add("tmv", inp["L0_attn_time_mix_v"], 8); add("tmg", inp["L0_attn_time_mix_g"], 8)
    add("s1", inp["L0_ffn_w1_s"], 32); add("s2", inp["L0_ffn_w2_s"], 32)
    add("sh1", inp["L0_ffn_h1_s"], 32); add("sh2", inp["L0_ffn_h2_s"], 32)
    add("s3", inp["L0_ffn_w3_s"], 8)
    add("bq", inp["L1_attn_bq"], 8); add("bk1", inp["L1_attn_bk"], 8)
    add("bv1", inp["L1_attn_bv"], 8); add("bp", inp["L1_attn_bp"], 8)
    add("decay", inp["L0_attn_decay"], 1); add("eta", inp["L0_attn_eta"], 1)
    return np.concatenate(cols, axis=1)

def host_weights(inp):
    w = {}
    for nm, src in [("wrT", "L0_attn_W_r"), ("wkT", "L0_attn_W_k"), ("wvT", "L0_attn_W_v"),
                    ("wgT", "L0_attn_W_g"), ("woT", "L0_attn_W_o"),
                    ("w1T", "L0_ffn_w1_w"), ("w2T", "L0_ffn_w2_w"),
                    ("h1T", "L0_ffn_h1_w"), ("h2T", "L0_ffn_h2_w"), ("w3T", "L0_ffn_w3_w"),
                    ("wqT", "L1_attn_wq"), ("wk1T", "L1_attn_wk"), ("wv1T", "L1_attn_wv"),
                    ("wpT", "L1_attn_wp")]:
        w[nm] = np.ascontiguousarray(np.asarray(inp[src]).T)
    import ml_dtypes
    brow = np.concatenate([np.asarray(inp[k], np.float32).reshape(1, -1) for k in
                           ("L0_attn_b_r", "L0_attn_b_k", "L0_attn_b_v", "L0_attn_b_g",
                            "L0_attn_b_o", "L1_attn_bq", "L1_attn_bk", "L1_attn_bv",
                            "L1_attn_bp")], axis=1)
    w["brow"] = brow.astype(ml_dtypes.bfloat16)
    w["gateT"] = np.ascontiguousarray(np.asarray(inp["L1_ffn_w_gate"]).T)
    w["upT"] = np.ascontiguousarray(np.asarray(inp["L1_ffn_w_up"]).T)
    w["downT"] = np.ascontiguousarray(np.asarray(inp["L1_ffn_w_down"]).T)
    return w

CONST_SPECS = {
    "ident": ([128, 128], BF16), "identf64": ([64, 64], F32),
    "iota_pf": ([128, 128], F32), "iota_f": ([128, 128], F32),
    "m_lowstrict": ([128, 128], F32), "m_upincl": ([128, 128], F32),
    "m_upstrict": ([128, 128], F32), "ones16f": ([16, 128], F32),
    "ones": ([128, 128], BF16), "hsel2": ([16, 128], F32), "halfsel": ([16, 8], F32),
    "ropecos": ([128, T], F32), "ropesin": ([128, T], F32), "maskT4": ([128, 2048], BF16),
    "fconst": ([128, 8], F32), "hselcol": ([128, 2], BF16), "bc2": ([2, 128], BF16),
    "onesrow": ([1, 512], BF16),
}
WEIGHT_SPECS = {
    "wrT": ([C, C], F32), "wkT": ([C, C], F32), "wvT": ([C, C], F32),
    "wgT": ([C, C], F32), "woT": ([C, C], F32),
    "w1T": ([C, 4096], F32), "w2T": ([C, 4096], F32),
    "h1T": ([4096, 4096], F32), "h2T": ([4096, 4096], F32), "w3T": ([4096, C], F32),
    "wqT": ([C, C], F32), "wk1T": ([C, C], F32), "wv1T": ([C, C], F32), "wpT": ([C, C], F32),
    "gateT": ([C, 3072], I32), "upT": ([C, 3072], I32), "downT": ([3072, C], I32),
    "brow": ([1, 9216], BF16),
}
SCRATCH_SPECS = {
    "xn_fm": ([C, T], BF16), "r_fm": ([C, T], BF16), "k_fm": ([C, T], BF16),
    "v_fm": ([C, T], BF16), "g0_fm": ([C, T], BF16), "wkv_fm": ([C, T], BF16),
    "stats": ([2, 16], F32), "s_fm": ([C, T], BF16), "gf_fm": ([C, T], BF16),
    "x1_tm": ([T, C], F32), "xn2_fm": ([C, T], BF16), "s2_fm": ([C, T], BF16),
    "o_scr": ([C, T], BF16),
}
PHASE_IO = {
    "A": (["x"], ["xn_fm"]),
    "B": (["xn_fm", "wrT", "wkT", "wvT", "wgT", "brow"], ["r_fm", "k_fm", "v_fm", "g0_fm"]),
    "C": (["r_fm", "k_fm", "v_fm"], ["wkv_fm", "stats"]),
    "D": (["wkv_fm", "stats", "g0_fm", "woT", "brow"], ["s_fm"]),
    "E": (["xn_fm", "w1T", "w2T", "h1T", "h2T", "w3T"], ["gf_fm"]),
    "F": (["x", "s_fm", "gf_fm"], ["x1_tm", "xn2_fm"]),
    "G": (["xn2_fm", "wqT", "wk1T", "wv1T", "wpT", "brow"], ["s2_fm", "o_scr"]),
    "H": (["xn2_fm", "x1_tm", "s2_fm", "gateT", "upT", "downT"], ["y"]),
}

class Sem:
    __slots__ = ("h", "n")
    def __init__(self, h):
        self.h = h
        self.n = 0
    def inc(self, ins, k=1):
        ins.then_inc(self.h, k)
        self.n += k
        return ins

class Ctx:
    def __init__(self, nc, blk, stack, d, nvec):
        self.nc, self.blk, self.stack, self.d, self.nvec = nc, blk, stack, d, nvec
        self.sems = {}
        for name in ("dml", "dmg", "dms", "mm", "ve", "ac", "ev", "gp", "pb", "aux", "aux2", "aux3"):
            self.sems[name] = Sem(stack.enter_context(nc.semaphore(name)))
        self.pb_target = 0
        self.sb = {}

    def s(self, name):
        return self.sems[name]

def vcol(name, j=0):
    c0, n = VEC_LAYOUT[name]
    assert j < n, (name, j, n)
    return c0 + j

LAST_INPUTS = []

def build_nc(phases, standalone=True, expose=()):
    nvec = sum(n for _, n in VEC_LAYOUT.values())
    nc = bass.Bass("TRN2", target_bir_lowering=False)
    used_in, used_out = set(), set()
    for p in phases:
        i, o = PHASE_IO[p]
        used_in |= set(i)
        used_out |= set(o)
    d = {}
    LAST_INPUTS.clear()
    def din(name, shape, dt):
        LAST_INPUTS.append(name)
        return nc.dram_tensor(name, shape, dt, kind="ExternalInput").ap()
    if "x" in used_in:
        d["x"] = din("x", [T, C], F32)
    if "y" in used_out:
        d["y"] = nc.dram_tensor("y", [T, C], F32, kind="ExternalOutput").ap()
    for name, (shape, dt) in WEIGHT_SPECS.items():
        if name in used_in:
            d[name] = din(name, shape, dt)
    for name, (shape, dt) in CONST_SPECS.items():
        d[name] = din(name, shape, dt)
    d["vecs"] = din("vecs", [128, nvec], F32)
    for name, (shape, dt) in SCRATCH_SPECS.items():
        if name in (used_in | used_out):
            produced = name in used_out
            consumed = name in used_in
            if standalone or name in expose:
                kind = "ExternalInput" if (consumed and not produced) else "ExternalOutput"
            else:
                kind = "Internal"
            if kind == "ExternalInput":
                LAST_INPUTS.append(name)
            d[name] = nc.dram_tensor(name, shape, dt, kind=kind).ap()
    stack = ExitStack()
    with stack:
        with nc.Block() as blk:
            ctx = Ctx(nc, blk, stack, d, nvec)
            prelude(ctx)
            for p in phases:
                globals()[f"phase_{p}"](ctx)
            finale(ctx)
    return nc

PHASE_LOCAL_CONSTS = ("ropecos", "ropesin", "maskT4", "iota_pf", "iota_f",
                      "m_lowstrict", "m_upstrict", "m_upincl", "identf64")

def prelude(ctx):
    nc, blk, d = ctx.nc, ctx.blk, ctx.d
    sb = ctx.sb
    names = [n for n in CONST_SPECS if n not in PHASE_LOCAL_CONSTS]
    for name in names:
        shape, dt = CONST_SPECS[name]
        sb[name] = ctx.stack.enter_context(nc.sbuf_tensor("c_" + name, shape, dt))
    sb["vecs"] = ctx.stack.enter_context(nc.sbuf_tensor("c_vecs", [128, ctx.nvec], F32))
    if "brow" in d:
        sb["brow_sb"] = ctx.stack.enter_context(nc.sbuf_tensor("c_brow", [1, 9216], BF16))
    dml = ctx.s("dml")
    sb["acwarm"] = ctx.stack.enter_context(nc.sbuf_tensor("c_acwarm", [128, 8], F32))
    sb["vwarm"] = ctx.stack.enter_context(nc.sbuf_tensor("c_vwarm", [128, 512], F32))
    vwb = ctx.stack.enter_context(nc.sbuf_tensor("c_vwb", [128, 512], BF16))
    vwb2 = ctx.stack.enter_context(nc.sbuf_tensor("c_vwb2", [128, 512], BF16))
    _warm_es = ExitStack()
    warm_ps = _warm_es.enter_context(nc.psum_tensor("c_warmps", [128, 512], F32))
    warm_psb = _warm_es.enter_context(nc.psum_tensor("c_warmpsb", [128, 128], BF16))
    wsem = ctx.stack.enter_context(nc.semaphore("warmsem"))

    @blk.tensor
    def _(tensor):
        tensor.wait_ge(wsem, 1)
        for i in range(8):
            tensor.matmul(out=warm_ps[:, 0:128], lhsT=vwb[:, 0:128], rhs=vwb[:, 0:128],
                          start=True, stop=True)
        tensor.matmul(out=warm_ps[:, 0:128], lhsT=sb["vwarm"][:, 0:128], rhs=sb["vwarm"][:, 0:128],
                      start=True, stop=True)
        for _ in range(3):
            tensor.matmul(out=warm_ps[0:16, 0:1], lhsT=sb["ones16f"][0:1, 0:16], rhs=sb["ones16f"][0:1, 0:1],
                          start=True, stop=True)
            tensor.matmul(out=warm_ps[:, 0:8], lhsT=sb["ones16f"][:, :], rhs=sb["halfsel"][:, 0:8],
                          start=True, stop=True)
            tensor.matmul(out=warm_ps[:, 0:8], lhsT=sb["hsel2"][:, :], rhs=sb["halfsel"][:, :],
                          start=True, stop=True)
        tensor.transpose(out=warm_psb[:, :], in_=vwb[:, 0:128], identity=vwb2[:, 0:128])
        tensor.nop().then_inc(wsem, 1)
        ctx.s("pb").inc(tensor.nop(), 1)

    @blk.vector
    def _(vector):
        v = vector
        for _warmrep in range(2):
            v.memset(sb["vwarm"][:, :], 1.0)
            v.memset(vwb[:, :], 1.0)
            v.memset(vwb2[:, :], 1.0)
            v.tensor_copy(out=vwb[:, :], in_=sb["vwarm"][:, :])
            v.tensor_copy(out=sb["vwarm"][:, :], in_=vwb[:, :])
            v.tensor_copy(out=vwb2[:, :], in_=vwb[:, :])
            v.tensor_add(out=sb["vwarm"][:, :], in0=sb["vwarm"][:, :], in1=sb["vwarm"][:, 0:512])
            v.tensor_mul(out=vwb2[:, :], in0=vwb[:, :], in1=vwb2[:, :])
            v.tensor_sub(out=vwb2[:, :], in0=vwb[:, :], in1=vwb2[:, :])
            v.tensor_mul(out=sb["vwarm"][:, :], in0=sb["vwarm"][:, :], in1=vwb[:, :])
            v.tensor_scalar(out=vwb2[:, :], in0=vwb[:, :], scalar1=1.0, scalar2=0.0, op0=ALU.mult, op1=ALU.add)
            v.tensor_scalar(out=vwb2[:, :], in0=vwb[:, :], scalar1=sb["vwarm"][:, 0:1], scalar2=sb["vwarm"][:, 1:2], op0=ALU.mult, op1=ALU.add)
            v.tensor_scalar(out=vwb2[:, :], in0=vwb[:, :], scalar1=sb["vwarm"][:, 0:1], scalar2=sb["vwarm"][:, 1:2], op0=ALU.subtract, op1=ALU.mult)
            v.tensor_scalar(out=sb["vwarm"][:, :], in0=sb["vwarm"][:, :], scalar1=-1.0, scalar2=sb["vwarm"][:, 1:2], op0=ALU.mult, op1=ALU.add)
            v.tensor_scalar_mul(out=sb["vwarm"][:, :], in0=sb["vwarm"][:, :], scalar1=0.5)
            v.tensor_scalar_mul(out=vwb2[:, :], in0=vwb[:, :], scalar1=sb["vwarm"][:, 0:1])
            v.tensor_scalar_max(out=sb["vwarm"][:, :], in0=sb["vwarm"][:, :], scalar1=1e-12)
            v.scalar_tensor_tensor(out=vwb2[:, :], in0=vwb[:, :], scalar=sb["vwarm"][:, 0:1], in1=vwb2[:, :], op0=ALU.mult, op1=ALU.add)
            v.scalar_tensor_tensor(out=sb["vwarm"][:, :], in0=sb["vwarm"][:, :], scalar=-1.0, in1=sb["vwarm"][:, 0:512], op0=ALU.mult, op1=ALU.mult)
            v.reduce_sum(out=sb["vwarm"][:, 0:1], in_=sb["vwarm"][:, :], axis=AX.X)
            v.reduce_max(out=sb["vwarm"][:, 1:2], in_=sb["vwarm"][:, :], axis=AX.X)
            v.reciprocal(out=sb["vwarm"][:, 2:3], in_=sb["vwarm"][:, 0:1])
            with nc.allow_low_precision(reason="warm"):
                v.reciprocal(out=vwb2[:, 0:1], in_=sb["vwarm"][:, 0:1])
        v.nop().then_inc(wsem, 1)
        v.wait_ge(wsem, 2)
        v.tensor_copy(out=sb["vwarm"][:, 0:128], in_=warm_ps[:, 0:128])
        v.tensor_copy(out=vwb[:, 0:128], in_=warm_ps[:, 0:128])
        v.tensor_add(out=sb["vwarm"][:, 0:128], in0=warm_ps[:, 0:128], in1=sb["vwarm"][:, 0:128])
        v.tensor_mul(out=vwb2[:, 0:128], in0=warm_ps[:, 0:128], in1=vwb[:, 0:128])
        v.tensor_mul(out=vwb2[:, 0:128], in0=vwb[:, 0:128], in1=warm_ps[:, 0:128])
        v.tensor_scalar(out=vwb2[:, 0:128], in0=warm_ps[:, 0:128], scalar1=1.0, scalar2=0.0, op0=ALU.mult, op1=ALU.add)
        v.scalar_tensor_tensor(out=vwb2[:, 0:128], in0=warm_ps[:, 0:128], scalar=sb["vwarm"][:, 0:1], in1=vwb2[:, 0:128], op0=ALU.mult, op1=ALU.add)
        v.tensor_copy(out=vwb2[:, 0:128], in_=warm_psb[:, :])
        v.wait_ge(wsem, 3)
        v.reduce_sum(out=sb["vwarm"][:, 5:6], in_=vwb2[:, 0:256], axis=AX.X)
        v.nop().then_inc(wsem, 1)
        v.wait_ge(wsem, 5)
        v.reduce_sum(out=sb["vwarm"][:, 5:6], in_=vwb2[:, 0:256], axis=AX.X)
        ctx.s("pb").inc(v.nop(), 1)

    @blk.gpsimd
    def _(gpsimd):
        gpsimd.wait_ge(wsem, 1)
        for _ in range(4):
            gpsimd.tensor_mul(out=vwb2[:, 256:384], in0=vwb[:, 256:384], in1=vwb2[:, 256:384])
            gpsimd.tensor_copy(out=vwb2[:, 384:512], in_=vwb[:, 384:512])
        ctx.s("pb").inc(gpsimd.nop(), 1)

    @blk.scalar
    def _(scalar):
        import os as _os
        scalar.wait_ge(wsem, 2)
        scalar.copy(out=sb["vwarm"][:, 3:4], in_=sb["vwarm"][:, 0:1])
        scalar.copy(out=vwb[:, 3:4], in_=warm_ps[:, 0:1])
        scalar.activation(out=sb["vwarm"][:, 4:5], in_=warm_ps[:, 0:1], func=AF.Silu, scale=sb["vwarm"][:, 0:1])
        scalar.activation(out=sb["vwarm"][:, 4:5], in_=warm_ps[:, 0:1], func=AF.Exp, scale=0.125, bias=sb["fconst"][:, 1:2])
        for rep in range(4):
            for f in (AF.Sigmoid, AF.Silu, AF.Exp, AF.Ln, AF.Gelu, AF.Sqrt, AF.Square):
                scalar.activation(out=sb["acwarm"][:, 0:1], in_=sb["acwarm"][:, 1:2], func=f)
        for _ in range(int(_os.environ.get("KB_ACT_STALL_NOPS", "0"))):
            scalar.nop(cycle_cnt=60000)
        for f in (AF.Sigmoid, AF.Silu, AF.Exp, AF.Ln, AF.Gelu, AF.Sqrt):
            scalar.activation(out=sb["acwarm"][:, 0:1], in_=sb["acwarm"][:, 1:2], func=f)
        for _ in range(int(_os.environ.get("KB_ACT_STALL2", "0"))):
            scalar.nop(cycle_cnt=60000)
        scalar.activation(out=vwb2[:, 0:256], in_=sb["vwarm"][:, 0:256], func=AF.Square)
        scalar.nop().then_inc(wsem, 1)
        scalar.wait_ge(wsem, 4)
        scalar.activation(out=vwb2[:, 0:256], in_=sb["vwarm"][:, 0:256], func=AF.Square)
        scalar.nop().then_inc(wsem, 1)
        scalar.wait_ge(dml.h, dml.n + 16 * (len(names) + 1 + (1 if "brow" in d else 0)))
        ctx.s("pb").inc(scalar.nop(), 1)

    @blk.sync
    def _(sync):
        import os
        for _ in range(int(os.environ.get("KB_WARMUP_NOPS", "0"))):
            sync.nop(cycle_cnt=60000)
        for name in names:
            dml.inc(sync.dma_start(out=sb[name][:, :], in_=d[name]), 16)
        dml.inc(sync.dma_start(out=sb["vecs"][:, :], in_=d["vecs"]), 16)
        if "brow" in d:
            dml.inc(sync.dma_start(out=sb["brow_sb"][:, :], in_=d["brow"]), 16)
        sync.wait_ge(dml.h, dml.n)
        ctx.s("pb").inc(sync.nop(), 1)
    ctx.pb_target += 5
    _warm_es.close()

def finale(ctx):
    blk = ctx.blk
    dms, dmg, dml = ctx.s("dms"), ctx.s("dmg"), ctx.s("dml")
    pb = ctx.s("pb")
    tgt = ctx.pb_target

    @blk.sync
    def _(sync):
        sync.wait_ge(pb.h, tgt)
        sync.wait_ge(dms.h, dms.n)
        sync.wait_ge(dmg.h, dmg.n)
        sync.wait_ge(dml.h, dml.n)

def _barrier(ctx, engines):
    ctx.pb_target += len(engines)

def _entry(eng, ctx):
    eng.wait_ge(ctx.s("pb").h, ctx.pb_target)


def phase_A(ctx):
    import os
    dbg = bool(int(os.environ.get("KB_DBG_A", "0")))
    nc, blk, d = ctx.nc, ctx.blk, ctx.d
    if dbg:
        d["dbg_xin0"] = nc.dram_tensor("dbg_xin0", [128, C], F32, kind="ExternalOutput").ap()
        d["dbg_st0"] = nc.dram_tensor("dbg_st0", [128, 8], F32, kind="ExternalOutput").ap()
        d["dbg_fc"] = nc.dram_tensor("dbg_fc", [128, 8], F32, kind="ExternalOutput").ap()
        d["dbg_xsq0"] = nc.dram_tensor("dbg_xsq0", [128, C], F32, kind="ExternalOutput").ap()
        d["dbg_xnbf0"] = nc.dram_tensor("dbg_xnbf0", [128, C], BF16, kind="ExternalOutput").ap()
    dml, dms, mm, ve, ac, ev = (ctx.s(n) for n in ("dml", "dms", "mm", "ve", "ac", "ev"))
    b_dml, b_dms, b_mm, b_ve, b_ac, b_ev = dml.n, dms.n, mm.n, ve.n, ac.n, ev.n
    pbt = ctx.pb_target
    pb = ctx.s("pb")
    ident = ctx.sb["ident"]
    redo = bool(int(os.environ.get("KB_A_REDO", "1")))
    tiles = list(range(NT)) + ([0] if redo else [])
    NPOS = len(tiles)
    es = ExitStack()
    xin = [es.enter_context(nc.sbuf_tensor(f"pa_xin{j}", [128, C], F32)) for j in range(2)]
    xsq = [es.enter_context(nc.sbuf_tensor(f"pa_xsq{j}", [128, C], F32)) for j in range(2)]
    st = es.enter_context(nc.sbuf_tensor("pa_st", [128, 8], F32))
    S1, S2, MEAN, EX2, M2, VAR, SD, RSTD = range(8)
    nmr = es.enter_context(nc.sbuf_tensor("pa_nmr", [128, 1], F32))
    touch = es.enter_context(nc.sbuf_tensor("pa_touch", [128, 1], F32))
    vtouch = es.enter_context(nc.sbuf_tensor("pa_vtouch", [128, 1], BF16))
    xnbf = [es.enter_context(nc.sbuf_tensor(f"pa_xnbf{j}", [128, C], BF16)) for j in range(2)]
    stage = [es.enter_context(nc.sbuf_tensor(f"pa_stage{j}", [128, C], BF16)) for j in range(2)]
    pst = [es.enter_context(nc.psum_tensor(f"pa_pst{j}", [128, 128], BF16)) for j in range(2)]
    with es:
        @blk.sync
        def _(sync):
            _entry(sync, ctx)
            for i in range(NPOS):
                ti = tiles[i]
                if i >= 2:
                    sync.wait_ge(ve.h, b_ve + 2 * (i - 2) + 2)
                    sync.wait_ge(ac.h, b_ac + 2 * (i - 2) + 1)
                dml.inc(sync.dma_start(out=xin[i % 2][:, :],
                                       in_=d["x"][ti * 128:(ti + 1) * 128, :]), 16)
                if dbg and i == 0:
                    sync.wait_ge(dml.h, b_dml + 16)
                    dms.inc(sync.dma_start(out=d["dbg_xin0"], in_=xin[0][:, :]), 16)
                    dms.inc(sync.dma_start(out=d["dbg_fc"], in_=ctx.sb["fconst"][:, :]), 16)
                    sync.wait_ge(ve.h, b_ve + 2)
                    dms.inc(sync.dma_start(out=d["dbg_st0"], in_=st[:, :]), 16)
                    dms.inc(sync.dma_start(out=d["dbg_xsq0"], in_=xsq[0][:, :]), 16)
                    dms.inc(sync.dma_start(out=d["dbg_xnbf0"], in_=xnbf[0][:, :]), 16)
                if i >= 1:
                    j = i - 1
                    tj = tiles[j]
                    sync.wait_ge(ev.h, b_ev + 8 * (j + 1))
                    dms.inc(sync.dma_start(
                        out=d["xn_fm"].rearrange("(k p) t -> p k t", p=128)[:, :, tj * 128:(tj + 1) * 128],
                        in_=stage[j % 2].rearrange("p (k n) -> p k n", n=128)), 16)
            j = NPOS - 1
            tj = tiles[j]
            sync.wait_ge(ev.h, b_ev + 8 * (j + 1))
            dms.inc(sync.dma_start(
                out=d["xn_fm"].rearrange("(k p) t -> p k t", p=128)[:, :, tj * 128:(tj + 1) * 128],
                in_=stage[j % 2].rearrange("p (k n) -> p k n", n=128)), 16)
            sync.wait_ge(dms.h, b_dms + 16 * NPOS)
            pb.inc(sync.nop(), 1)

        @blk.vector
        def _(vector):
            _entry(vector, ctx)
            for i in range(NPOS):
                p = i % 2
                if dbg and i == 1:
                    vector.wait_ge(dms.h, b_dms + 16 * 5)
                vector.wait_ge(dml.h, b_dml + 16 * (i + 1))
                vector.reduce_sum(out=st[:, S1:S1 + 1], in_=xin[p][:, :], axis=AX.X)
                vector.wait_ge(ac.h, b_ac + 2 * i + 1)
                vector.reduce_sum(out=st[:, S2:S2 + 1], in_=xsq[p][:, :], axis=AX.X)
                vector.tensor_scalar_mul(out=st[:, MEAN:MEAN + 1], in0=st[:, S1:S1 + 1], scalar1=1.0 / C)
                vector.tensor_scalar_mul(out=st[:, EX2:EX2 + 1], in0=st[:, S2:S2 + 1], scalar1=1.0 / C)
                vector.tensor_mul(out=st[:, M2:M2 + 1], in0=st[:, MEAN:MEAN + 1], in1=st[:, MEAN:MEAN + 1])
                ve.inc(vector.tensor_sub(out=st[:, VAR:VAR + 1], in0=st[:, EX2:EX2 + 1], in1=st[:, M2:M2 + 1]), 1)
                vector.wait_ge(ac.h, b_ac + 2 * i + 2)
                vector.reciprocal(out=st[:, RSTD:RSTD + 1], in_=st[:, SD:SD + 1])
                if i >= 2:
                    vector.wait_ge(mm.h, b_mm + 8 * (i - 2) + 8)
                vector.tensor_scalar(out=xnbf[p][:, :], in0=xin[p][:, :],
                                     scalar1=st[:, MEAN:MEAN + 1], scalar2=st[:, RSTD:RSTD + 1],
                                     op0=ALU.subtract, op1=ALU.mult)
                ve.inc(vector.tensor_copy(out=vtouch[:, :], in_=xnbf[p][:, 0:1]), 1)
                if i >= 1:
                    jj = i - 1
                    if jj >= 2:
                        vector.wait_ge(dms.h, b_dms + 16 * (jj - 1))
                    for k in range(KC):
                        vector.wait_ge(mm.h, b_mm + 8 * jj + k + 1)
                        ev.inc(vector.tensor_copy(out=stage[jj % 2][:, k * 128:(k + 1) * 128],
                                                  in_=pst[(8 * jj + k) % 2][:, :]), 1)
            jj = NPOS - 1
            vector.wait_ge(dms.h, b_dms + 16 * (jj - 1))
            for k in range(KC):
                vector.wait_ge(mm.h, b_mm + 8 * jj + k + 1)
                ev.inc(vector.tensor_copy(out=stage[jj % 2][:, k * 128:(k + 1) * 128],
                                          in_=pst[(8 * jj + k) % 2][:, :]), 1)
            pb.inc(vector.nop(), 1)

        @blk.scalar
        def _(scalar):
            _entry(scalar, ctx)
            for i in range(NPOS):
                p = i % 2
                scalar.wait_ge(dml.h, b_dml + 16 * (i + 1))
                if i >= 2:
                    scalar.wait_ge(ve.h, b_ve + 2 * (i - 2) + 1)
                scalar.activation(out=xsq[p][:, :], in_=xin[p][:, :], func=AF.Square)
                ac.inc(scalar.activation(out=touch[:, :], in_=xsq[p][:, 0:1], func=AF.Square), 1)
                scalar.wait_ge(ve.h, b_ve + 2 * i + 1)
                scalar.activation(out=st[:, SD:SD + 1], in_=st[:, VAR:VAR + 1],
                                  func=AF.Sqrt, bias=ctx.sb["fconst"][:, 0:1])
                ac.inc(scalar.activation(out=touch[:, :], in_=st[:, SD:SD + 1], func=AF.Square), 1)
            pb.inc(scalar.nop(), 1)

        @blk.tensor
        def _(tensor):
            _entry(tensor, ctx)
            for i in range(NPOS):
                p = i % 2
                tensor.wait_ge(ve.h, b_ve + 2 * i + 2)
                for k in range(KC):
                    t = 8 * i + k
                    if t >= 2:
                        tensor.wait_ge(ev.h, b_ev + t - 1)
                    mm.inc(tensor.transpose(out=pst[t % 2][:, :],
                                            in_=xnbf[p][:, k * 128:(k + 1) * 128],
                                            identity=ident[:, :]), 1)
            pb.inc(tensor.nop(), 1)

    _barrier(ctx, ["sync", "vector", "scalar", "tensor"])
    assert dml.n == b_dml + 16 * NPOS
    assert ac.n == b_ac + 2 * NPOS
    assert ve.n == b_ve + 2 * NPOS
    assert mm.n == b_mm + 8 * NPOS
    assert ev.n == b_ev + 8 * NPOS



BROW = {"b_r": 0, "b_k": 1, "b_v": 2, "b_g": 3, "b_o": 4, "bq": 5, "bk1": 6, "bv1": 7, "bp": 8}

def phase_B(ctx):
    nc, blk, d, sb = ctx.nc, ctx.blk, ctx.d, ctx.sb
    dml, dmg, dms, mm, ve, ac, ev = (ctx.s(n) for n in ("dml", "dmg", "dms", "mm", "ve", "ac", "ev"))
    b = {s: ctx.s(s).n for s in ("dml", "dmg", "dms", "mm", "ve", "ac", "ev")}
    pbt = ctx.pb_target
    pb = ctx.s("pb")
    TT8 = 8
    MM_T = 4 * 8 * 1 + 16 + 16
    VE_T = 1 + 16 + 16
    AC_T = 32 + 16
    DMS_T = 4 * 16
    units = [(p, m) for p in ("r", "k", "v", "g") for m in range(8)]
    WNAME = {"r": "wrT", "k": "wkT", "v": "wvT", "g": "wgT"}
    BIDX = {"r": BROW["b_r"], "k": BROW["b_k"], "v": BROW["b_v"], "g": BROW["b_g"]}
    kv_units = [j for j, (p, _) in enumerate(units) if p in ("k", "v")]
    kvpos = {j: idx for idx, j in enumerate(kv_units)}

    es = ExitStack()
    wsb = {p: es.enter_context(nc.sbuf_tensor(f"pb_w{p}", [128, 8 * 1024], BF16))
           for p in ("r", "k", "v", "g")}
    xnt = [es.enter_context(nc.sbuf_tensor(f"pb_xnt{j}", [128, 8 * 513], BF16)) for j in range(2)]
    dt1 = es.enter_context(nc.sbuf_tensor("pb_dt1", [128, 512], BF16))
    xp = {p: es.enter_context(nc.sbuf_tensor(f"pb_xp{p}", [128, 8 * 512], BF16))
          for p in ("r", "k", "v", "g")}
    obuf = {p: es.enter_context(nc.sbuf_tensor(f"pb_o{p}", [128, 8 * 512], BF16))
            for p in ("r", "k", "v", "g")}
    raw = {p: es.enter_context(nc.sbuf_tensor(f"pb_raw{p}", [128, 8 * 512], BF16)) for p in ("k", "v")}
    sqb = {p: es.enter_context(nc.sbuf_tensor(f"pb_sq{p}", [128, 8 * 512], BF16)) for p in ("k", "v")}
    sqn = es.enter_context(nc.sbuf_tensor("pb_sqn", [2, 512], F32))
    rnb = es.enter_context(nc.sbuf_tensor("pb_rnb", [2, 512], BF16))
    psA = [es.enter_context(nc.psum_tensor(f"pb_psA{j}", [128, 512], F32)) for j in range(4)]
    psN = [es.enter_context(nc.psum_tensor(f"pb_psN{j}", [2, 512], F32)) for j in range(2)]
    psB = [es.enter_context(nc.psum_tensor(f"pb_psB{j}", [128, 512], F32)) for j in range(2)]
    SCR = {"r": "r_fm", "k": "k_fm", "v": "v_fm", "g": "g0_fm"}

    def unit_mm_done(i, j):
        return b["mm"] + MM_T * i + (j + 1)
    def sumsq_done(i, jj):
        return b["mm"] + MM_T * i + 32 + 2 * jj + 1
    def bcast_done(i, jj):
        return b["mm"] + MM_T * i + 32 + 2 * jj + 2
    def evac_done(i, j):
        return b["ac"] + AC_T * i + (j + 1)
    def sqrt_done(i, jj):
        return b["ac"] + AC_T * i + 32 + jj + 1
    def tmix_done(i):
        return b["ve"] + VE_T * i + 1
    def recip_done(i, jj):
        return b["ve"] + VE_T * i + 1 + 2 * jj + 1
    def knorm_done(i, jj):
        return b["ve"] + VE_T * i + 1 + 2 * jj + 2

    with es:
        @blk.gpsimd
        def _(gpsimd):
            _entry(gpsimd, ctx)
            for p in ("r", "k", "v", "g"):
                dmg.inc(gpsimd.dma_start(
                    out=wsb[p].rearrange("p (k n) -> p k n", n=1024),
                    in_=d[WNAME[p]].rearrange("(k p) n -> p k n", p=128)), 16)
            gpsimd.wait_ge(dmg.h, b["dmg"] + 64)
            pb.inc(gpsimd.nop(), 1)

        @blk.sync
        def _(sync):
            _entry(sync, ctx)
            for i in range(TT8):
                if i >= 2:
                    sync.wait_ge(ve.h, tmix_done(i - 2))
                t0 = i * 512
                if i == 0:
                    dml.inc(sync.dma_start(
                        out=xnt[0].rearrange("p (k n) -> p k n", n=513)[:, :, 1:513],
                        in_=d["xn_fm"].rearrange("(k p) t -> p k t", p=128)[:, :, 0:512]), 16)
                else:
                    dml.inc(sync.dma_start(
                        out=xnt[i % 2].rearrange("p (k n) -> p k n", n=513),
                        in_=d["xn_fm"].rearrange("(k p) t -> p k t", p=128)[:, :, t0 - 1:t0 + 512]), 16)
                for p in ("r", "g"):
                    j = units.index((p, 7))
                    sync.wait_ge(ac.h, evac_done(i, j))
                    dms.inc(sync.dma_start(
                        out=d[SCR[p]].rearrange("(k p) t -> p k t", p=128)[:, :, t0:t0 + 512],
                        in_=obuf[p].rearrange("p (k n) -> p k n", n=512)), 16)
                for p in ("k", "v"):
                    jj = kvpos[units.index((p, 7))]
                    sync.wait_ge(ve.h, knorm_done(i, jj))
                    dms.inc(sync.dma_start(
                        out=d[SCR[p]].rearrange("(k p) t -> p k t", p=128)[:, :, t0:t0 + 512],
                        in_=obuf[p].rearrange("p (k n) -> p k n", n=512)), 16)
            sync.wait_ge(dms.h, b["dms"] + DMS_T * TT8)
            pb.inc(sync.nop(), 1)

        @blk.vector
        def _(vector):
            _entry(vector, ctx)
            for i in range(TT8):
                vector.wait_ge(dml.h, b["dml"] + 16 * (i + 1))
                if i >= 1:
                    vector.wait_ge(mm.h, b["mm"] + MM_T * (i - 1) + 32)
                xv = xnt[i % 2]
                if i == 0:
                    for k in range(KC):
                        vector.memset(xv[:, k * 513:k * 513 + 1], 0.0)
                last = None
                for k in range(KC):
                    vector.tensor_sub(out=dt1[:, :],
                                      in0=xv[:, k * 513 + 1:k * 513 + 513],
                                      in1=xv[:, k * 513:k * 513 + 512])
                    for p, tm in (("r", "tmr"), ("k", "tmk"), ("v", "tmv"), ("g", "tmg")):
                        last = vector.scalar_tensor_tensor(
                            out=xp[p][:, k * 512:(k + 1) * 512],
                            in0=dt1[:, :],
                            scalar=sb["vecs"][:, vcol(tm, k):vcol(tm, k) + 1],
                            in1=xv[:, k * 513:k * 513 + 512],
                            op0=ALU.mult, op1=ALU.add)
                ve.inc(last, 1)
                for jj in range(16):
                    j = kv_units[jj]
                    p, m = units[j]
                    if i >= 1 and m == 0:
                        vector.wait_ge(dms.h, b["dms"] + DMS_T * (i - 1) + 16 * (3 if p == "k" else 4))
                    vector.wait_ge(ac.h, sqrt_done(i, jj))
                    vector.tensor_scalar_max(out=sqn[:, :], in0=sqn[:, :], scalar1=1e-12)
                    with nc.allow_low_precision(reason="l2norm recip bf16 operand"):
                        ve.inc(vector.reciprocal(out=rnb[:, :], in_=sqn[:, :]), 1)
                    vector.wait_ge(mm.h, bcast_done(i, jj))
                    ve.inc(vector.tensor_tensor(
                        out=obuf[p][:, m * 512:(m + 1) * 512],
                        in0=raw[p][:, m * 512:(m + 1) * 512],
                        in1=psB[jj % 2][:, :], op=ALU.mult), 1)
            pb.inc(vector.nop(), 1)

        @blk.scalar
        def _(scalar):
            _entry(scalar, ctx)
            for i in range(TT8):
                for j, (p, m) in enumerate(units):
                    scalar.wait_ge(mm.h, unit_mm_done(i, j))
                    ps = psA[j % 4]
                    sl = slice(m * 512, (m + 1) * 512)
                    if i >= 1 and p in ("r", "g"):
                        scalar.wait_ge(dms.h, b["dms"] + DMS_T * (i - 1) + 16 * (2 if p == "g" else 1))
                    if p == "r":
                        ac.inc(scalar.activation(out=obuf["r"][:, sl], in_=ps[:, :], func=AF.Sigmoid), 1)
                    elif p == "g":
                        ac.inc(scalar.activation(out=obuf["g"][:, sl], in_=ps[:, :], func=AF.Silu), 1)
                    else:
                        scalar.activation(out=raw[p][:, sl], in_=ps[:, :], func=AF.Copy)
                        ac.inc(scalar.activation(out=sqb[p][:, sl], in_=ps[:, :], func=AF.Square), 1)
                for jj in range(16):
                    scalar.wait_ge(mm.h, sumsq_done(i, jj))
                    if jj >= 1:
                        scalar.wait_ge(ve.h, recip_done(i, jj - 1))
                    elif i >= 1:
                        scalar.wait_ge(ve.h, recip_done(i - 1, 15))
                    ac.inc(scalar.activation(out=sqn[:, :], in_=psN[jj % 2][:, :], func=AF.Sqrt), 1)
            pb.inc(scalar.nop(), 1)

        @blk.tensor
        def _(tensor):
            _entry(tensor, ctx)
            tensor.wait_ge(dmg.h, b["dmg"] + 64)
            for i in range(TT8):
                tensor.wait_ge(ve.h, tmix_done(i))
                for j, (p, m) in enumerate(units):
                    ps = psA[j % 4]
                    if i > 0 or j >= 4:
                        pi, pj = (i, j - 4) if j >= 4 else (i - 1, j + 28)
                        tensor.wait_ge(ac.h, evac_done(pi, pj))
                    for k in range(KC):
                        tensor.matmul(out=ps[:, :],
                                      lhsT=wsb[p][:, k * 1024 + m * 128:k * 1024 + (m + 1) * 128],
                                      rhs=xp[p][:, k * 512:(k + 1) * 512],
                                      start=(k == 0), stop=False)
                    mm.inc(tensor.matmul(out=ps[:, :],
                                         lhsT=sb["brow_sb"][0:1, BIDX[p] * 1024 + m * 128:BIDX[p] * 1024 + (m + 1) * 128],
                                         rhs=sb["onesrow"][0:1, :],
                                         start=False, stop=True), 1)
                for jj in range(16):
                    j = kv_units[jj]
                    p, m = units[j]
                    tensor.wait_ge(ac.h, evac_done(i, j))
                    if jj >= 2:
                        tensor.wait_ge(ac.h, sqrt_done(i, jj - 2))
                    mm.inc(tensor.matmul(out=psN[jj % 2][:, :], lhsT=sb["hselcol"][:, :],
                                         rhs=sqb[p][:, m * 512:(m + 1) * 512],
                                         start=True, stop=True), 1)
                    tensor.wait_ge(ve.h, recip_done(i, jj))
                    if jj >= 2:
                        tensor.wait_ge(ve.h, knorm_done(i, jj - 2))
                    mm.inc(tensor.matmul(out=psB[jj % 2][:, :], lhsT=sb["bc2"][:, :],
                                         rhs=rnb[:, :], start=True, stop=True), 1)
            pb.inc(tensor.nop(), 1)

    _barrier(ctx, ["sync", "vector", "scalar", "tensor", "gpsimd"])
    assert mm.n == b["mm"] + MM_T * TT8, (mm.n, b["mm"] + MM_T * TT8)
    assert ve.n == b["ve"] + VE_T * TT8
    assert ac.n == b["ac"] + AC_T * TT8
    assert dms.n == b["dms"] + DMS_T * TT8


ENG_SEM = {"tensor": "mm", "vector": "ve", "scalar": "ac", "sync": "dml", "gpsimd": "dmg"}
DMA_ENGS = ("sync", "gpsimd")

class Step:
    __slots__ = ("eng", "seq", "fn", "deps")
    def __init__(self, eng, seq, fn, deps):
        self.eng, self.seq, self.fn, self.deps = eng, seq, fn, deps


def _phase_par(name):
    import os as _o
    return name in _o.environ.get("KB_PAR_PHASES", "CDEFG")

class Script:
    def __init__(self, ctx, serial=None, serial_first=0):
        import os as _os
        self.ctx = ctx
        self.steps = {e: [] for e in ENG_SEM}
        self.all = []
        if serial is None:
            serial = not bool(int(_os.environ.get("KB_PARALLEL", "0")))
        self.serial = serial
        self.serial_first = serial_first

    def add(self, eng, fn, deps=()):
        deps = list(deps)
        if self.all and (self.serial is True or len(self.all) < self.serial_first
                         or (isinstance(self.serial, str) and eng in self.serial.split(","))):
            deps.append(self.all[-1])
        s = Step(eng, len(self.steps[eng]), fn, [d for d in deps if d is not None and d.eng != eng])
        self.steps[eng].append(s)
        self.all.append(s)
        return s

    def emit(self, pb_engines=("sync", "gpsimd", "vector", "scalar", "tensor")):
        ctx = self.ctx
        blk = ctx.blk
        base = {e: ctx.s(ENG_SEM[e]).n for e in ENG_SEM}
        stepv = {e: (16 if e in DMA_ENGS else 1) for e in ENG_SEM}
        needed = {e: set() for e in ENG_SEM}
        for e in DMA_ENGS:
            needed[e] = set(range(len(self.steps[e])))
        for e in ENG_SEM:
            for s in self.steps[e]:
                for dp in s.deps:
                    needed[dp.eng].add(dp.seq)
        incval = {}
        for e in ENG_SEM:
            incval[e] = {}
            for i, seq in enumerate(sorted(needed[e])):
                incval[e][seq] = (i + 1) * stepv[e]
        pbt = ctx.pb_target
        for eng in ENG_SEM:
            steps = self.steps[eng]
            if not steps and eng not in pb_engines:
                continue
            def body(engine, steps=steps, eng=eng):
                _entry(engine, ctx)
                waited = {}
                for s in steps:
                    fresh_mm_wait = False
                    for dp in s.deps:
                        tgt = base[dp.eng] + incval[dp.eng][dp.seq]
                        if waited.get(dp.eng, -1) < tgt:
                            engine.wait_ge(ctx.s(ENG_SEM[dp.eng]).h, tgt)
                            waited[dp.eng] = tgt
                            if dp.eng == "tensor":
                                fresh_mm_wait = True
                    if fresh_mm_wait and eng == "vector":
                        engine.tensor_copy(out=ctx.sb["vwarm"][:, 8:16], in_=ctx.sb["vwarm"][:, 0:8])
                    elif fresh_mm_wait and eng == "scalar":
                        engine.copy(out=ctx.sb["acwarm"][:, 2:3], in_=ctx.sb["acwarm"][:, 1:2])
                    ins = s.fn(engine)
                    if s.seq in incval[eng]:
                        ctx.s(ENG_SEM[eng]).inc(ins, stepv[eng])
                if eng in pb_engines:
                    if eng in DMA_ENGS and steps:
                        engine.wait_ge(ctx.s(ENG_SEM[eng]).h, base[eng] + len(steps) * 16)
                    ctx.s("pb").inc(engine.nop(), 1)
            getattr(blk, eng)(body)
        ctx.pb_target += len(pb_engines)

class BD:
    def __init__(self, S):
        self.S = S
        self.w = {}
        self.r = {}

    def step(self, eng, fn, reads=(), writes=()):
        deps = []
        for bname in reads:
            if bname in self.w:
                deps.append(self.w[bname])
        for bname in writes:
            deps.extend(self.r.get(bname, ()))
            if bname in self.w:
                deps.append(self.w[bname])
        s = self.S.add(eng, fn, deps)
        for bname in reads:
            self.r.setdefault(bname, []).append(s)
        for bname in writes:
            self.w[bname] = s
            self.r[bname] = []
        return s


def phase_C(ctx):
    nc, blk, d, sb = ctx.nc, ctx.blk, ctx.d, ctx.sb
    import os as _oc
    _serc = _oc.environ.get("KB_SER_C")
    S = Script(ctx, serial=(_serc if _serc is not None else not _phase_par("C")),
               serial_first=int(_oc.environ.get("KB_SERF_C", "230")))
    B = BD(S)
    es = ExitStack()
    sbt = lambda name, shape, dt: es.enter_context(nc.sbuf_tensor(name, shape, dt))
    pst = lambda name, shape, dt: es.enter_context(nc.psum_tensor(name, shape, dt))
    kh = [sbt(f"pc_kh{j}", [64, T], BF16) for j in range(2)]
    vh = [sbt(f"pc_vh{j}", [64, T], BF16) for j in range(2)]
    rh = [sbt(f"pc_rh{j}", [64, T], BF16) for j in range(2)]
    wkvh = [sbt(f"pc_wkvh{j}", [64, T], BF16) for j in range(2)]
    declow = [sbt(f"pc_dlo{h}", [128, 128], BF16) for h in range(H)]
    decup = [sbt(f"pc_dup{h}", [128, 128], BF16) for h in range(H)]
    psiup = [sbt(f"pc_psi{h}", [128, 128], BF16) for h in range(H)]
    dgrow = [sbt(f"pc_dgr{h}", [64, 128], BF16) for h in range(H)]
    dpow0 = [sbt(f"pc_dp0{h}", [128, 1], F32) for h in range(H)]
    dpowr = [sbt(f"pc_dpr{h}", [128, 1], F32) for h in range(H)]
    decLc = [sbt(f"pc_dL{h}", [128, 1], F32) for h in range(H)]
    idecl = [sbt(f"pc_idL{h}", [64, 64], BF16) for h in range(H)]
    sc16 = sbt("pc_sc16", [16, 8], F32)
    DEC16, ET16, LND16, LNET16, LNB16 = 0, 1, 2, 3, 4
    diag = sbt("pc_diag", [16, 16], F32)
    LND = sbt("pc_LND", [128, 16], F32)
    NLND = sbt("pc_NLND", [128, 16], F32)
    LNB = sbt("pc_LNB", [128, 16], F32)
    ET = sbt("pc_ET", [128, 16], F32)
    NET = sbt("pc_NET", [128, 16], F32)
    tmpf = sbt("pc_tmpf", [128, 128], F32)
    c_iota_pf = sbt("pc_iotapf", [128, 128], F32)
    c_iota_f = sbt("pc_iotaf", [128, 128], F32)
    c_mlow = sbt("pc_mlow", [128, 128], F32)
    c_mups = sbt("pc_mups", [128, 128], F32)
    c_mupi = sbt("pc_mupi", [128, 128], F32)
    c_idf64 = sbt("pc_idf64", [64, 64], F32)
    Cb = [sbt(f"pc_C{u}", [128, 128], BF16) for u in range(2)]
    Ctb = [sbt(f"pc_Ct{u}", [128, 128], BF16) for u in range(2)]
    PSIb = [sbt(f"pc_PSIb{u}", [128, 128], BF16) for u in range(2)]
    Mt = [[sbt(f"pc_M{u}_{j}", [128, 128], BF16) for j in range(2)] for u in range(2)]
    Pb = [[sbt(f"pc_P{u}_{j}", [128, 128], BF16) for j in range(2)] for u in range(2)]
    Qb = [[sbt(f"pc_Q{u}_{j}", [128, 128], BF16) for j in range(2)] for u in range(2)]
    Ktm = [sbt(f"pc_Ktm{u}", [128, 64], BF16) for u in range(2)]
    Vtm = [sbt(f"pc_Vtm{u}", [128, 64], BF16) for u in range(2)]
    Kd = [sbt(f"pc_Kd{u}", [128, 64], BF16) for u in range(2)]
    FT = [sbt(f"pc_FT{u}", [64, 64], BF16) for u in range(2)]
    Gs = [sbt(f"pc_G{u}", [64, 64], F32) for u in range(2)]
    WVT = [sbt(f"pc_WVT{u}", [64, 128], F32) for u in range(2)]
    RT = [sbt(f"pc_RT{u}", [64, 128], BF16) for u in range(2)]
    Rg = [sbt(f"pc_Rg{u}", [64, 128], BF16) for u in range(2)]
    wkvf = [sbt(f"pc_wkvf{u}", [64, 128], F32) for u in range(2)]
    wsq = [sbt(f"pc_wsq{u}", [64, 128], BF16) for u in range(2)]
    Hs = sbt("pc_H", [64, 64], F32)
    Hb = sbt("pc_Hb", [64, 64], BF16)
    stbf = sbt("pc_stbf", [128, 2], BF16)
    ztile = sbt("pc_ztile", [64, 128], BF16)
    statsum = sbt("pc_statsum", [1, 16], F32)
    statsq = sbt("pc_statsq", [1, 16], F32)
    PG = pst("pc_PG", [128, 128], F32)
    PR = pst("pc_PR", [128, 128], F32)
    PT = pst("pc_PT", [128, 64], BF16)
    PAu = [pst(f"pc_PA{u}", [128, 128], F32) for u in range(2)]
    PA = PAu[0]
    P4 = pst("pc_P4", [64, 128], F32)
    P5 = pst("pc_P5", [64, 128], F32)
    STB = pst("pc_STB", [128, 2], F32)
    ST1 = STB[:, 0:1]
    ST2 = STB[:, 1:2]
    ident = sb["ident"]
    dcol = vcol("decay")
    ecol = vcol("eta")

    with es:
        for nmc, tile in (("iota_pf", c_iota_pf), ("iota_f", c_iota_f), ("m_lowstrict", c_mlow),
                          ("m_upstrict", c_mups), ("m_upincl", c_mupi), ("identf64", c_idf64)):
            B.step("sync", lambda e, nmc=nmc, tile=tile: e.dma_start(out=tile[:, :], in_=d[nmc]), writes=(nmc + "L",))
        B.step("scalar", lambda e: e.activation(out=sc16[:, DEC16:DEC16+1], in_=sb["vecs"][0:16, dcol:dcol+1], func=AF.Sigmoid), writes=("sc16",))
        B.step("scalar", lambda e: e.activation(out=sc16[:, ET16:ET16+1], in_=sb["vecs"][0:16, ecol:ecol+1], func=AF.Sigmoid), writes=("sc16",), reads=("sc16",))
        B.step("scalar", lambda e: e.activation(out=sc16[:, LND16:LND16+1], in_=sc16[:, DEC16:DEC16+1], func=AF.Ln), reads=("sc16",), writes=("sc16",))
        B.step("scalar", lambda e: e.activation(out=sc16[:, LNET16:LNET16+1], in_=sc16[:, ET16:ET16+1], func=AF.Ln), reads=("sc16",), writes=("sc16",))
        B.step("vector", lambda e: e.tensor_sub(out=sc16[:, LNB16:LNB16+1], in0=sc16[:, LNET16:LNET16+1], in1=sc16[:, LND16:LND16+1]), reads=("sc16",), writes=("sc16",))
        for tab, col in ((LND, LND16), (LNB, LNB16), (ET, ET16)):
            B.step("vector", lambda e, col=col: e.tensor_scalar_mul(out=diag[:, :], in0=c_idf64[0:16, 0:16], scalar1=sc16[:, col:col+1]), reads=("sc16",), writes=("diag",))
            B.step("tensor", lambda e: e.matmul(out=PA[:, 0:16], lhsT=sb["ones16f"][:, :], rhs=diag[:, :], start=True, stop=True), reads=("diag",), writes=("PA",))
            B.step("vector", lambda e, tab=tab: e.tensor_copy(out=tab[:, :], in_=PA[:, 0:16]), reads=("PA",), writes=(f"tab{col}",))
        B.step("vector", lambda e: e.tensor_scalar_mul(out=NLND[:, :], in0=LND[:, :], scalar1=-1.0), reads=(f"tab{LND16}",), writes=("NLND",))
        B.step("vector", lambda e: e.tensor_scalar_mul(out=NET[:, :], in0=ET[:, :], scalar1=-1.0), reads=(f"tab{ET16}",), writes=("NET",))
        for h in range(H):
            lnd = LND[:, h:h+1]; nlnd = NLND[:, h:h+1]; lnb = LNB[:, h:h+1]
            B.step("scalar", lambda e, h=h, lnd=lnd, lnb=lnb: e.activation(out=tmpf[:, :], in_=c_iota_pf[:, :], func=AF.Exp, scale=lnd, bias=lnb), reads=(f"tab{LND16}", f"tab{LNB16}"), writes=("tmpf",))
            B.step("vector", lambda e, h=h: e.tensor_mul(out=declow[h][:, :], in0=tmpf[:, :], in1=c_mlow[:, :]), reads=("tmpf",), writes=(f"dlo{h}",))
            B.step("scalar", lambda e, h=h, nlnd=nlnd, lnb=lnb: e.activation(out=tmpf[:, :], in_=c_iota_pf[:, :], func=AF.Exp, scale=nlnd, bias=lnb), reads=("NLND", f"tab{LNB16}", f"dlo{h}"), writes=("tmpf",))
            B.step("vector", lambda e, h=h: e.tensor_mul(out=decup[h][:, :], in0=tmpf[:, :], in1=c_mups[:, :]), reads=("tmpf",), writes=(f"dup{h}",))
            B.step("scalar", lambda e, h=h, nlnd=nlnd: e.activation(out=tmpf[:, :], in_=c_iota_pf[:, :], func=AF.Exp, scale=nlnd), reads=(f"dup{h}",), writes=("tmpf",))
            B.step("vector", lambda e, h=h: e.tensor_mul(out=psiup[h][:, :], in0=tmpf[:, :], in1=c_mupi[:, :]), reads=("tmpf",), writes=(f"psi{h}",))
            B.step("scalar", lambda e, h=h, lnd=lnd: e.activation(out=dgrow[h][:, :], in_=c_iota_f[0:64, :], func=AF.Exp, scale=lnd[0:64, :], bias=lnd[0:64, :]), reads=(f"tab{LND16}",), writes=(f"dgr{h}",))
            B.step("scalar", lambda e, h=h, lnd=lnd: e.activation(out=dpow0[h][:, :], in_=c_iota_pf[:, 0:1], func=AF.Exp, scale=lnd), reads=(f"tab{LND16}",), writes=(f"dp0{h}",))
            B.step("scalar", lambda e, h=h, nlnd=nlnd: e.activation(out=dpowr[h][:, :], in_=c_iota_pf[:, 127:128], func=AF.Exp, scale=nlnd), reads=("NLND",), writes=(f"dpr{h}",))
            B.step("scalar", lambda e, h=h, lnd=lnd: e.activation(out=decLc[h][:, :], in_=sb["fconst"][:, 3:4], func=AF.Exp, scale=lnd), reads=(f"tab{LND16}",), writes=(f"dL{h}",))
            B.step("vector", lambda e, h=h: e.tensor_scalar_mul(out=idecl[h][:, :], in0=ident[0:64, 0:64], scalar1=decLc[h][0:64, :]), reads=(f"dL{h}",), writes=(f"idL{h}",))
        B.step("vector", lambda e: e.memset(ztile[:, :], 0.0), writes=("ztile",))
        for h in range(H):
            hb = h % 2
            B.step("sync", lambda e, h=h, hb=hb: e.dma_start(out=kh[hb][:, :], in_=d["k_fm"][64*h:64*h+64, :]), writes=(f"kh{hb}",))
            B.step("sync", lambda e, h=h, hb=hb: e.dma_start(out=vh[hb][:, :], in_=d["v_fm"][64*h:64*h+64, :]), writes=(f"vh{hb}",))
            B.step("sync", lambda e, h=h, hb=hb: e.dma_start(out=rh[hb][:, :], in_=d["r_fm"][64*h:64*h+64, :]), writes=(f"rh{hb}",))
            B.step("vector", lambda e: e.memset(Hs[:, :], 0.0), writes=("Hs",))
            B.step("vector", lambda e: e.memset(Hb[:, :], 0.0), writes=("Hb",))
            B.step("tensor", lambda e: e.matmul(out=STB[:, 0:2], lhsT=ztile[:, :], rhs=sb["ones"][0:64, 0:2],
                   start=True, stop=True), reads=("ztile",), writes=("ST1w", "ST2w"))
            def emit_stage1(c):
                u = c % 2
                cs = slice(128 * c, 128 * (c + 1))
                Kf = kh[hb][:, cs]; Vf = vh[hb][:, cs]; Rf = rh[hb][:, cs]
                kread, vread, rread = f"kh{hb}", f"vh{hb}", f"rh{hb}"
                B.step("tensor", lambda e, Kf=Kf: e.matmul(out=PG[:, :], lhsT=Kf, rhs=Kf, start=True, stop=True), reads=(kread,), writes=("PG",))
                B.step("vector", lambda e, u=u, h=h: e.tensor_mul(out=Cb[u][:, :], in0=PG[:, :], in1=declow[h][:, :]), reads=("PG", f"dlo{h}"), writes=(f"C{u}",))
                B.step("vector", lambda e, u=u, h=h: e.tensor_mul(out=Ctb[u][:, :], in0=PG[:, :], in1=decup[h][:, :]), reads=("PG", f"dup{h}"), writes=(f"Ct{u}",))
                B.step("tensor", lambda e, Kf=Kf, Rf=Rf: e.matmul(out=PR[:, :], lhsT=Kf, rhs=Rf, start=True, stop=True), reads=(kread, rread, f"C{u}", f"Ct{u}"), writes=("PR",))
                B.step("vector", lambda e, u=u, h=h: e.tensor_mul(out=PSIb[u][:, :], in0=PR[:, :], in1=psiup[h][:, :]), reads=("PR", f"psi{h}"), writes=(f"PSIb{u}",))
                B.step("tensor", lambda e, Kf=Kf: e.transpose(out=PT[:, :], in_=Kf, identity=ident[0:64, 0:64]), reads=(kread,), writes=("PT",))
                B.step("vector", lambda e, u=u: e.tensor_copy(out=Ktm[u][:, :], in_=PT[:, :]), reads=("PT",), writes=(f"Ktm{u}",))
                B.step("tensor", lambda e, Vf=Vf: e.transpose(out=PT[:, :], in_=Vf, identity=ident[0:64, 0:64]), reads=(vread, f"Ktm{u}"), writes=("PT",))
                B.step("vector", lambda e, u=u: e.tensor_copy(out=Vtm[u][:, :], in_=PT[:, :]), reads=("PT",), writes=(f"Vtm{u}",))
                B.step("vector", lambda e, u=u, h=h: e.tensor_scalar_mul(out=Mt[u][0][:, 0:64], in0=Ktm[u][:, :], scalar1=dpow0[h][:, :]), reads=(f"Ktm{u}", f"dp0{h}"), writes=(f"M{u}_0",))
                B.step("vector", lambda e, u=u, h=h: e.tensor_scalar_mul(out=Mt[u][0][:, 64:128], in0=Vtm[u][:, :], scalar1=ET[:, h:h+1]), reads=(f"Vtm{u}", f"tab{ET16}"), writes=(f"M{u}_0",), )
                B.step("tensor", lambda e, u=u: e.matmul(out=PAu[u][:, :], lhsT=Ctb[u][:, :], rhs=Mt[u][0][:, :], start=True, stop=True), reads=(f"Ct{u}", f"M{u}_0"), writes=(f"PA{u}",))
                B.step("vector", lambda e, u=u: e.tensor_sub(out=Mt[u][1][:, :], in0=Mt[u][0][:, :], in1=PAu[u][:, :]), reads=(f"PA{u}", f"M{u}_0"), writes=(f"M{u}_1",))
                pprev, qprev = f"C{u}", f"Ct{u}"
                pptile, qptile = Cb[u], Ctb[u]
                for lvl in range(1, 7):
                    pcur, qcur = f"P{u}_{lvl%2}", f"Q{u}_{lvl%2}"
                    pt, qt = Pb[u][lvl % 2], Qb[u][lvl % 2]
                    B.step("tensor", lambda e, u=u, qptile=qptile, pptile=pptile: e.matmul(out=PAu[u][:, :], lhsT=qptile[:, :], rhs=pptile[:, :], start=True, stop=True), reads=(pprev, qprev), writes=(f"PA{u}",))
                    B.step("vector", lambda e, u=u, pt=pt: e.tensor_copy(out=pt[:, :], in_=PAu[u][:, :]), reads=(f"PA{u}",), writes=(pcur,))
                    B.step("tensor", lambda e, u=u, qptile=qptile, pptile=pptile: e.matmul(out=PAu[u][:, :], lhsT=pptile[:, :], rhs=qptile[:, :], start=True, stop=True), reads=(pprev, qprev), writes=(f"PA{u}",))
                    B.step("vector", lambda e, u=u, qt=qt: e.tensor_copy(out=qt[:, :], in_=PAu[u][:, :]), reads=(f"PA{u}",), writes=(qcur,))
                    mprev, mcur = f"M{u}_{lvl%2}", f"M{u}_{(lvl+1)%2}"
                    mpt, mct = Mt[u][lvl % 2], Mt[u][(lvl + 1) % 2]
                    B.step("tensor", lambda e, u=u, qt=qt, mpt=mpt: e.matmul(out=PAu[u][:, :], lhsT=qt[:, :], rhs=mpt[:, :], start=True, stop=True), reads=(qcur, mprev), writes=(f"PA{u}",))
                    B.step("vector", lambda e, u=u, mct=mct, mpt=mpt: e.tensor_add(out=mct[:, :], in0=mpt[:, :], in1=PAu[u][:, :]), reads=(f"PA{u}", mprev), writes=(mcur,))
                    pprev, qprev = pcur, qcur
                    pptile, qptile = pt, qt
                MF = Mt[u][1]
                mfk = f"M{u}_1"
                B.step("vector", lambda e, u=u, h=h: e.tensor_scalar_mul(out=Kd[u][:, :], in0=Ktm[u][:, :], scalar1=dpowr[h][:, :]), reads=(f"Ktm{u}", f"dpr{h}"), writes=(f"Kd{u}",))
                B.step("tensor", lambda e, MF=MF, u=u, mfk=mfk: e.matmul(out=P4[0:64, :64], lhsT=MF[:, 0:64], rhs=Kd[u][:, :], start=True, stop=True), reads=(mfk, f"Kd{u}"), writes=("P4",))
                B.step("vector", lambda e, u=u, h=h: e.scalar_tensor_tensor(out=FT[u][:, :], in0=P4[0:64, :64], scalar=NET[0:64, h:h+1], in1=idecl[h][:, :], op0=ALU.mult, op1=ALU.add), reads=("P4", "NET", f"idL{h}"), writes=(f"FT{u}",))
                B.step("tensor", lambda e, MF=MF, u=u, mfk=mfk: e.matmul(out=P5[0:64, :64], lhsT=Kd[u][:, :], rhs=MF[:, 64:128], start=True, stop=True), reads=(mfk, f"Kd{u}"), writes=("P5",))
                B.step("vector", lambda e, u=u: e.tensor_copy(out=Gs[u][:, :], in_=P5[0:64, :64]), reads=("P5",), writes=(f"G{u}",))
                B.step("tensor", lambda e, MF=MF, u=u, mfk=mfk: e.matmul(out=P4[0:64, :], lhsT=MF[:, 64:128], rhs=PSIb[u][:, :], start=True, stop=True), reads=(mfk, f"PSIb{u}", f"FT{u}"), writes=("P4",))
                B.step("vector", lambda e, u=u: e.tensor_copy(out=WVT[u][:, :], in_=P4[0:64, :]), reads=("P4",), writes=(f"WVT{u}",))
                B.step("tensor", lambda e, MF=MF, u=u, mfk=mfk: e.matmul(out=P5[0:64, :], lhsT=MF[:, 0:64], rhs=PSIb[u][:, :], start=True, stop=True), reads=(mfk, f"PSIb{u}", f"G{u}"), writes=("P5",))
                B.step("vector", lambda e, u=u, h=h, Rf=Rf, rread=rread: e.tensor_mul(out=Rg[u][:, :], in0=Rf, in1=dgrow[h][:, :]), reads=(rread, f"dgr{h}"), writes=(f"Rg{u}",))
                B.step("vector", lambda e, u=u, h=h: e.scalar_tensor_tensor(out=RT[u][:, :], in0=P5[0:64, :], scalar=NET[0:64, h:h+1], in1=Rg[u][:, :], op0=ALU.mult, op1=ALU.add), reads=("P5", "NET", f"Rg{u}"), writes=(f"RT{u}",))

            def emit_stage2(c):
                u = c % 2
                cs = slice(128 * c, 128 * (c + 1))
                B.step("tensor", lambda e, u=u: e.matmul(out=P4[0:64, :], lhsT=Hb[:, :], rhs=RT[u][:, :], start=True, stop=True), reads=("Hb", f"RT{u}", f"WVT{u}"), writes=("P4",))
                B.step("vector", lambda e, u=u: e.tensor_add(out=wkvf[u][:, :], in0=P4[0:64, :], in1=WVT[u][:, :]), reads=("P4", f"WVT{u}"), writes=(f"wkvf{u}",))
                B.step("scalar", lambda e, u=u, cs=cs, hb=hb: e.copy(out=wkvh[hb][:, cs], in_=wkvf[u][:, :]), reads=(f"wkvf{u}",), writes=(f"wkvh{hb}", f"wkvbc{u}"))
                B.step("scalar", lambda e, u=u: e.activation(out=wsq[u][:, :], in_=wkvf[u][:, :], func=AF.Square), reads=(f"wkvf{u}",), writes=(f"wsq{u}",))
                B.step("tensor", lambda e, cs=cs, c=c, u=u, hb=hb: e.matmul(out=ST1, lhsT=wkvh[hb][:, cs], rhs=sb["ones"][0:64, 0:1], start=False, stop=(c == NCH - 1)), reads=(f"wkvbc{u}",), writes=("ST1w",))
                B.step("tensor", lambda e, u=u, c=c: e.matmul(out=ST2, lhsT=wsq[u][:, :], rhs=sb["ones"][0:64, 0:1], start=False, stop=(c == NCH - 1)), reads=(f"wsq{u}",), writes=("ST2w",))
                B.step("tensor", lambda e, u=u: e.matmul(out=P5[0:64, :64], lhsT=FT[u][:, :], rhs=Hb[:, :], start=True, stop=True), reads=(f"FT{u}", "Hb", f"G{u}"), writes=("P5",))
                B.step("vector", lambda e, u=u: e.tensor_add(out=Hs[:, :], in0=P5[0:64, :64], in1=Gs[u][:, :]), reads=("P5", f"G{u}", "Hb"), writes=("Hs",))
                B.step("scalar", lambda e: e.copy(out=Hb[:, :], in_=Hs[:, :]), reads=("Hs",), writes=("Hb",))

            emit_stage1(0)
            emit_stage1(1)
            for c in range(NCH):
                emit_stage2(c)
                if c + 2 < NCH:
                    emit_stage1(c + 2)
            B.step("vector", lambda e: e.tensor_copy(out=stbf[:, 0:1], in_=ST1), reads=("ST1w",), writes=("stbf", "ST1"))
            B.step("vector", lambda e: e.tensor_copy(out=stbf[:, 1:2], in_=ST2), reads=("ST2w",), writes=("stbf", "ST2"))
            B.step("tensor", lambda e: e.matmul(out=P4[0:1, 0:1], lhsT=stbf[:, 0:1], rhs=sb["ones"][:, 0:1], start=True, stop=True), reads=("stbf",), writes=("P4",))
            B.step("vector", lambda e, h=h: e.tensor_copy(out=statsum[:, h:h+1], in_=P4[0:1, 0:1]), reads=("P4",), writes=("statsum",))
            B.step("tensor", lambda e: e.matmul(out=P4[0:1, 0:1], lhsT=stbf[:, 1:2], rhs=sb["ones"][:, 0:1], start=True, stop=True), reads=("stbf", "statsum"), writes=("P4",))
            B.step("vector", lambda e, h=h: e.tensor_copy(out=statsq[:, h:h+1], in_=P4[0:1, 0:1]), reads=("P4",), writes=("statsq",))
            B.step("sync", lambda e, h=h, hb=hb: e.dma_start(out=d["wkv_fm"][64*h:64*h+64, :], in_=wkvh[hb][:, :]), reads=(f"wkvh{hb}",), writes=(f"wkvst{hb}",))
        B.step("sync", lambda e: e.dma_start(out=d["stats"][0:1, :], in_=statsum[:, :]), reads=("statsum",))
        B.step("sync", lambda e: e.dma_start(out=d["stats"][1:2, :], in_=statsq[:, :]), reads=("statsq",))
        S.emit()


def phase_D(ctx):
    nc, blk, d, sb = ctx.nc, ctx.blk, ctx.d, ctx.sb
    S = Script(ctx, serial=not _phase_par("D"))
    B = BD(S)
    es = ExitStack()
    sbt = lambda name, shape, dt: es.enter_context(nc.sbuf_tensor(name, shape, dt))
    pst = lambda name, shape, dt: es.enter_context(nc.psum_tensor(name, shape, dt))
    wo = sbt("pd_wo", [128, 8 * 1024], BF16)
    woi = sbt("pd_woi", [128, 8 * 1024], F32)
    ssum = sbt("pd_ssum", [1, 16], F32)
    ssq = sbt("pd_ssq", [1, 16], F32)
    sm = sbt("pd_sm", [1, 16], F32)
    sq2 = sbt("pd_sq2", [1, 16], F32)
    m2t = sbt("pd_m2t", [1, 16], F32)
    var = sbt("pd_var", [1, 16], F32)
    rstd = sbt("pd_rstd", [1, 16], F32)
    mt = sbt("pd_mt", [16, 1], F32)
    rt = sbt("pd_rt", [16, 1], F32)
    rhsm = sbt("pd_rhsm", [16, 8], F32)
    MB = sbt("pd_MB", [128, 8], F32)
    RB = sbt("pd_RB", [128, 8], F32)
    wn1 = sbt("pd_wn1", [128, 512], BF16)
    wkvt = [sbt(f"pd_wkvt{j}", [128, 8 * 512], BF16) for j in range(2)]
    g0t = [sbt(f"pd_g0t{j}", [128, 8 * 512], BF16) for j in range(2)]
    wnt = [sbt(f"pd_wnt{j}", [128, 8 * 512], BF16) for j in range(2)]
    sft = [sbt(f"pd_sft{j}", [128, 8 * 512], BF16) for j in range(2)]
    PS = [pst(f"pd_PS{j}", [128, 512], F32) for j in range(4)]
    PQ = pst("pd_PQ", [128, 16], F32)
    NRM = 1.0 / (T * D)

    import os
    dbg = bool(int(os.environ.get("KB_DBG_D", "0")))
    if dbg:
        d["dbg_d"] = nc.dram_tensor("dbg_d", [128, 40], F32, kind="ExternalOutput").ap()
        d["dbg_wnt"] = nc.dram_tensor("dbg_wnt", [128, 4096], BF16, kind="ExternalOutput").ap()
        d["dbg_wo"] = nc.dram_tensor("dbg_wo", [128, 8192], BF16, kind="ExternalOutput").ap()
    with es:
        B.step("sync", lambda e: e.dma_start(out=woi.rearrange("p (k n) -> p k n", n=1024),
                                             in_=d["woT"].rearrange("(k p) n -> p k n", p=128)), writes=("woi",))
        B.step("vector", lambda e: e.tensor_copy(out=wo[:, :], in_=woi[:, :]), reads=("woi",), writes=("wo",))
        B.step("sync", lambda e: e.dma_start(out=ssum[:, :], in_=d["stats"][0:1, :]), writes=("ssum",))
        B.step("sync", lambda e: e.dma_start(out=ssq[:, :], in_=d["stats"][1:2, :]), writes=("ssq",))
        B.step("vector", lambda e: e.tensor_scalar_mul(out=sm[:, :], in0=ssum[:, :], scalar1=NRM), reads=("ssum",), writes=("sm",))
        B.step("vector", lambda e: e.tensor_scalar_mul(out=sq2[:, :], in0=ssq[:, :], scalar1=NRM), reads=("ssq",), writes=("sq2",))
        B.step("scalar", lambda e: e.activation(out=m2t[:, :], in_=sm[:, :], func=AF.Square), reads=("sm",), writes=("m2t",))
        B.step("vector", lambda e: e.tensor_sub(out=var[:, :], in0=sq2[:, :], in1=m2t[:, :]), reads=("sq2", "m2t"), writes=("var",))
        B.step("scalar", lambda e: e.activation(out=rstd[:, :], in_=var[:, :], func=AF.Sqrt, bias=sb["fconst"][0:1, 0:1]), reads=("var",), writes=("rstd",))
        B.step("vector", lambda e: e.reciprocal(out=rstd[:, :], in_=rstd[:, :]), reads=("rstd",), writes=("rstd",))
        B.step("tensor", lambda e: e.matmul(out=PQ[0:16, 0:1], lhsT=sm[0:1, :], rhs=sb["ones16f"][0:1, 0:1], start=True, stop=True), reads=("sm",), writes=("PQ",))
        B.step("vector", lambda e: e.tensor_copy(out=mt[:, :], in_=PQ[0:16, 0:1]), reads=("PQ",), writes=("mt",))
        B.step("tensor", lambda e: e.matmul(out=PQ[0:16, 0:1], lhsT=rstd[0:1, :], rhs=sb["ones16f"][0:1, 0:1], start=True, stop=True), reads=("rstd", "mt"), writes=("PQ",))
        B.step("vector", lambda e: e.tensor_copy(out=rt[:, :], in_=PQ[0:16, 0:1]), reads=("PQ",), writes=("rt",))
        for nm, src, dst in (("m", mt, MB), ("r", rt, RB)):
            B.step("vector", lambda e, src=src: e.tensor_scalar_mul(out=rhsm[:, :], in0=sb["halfsel"][:, :], scalar1=src[:, :]), reads=(nm + "tkey", "mt" if nm == "m" else "rt"), writes=("rhsm",))
            B.step("tensor", lambda e: e.matmul(out=PQ[:, 0:8], lhsT=sb["hsel2"][:, :], rhs=rhsm[:, :], start=True, stop=True), reads=("rhsm",), writes=("PQ",))
            B.step("vector", lambda e, dst=dst: e.tensor_copy(out=dst[:, :], in_=PQ[:, 0:8]), reads=("PQ",), writes=(nm + "B",))

        if dbg:
            B.step("sync", lambda e: e.dma_start(out=d["dbg_wo"], in_=wo[:, :]), reads=("wo",))
            dd = es.enter_context(nc.sbuf_tensor("pd_dbg", [128, 40], F32))
            def dumpit(e):
                e.memset(dd[:, :], 0.0)
                e.tensor_copy(out=dd[0:1, 0:16], in_=var[:, :])
                e.tensor_copy(out=dd[0:1, 16:32], in_=rstd[:, :])
                e.tensor_copy(out=dd[:, 32:40], in_=MB[:, :])
                ins = e.tensor_copy(out=dd[:, 24:32], in_=RB[:, :])
                return ins
            B.step("vector", dumpit, reads=("var", "rstd", "mB", "rB"), writes=("dd",))
            B.step("sync", lambda e: e.dma_start(out=d["dbg_d"], in_=dd[:, :]), reads=("dd",))
        for i in range(8):
            j = i % 2
            t0 = i * 512
            B.step("sync", lambda e, j=j, t0=t0: e.dma_start(
                out=wkvt[j].rearrange("p (k n) -> p k n", n=512),
                in_=d["wkv_fm"].rearrange("(k p) t -> p k t", p=128)[:, :, t0:t0 + 512]), writes=(f"wkvt{j}",))
            B.step("sync", lambda e, j=j, t0=t0: e.dma_start(
                out=g0t[j].rearrange("p (k n) -> p k n", n=512),
                in_=d["g0_fm"].rearrange("(k p) t -> p k t", p=128)[:, :, t0:t0 + 512]), writes=(f"g0t{j}",))
            for k in range(KC):
                gw = sb["vecs"][:, vcol("gnw", k):vcol("gnw", k) + 1]
                gb = sb["vecs"][:, vcol("gnb", k):vcol("gnb", k) + 1]
                B.step("vector", lambda e, j=j, k=k: e.tensor_scalar(
                    out=wn1[:, :], in0=wkvt[j][:, k*512:(k+1)*512],
                    scalar1=MB[:, k:k+1], scalar2=RB[:, k:k+1],
                    op0=ALU.subtract, op1=ALU.mult), reads=(f"wkvt{j}", "mB", "rB"), writes=("wn1",))
                B.step("vector", lambda e, j=j, k=k, gw=gw, gb=gb: e.tensor_scalar(
                    out=wnt[j][:, k*512:(k+1)*512], in0=wn1[:, :],
                    scalar1=gw, scalar2=gb,
                    op0=ALU.mult, op1=ALU.add), reads=("wn1",), writes=(f"wnt{j}",))
            if dbg and i == 0:
                B.step("sync", lambda e: e.dma_start(out=d["dbg_wnt"], in_=wnt[0][:, :]), reads=("wnt0",))
            for m in range(KC):
                bank = f"PS{m % 4}"
                pso = PS[m % 4]
                for k in range(KC):
                    B.step("tensor", lambda e, j=j, k=k, m=m, pso=pso: e.matmul(
                        out=pso[:, :], lhsT=wo[:, k*1024 + m*128:k*1024 + (m+1)*128],
                        rhs=wnt[j][:, k*512:(k+1)*512], start=(k == 0), stop=False),
                        reads=(f"wnt{j}", "wo"), writes=(bank,))
                B.step("tensor", lambda e, m=m, pso=pso: e.matmul(
                    out=pso[:, :], lhsT=sb["brow_sb"][0:1, BROW["b_o"]*1024 + m*128:BROW["b_o"]*1024 + (m+1)*128],
                    rhs=sb["onesrow"][0:1, :], start=False, stop=True), writes=(bank,))
                B.step("vector", lambda e, j=j, m=m, pso=pso: e.tensor_mul(
                    out=sft[j][:, m*512:(m+1)*512], in0=pso[:, :], in1=g0t[j][:, m*512:(m+1)*512]),
                    reads=(bank, f"g0t{j}"), writes=(f"sft{j}",))
            B.step("sync", lambda e, j=j, t0=t0: e.dma_start(
                out=d["s_fm"].rearrange("(k p) t -> p k t", p=128)[:, :, t0:t0 + 512],
                in_=sft[j].rearrange("p (k n) -> p k n", n=512)), reads=(f"sft{j}",), writes=(f"sst{j}",))
        S.emit()


def phase_E(ctx):
    nc, blk, d, sb = ctx.nc, ctx.blk, ctx.d, ctx.sb
    S = Script(ctx, serial=not _phase_par("E"))
    B = BD(S)
    es = ExitStack()
    sbt = lambda name, shape, dt: es.enter_context(nc.sbuf_tensor(name, shape, dt))
    pst = lambda name, shape, dt: es.enter_context(nc.psum_tensor(name, shape, dt))
    xnt = sbt("pe_xnt", [128, 8 * 1024], BF16)
    BAm = [sbt(f"pe_BA{m}", [128, 1024], BF16) for m in range(32)]
    BBm = [sbt(f"pe_BB{m}", [128, 1024], BF16) for m in range(32)]
    wA = [sbt(f"pe_wA{j}", [128, 8 * 128], BF16) for j in range(2)]
    wB = [sbt(f"pe_wB{j}", [128, 8 * 128], BF16) for j in range(2)]
    wH = [sbt(f"pe_wH{j}", [128, 512], BF16) for j in range(4)]
    t1b = sbt("pe_t1", [128, 1024], BF16)
    t2b = sbt("pe_t2", [128, 1024], BF16)
    gft = sbt("pe_gft", [128, 8 * 1024], BF16)
    PS = [pst(f"pe_PS{j}", [128, 512], F32) for j in range(8)]

    def wslice(name, m):
        return d[name].rearrange("(k p) n -> p k n", p=128)[:, :, m*128:(m+1)*128]

    with es:
        for it in range(4):
            t0 = it * 1024
            B.step("sync", lambda e, t0=t0: e.dma_start(
                out=xnt.rearrange("p (k n) -> p k n", n=1024),
                in_=d["xn_fm"].rearrange("(k p) t -> p k t", p=128)[:, :, t0:t0 + 1024]), writes=("xnt",))
            for m in range(32):
                j = m % 2
                B.step("gpsimd", lambda e, j=j, m=m: e.dma_start(
                    out=wA[j].rearrange("p (k n) -> p k n", n=128), in_=wslice("w1T", m)), writes=(f"wA{j}",))
                B.step("gpsimd", lambda e, j=j, m=m: e.dma_start(
                    out=wB[j].rearrange("p (k n) -> p k n", n=128), in_=wslice("w2T", m)), writes=(f"wB{j}",))
                for half in range(2):
                    pa, pb = PS[half], PS[2 + half]
                    for k in range(KC):
                        B.step("tensor", lambda e, j=j, k=k, half=half, pa=pa: e.matmul(
                            out=pa[:, :], lhsT=wA[j][:, k*128:(k+1)*128],
                            rhs=xnt[:, k*1024 + half*512:k*1024 + half*512 + 512],
                            start=(k == 0), stop=(k == KC - 1)), reads=(f"wA{j}", "xnt"), writes=(f"bank{half}",))
                    for k in range(KC):
                        B.step("tensor", lambda e, j=j, k=k, half=half, pb=pb: e.matmul(
                            out=pb[:, :], lhsT=wB[j][:, k*128:(k+1)*128],
                            rhs=xnt[:, k*1024 + half*512:k*1024 + half*512 + 512],
                            start=(k == 0), stop=(k == KC - 1)), reads=(f"wB{j}", "xnt"), writes=(f"bank{2+half}",))
                    s1c = sb["vecs"][:, vcol("s1", m):vcol("s1", m) + 1]
                    s2c = sb["vecs"][:, vcol("s2", m):vcol("s2", m) + 1]
                    B.step("scalar", lambda e, half=half, pa=pa, s1c=s1c: e.activation(
                        out=t1b[:, half*512:half*512+512], in_=pa[:, :], func=AF.Silu, scale=s1c),
                        reads=(f"bank{half}",), writes=("t1b",))
                    B.step("vector", lambda e, half=half, pb=pb, s2c=s2c: e.tensor_scalar_mul(
                        out=t2b[:, half*512:half*512+512], in0=pb[:, :], scalar1=s2c),
                        reads=(f"bank{2+half}",), writes=("t2b",))
                    B.step("vector", lambda e, m=m, half=half: e.tensor_mul(
                        out=BBm[m][:, half*512:half*512+512], in0=t1b[:, half*512:half*512+512],
                        in1=t2b[:, half*512:half*512+512]), reads=("t1b", "t2b"), writes=(f"BB{m}",))
            for stepname, wname, scol, SRC, DST, srck, dstk in (
                    ("B", "h1T", "sh1", BBm, BAm, "BB", "BA"),
                    ("C", "h2T", "sh2", BAm, BBm, "BA", "BB")):
                for mg in range(8):
                    for k in range(32):
                        j = k % 4
                        B.step("gpsimd", lambda e, j=j, k=k, mg=mg, wname=wname: e.dma_start(
                            out=wH[j][:, :],
                            in_=d[wname][k*128:(k+1)*128, mg*512:(mg+1)*512]), writes=(f"wH{j}",))
                        for mi in range(4):
                            m = mg * 4 + mi
                            for half in range(2):
                                ps = PS[mi * 2 + half]
                                B.step("tensor", lambda e, j=j, k=k, mi=mi, half=half, ps=ps, m=m, SRC=SRC: e.matmul(
                                    out=ps[:, :], lhsT=wH[j][:, mi*128:(mi+1)*128],
                                    rhs=SRC[k][:, half*512:half*512+512],
                                    start=(k == 0), stop=(k == 31)),
                                    reads=(f"wH{j}", f"{srck}{k}"), writes=(f"bank{mi*2+half}",))
                    for mi in range(4):
                        m = mg * 4 + mi
                        sc = sb["vecs"][:, vcol(scol, m):vcol(scol, m) + 1]
                        for half in range(2):
                            ps = PS[mi * 2 + half]
                            B.step("scalar", lambda e, ps=ps, m=m, half=half, sc=sc, DST=DST: e.activation(
                                out=DST[m][:, half*512:half*512+512], in_=ps[:, :], func=AF.Gelu, scale=sc),
                                reads=(f"bank{mi*2+half}",), writes=(f"{dstk}{m}",))
            for mg in range(2):
                for k in range(32):
                    j = k % 4
                    B.step("gpsimd", lambda e, j=j, k=k, mg=mg: e.dma_start(
                        out=wH[j][:, :],
                        in_=d["w3T"][k*128:(k+1)*128, mg*512:(mg+1)*512]), writes=(f"wH{j}",))
                    for mi in range(4):
                        m = mg * 4 + mi
                        for half in range(2):
                            ps = PS[mi * 2 + half]
                            B.step("tensor", lambda e, j=j, k=k, mi=mi, half=half, ps=ps: e.matmul(
                                out=ps[:, :], lhsT=wH[j][:, mi*128:(mi+1)*128],
                                rhs=BBm[k][:, half*512:half*512+512],
                                start=(k == 0), stop=(k == 31)),
                                reads=(f"wH{j}", f"BB{k}"), writes=(f"bank{mi*2+half}",))
                for mi in range(4):
                    m = mg * 4 + mi
                    sc = sb["vecs"][:, vcol("s3", m):vcol("s3", m) + 1]
                    for half in range(2):
                        ps = PS[mi * 2 + half]
                        B.step("scalar", lambda e, ps=ps, m=m, half=half, sc=sc: e.activation(
                            out=gft[:, m*1024 + half*512:m*1024 + half*512 + 512],
                            in_=ps[:, :], func=AF.Silu, scale=sc),
                            reads=(f"bank{mi*2+half}",), writes=("gft",))
            B.step("sync", lambda e, t0=t0: e.dma_start(
                out=d["gf_fm"].rearrange("(k p) t -> p k t", p=128)[:, :, t0:t0 + 1024],
                in_=gft.rearrange("p (k n) -> p k n", n=1024)), reads=("gft",), writes=("gfst",))
        S.emit()


def phase_F(ctx):
    nc, blk, d, sb = ctx.nc, ctx.blk, ctx.d, ctx.sb
    S = Script(ctx, serial=not _phase_par("F"))
    B = BD(S)
    es = ExitStack()
    sbt = lambda name, shape, dt: es.enter_context(nc.sbuf_tensor(name, shape, dt))
    pst = lambda name, shape, dt: es.enter_context(nc.psum_tensor(name, shape, dt))
    xin = [sbt(f"pf_x{j}", [128, C], F32) for j in range(2)]
    sft = [sbt(f"pf_s{j}", [128, C], BF16) for j in range(2)]
    gft = [sbt(f"pf_g{j}", [128, C], BF16) for j in range(2)]
    prod = [sbt(f"pf_p{j}", [128, C], BF16) for j in range(2)]
    x1t = [sbt(f"pf_x1{j}", [128, C], F32) for j in range(2)]
    xsq = [sbt(f"pf_xsq{j}", [128, C], F32) for j in range(2)]
    xnbf = [sbt(f"pf_xnbf{j}", [128, C], BF16) for j in range(2)]
    stage = [sbt(f"pf_st{j}", [128, C], BF16) for j in range(2)]
    st = sbt("pf_stat", [128, 8], F32)
    S1c, S2c, MEAN, EX2, M2, VAR, SD, RSTD = range(8)
    pstr = [pst(f"pf_ps{j}", [128, 128], BF16) for j in range(2)]
    ident = sb["ident"]

    with es:
        for pos, i in enumerate(list(range(NT)) + [0]):
            j = pos % 2
            r0 = i * 128
            B.step("sync", lambda e, j=j, r0=r0: e.dma_start(out=xin[j][:, :], in_=d["x"][r0:r0+128, :]), writes=(f"x{j}",))
            B.step("sync", lambda e, j=j, r0=r0: e.dma_start(
                out=sft[j].rearrange("p (k n) -> p k n", n=128),
                in_=d["s_fm"].rearrange("(k p) t -> p k t", p=128)[:, :, r0:r0+128]), writes=(f"s{j}",))
            B.step("sync", lambda e, j=j, r0=r0: e.dma_start(
                out=gft[j].rearrange("p (k n) -> p k n", n=128),
                in_=d["gf_fm"].rearrange("(k p) t -> p k t", p=128)[:, :, r0:r0+128]), writes=(f"g{j}",))
            B.step("vector", lambda e, j=j: e.tensor_mul(out=prod[j][:, :], in0=sft[j][:, :], in1=gft[j][:, :]),
                   reads=(f"s{j}", f"g{j}"), writes=(f"p{j}",))
            for k in range(KC):
                pj = (pos * KC + k) % 2
                B.step("tensor", lambda e, j=j, k=k, pj=pj: e.transpose(
                    out=pstr[pj][:, :], in_=prod[j][:, k*128:(k+1)*128], identity=ident[:, :]),
                    reads=(f"p{j}",), writes=(f"ptr{pj}",))
                B.step("vector", lambda e, j=j, k=k, pj=pj: e.tensor_add(
                    out=x1t[j][:, k*128:(k+1)*128], in0=pstr[pj][:, :], in1=xin[j][:, k*128:(k+1)*128]),
                    reads=(f"ptr{pj}", f"x{j}"), writes=(f"x1{j}",))
            B.step("sync", lambda e, j=j, r0=r0: e.dma_start(out=d["x1_tm"][r0:r0+128, :], in_=x1t[j][:, :]),
                   reads=(f"x1{j}",), writes=(f"x1st{j}",))
            B.step("vector", lambda e, j=j: e.reduce_sum(out=st[:, S1c:S1c+1], in_=x1t[j][:, :], axis=AX.X),
                   reads=(f"x1{j}",), writes=("st",))
            B.step("scalar", lambda e, j=j: e.activation(out=xsq[j][:, :], in_=x1t[j][:, :], func=AF.Square),
                   reads=(f"x1{j}",), writes=(f"xsq{j}",))
            B.step("vector", lambda e, j=j: e.reduce_sum(out=st[:, S2c:S2c+1], in_=xsq[j][:, :], axis=AX.X),
                   reads=(f"xsq{j}",), writes=("st",))
            B.step("vector", lambda e: e.tensor_scalar_mul(out=st[:, MEAN:MEAN+1], in0=st[:, S1c:S1c+1], scalar1=1.0/C), reads=("st",), writes=("st",))
            B.step("vector", lambda e: e.tensor_scalar_mul(out=st[:, EX2:EX2+1], in0=st[:, S2c:S2c+1], scalar1=1.0/C), reads=("st",), writes=("st",))
            B.step("vector", lambda e: e.tensor_scalar_mul(out=st[:, M2:M2+1], in0=st[:, MEAN:MEAN+1], scalar1=st[:, MEAN:MEAN+1]), reads=("st",), writes=("st",))
            B.step("vector", lambda e: e.tensor_sub(out=st[:, VAR:VAR+1], in0=st[:, EX2:EX2+1], in1=st[:, M2:M2+1]), reads=("st",), writes=("st",))
            B.step("scalar", lambda e: e.activation(out=st[:, SD:SD+1], in_=st[:, VAR:VAR+1], func=AF.Sqrt, bias=sb["fconst"][:, 0:1]), reads=("st",), writes=("st2",))
            B.step("vector", lambda e: e.reciprocal(out=st[:, RSTD:RSTD+1], in_=st[:, SD:SD+1]), reads=("st2",), writes=("st",))
            B.step("vector", lambda e, j=j: e.tensor_scalar(out=xnbf[j][:, :], in0=x1t[j][:, :],
                   scalar1=st[:, MEAN:MEAN+1], scalar2=st[:, RSTD:RSTD+1], op0=ALU.subtract, op1=ALU.mult),
                   reads=("st", f"x1{j}"), writes=(f"xnbf{j}",))
            for k in range(KC):
                pj = (pos * KC + k) % 2
                B.step("tensor", lambda e, j=j, k=k, pj=pj: e.transpose(
                    out=pstr[pj][:, :], in_=xnbf[j][:, k*128:(k+1)*128], identity=ident[:, :]),
                    reads=(f"xnbf{j}",), writes=(f"ptr{pj}",))
                B.step("vector", lambda e, j=j, k=k, pj=pj: e.tensor_copy(
                    out=stage[j][:, k*128:(k+1)*128], in_=pstr[pj][:, :]),
                    reads=(f"ptr{pj}",), writes=(f"stg{j}",))
            B.step("sync", lambda e, j=j, r0=r0: e.dma_start(
                out=d["xn2_fm"].rearrange("(k p) t -> p k t", p=128)[:, :, r0:r0+128],
                in_=stage[j].rearrange("p (k n) -> p k n", n=128)),
                reads=(f"stg{j}",), writes=(f"xnst{j}",))
        S.emit()


def phase_G(ctx):
    nc, blk, d, sb = ctx.nc, ctx.blk, ctx.d, ctx.sb
    S = Script(ctx, serial=not _phase_par("G"))
    B = BD(S)
    es = ExitStack()
    sbt = lambda name, shape, dt: es.enter_context(nc.sbuf_tensor(name, shape, dt))
    pst = lambda name, shape, dt: es.enter_context(nc.psum_tensor(name, shape, dt))
    xnt = sbt("pg_xnt", [128, 8 * T], BF16)
    ropec = sbt("pg_ropec", [128, T], F32)
    ropes = sbt("pg_ropes", [128, T], F32)
    maskT4 = sbt("pg_maskT4", [128, 2048], BF16)
    wsb = {nm: sbt(f"pg_w{nm}", [128, 8 * 128], BF16) for nm in ("q", "k", "v")}
    qf = sbt("pg_qf", [128, T], F32)
    kf = qf
    t1 = sbt("pg_t1", [128, T], BF16)
    t2 = sbt("pg_t2", [128, T], BF16)
    Qb = sbt("pg_Qb", [128, T], BF16)
    Kb = sbt("pg_Kb", [128, T], BF16)
    Vb = sbt("pg_Vb", [128, T], BF16)
    v0 = sbt("pg_v0", [64, 128], BF16)
    vtm = sbt("pg_vtm", [128, 64 * 65], BF16)
    Pb = [sbt(f"pg_P{j}", [128, 512], BF16) for j in range(2)]
    rn = sbt("pg_rn", [1, 512], F32)
    osb = sbt("pg_osb", [64, 512], F32)
    ob2 = [sbt(f"pg_ob2{j}", [64, 512], BF16) for j in range(2)]
    PS = pst("pg_PS", [128, 512], F32)
    PO = pst("pg_PO", [128, 512], F32)
    PB2 = pst("pg_PB2", [64, 512], F32)
    PP = [pst(f"pg_PP{j}", [128, 512], F32) for j in range(4)]
    PTv = pst("pg_PTv", [128, 64], BF16)
    o_scr = d["o_scr"]

    if True:
        B.step("sync", lambda e: e.dma_start(out=maskT4[:, :], in_=d["maskT4"]), writes=("maskT4l",))
        B.step("sync", lambda e: e.dma_start(out=ropec[:, :], in_=d["ropecos"]), writes=("ropec",))
        B.step("sync", lambda e: e.dma_start(out=ropes[:, :], in_=d["ropes2"] if False else d["ropesin"]), writes=("ropes",))
        B.step("sync", lambda e: e.dma_start(
            out=xnt.rearrange("p (k n) -> p k n", n=T),
            in_=d["xn2_fm"].rearrange("(k p) t -> p k t", p=128)), writes=("xnt",))
        for hp in range(8):
            for nm, wname, bname in (("q", "wqT", "bq"), ("k", "wk1T", "bk1"), ("v", "wv1T", "bv1")):
                B.step("gpsimd", lambda e, nm=nm, wname=wname, hp=hp: e.dma_start(
                    out=wsb[nm].rearrange("p (k n) -> p k n", n=128),
                    in_=d[wname].rearrange("(k p) n -> p k n", p=128)[:, :, hp*128:(hp+1)*128]),
                    writes=(f"w{nm}",))
            for nm, bname, dstk, dst in (("q", "bq", "qf", qf), ("v", "bv1", None, None)):
                for tt in range(8):
                    for k in range(KC):
                        B.step("tensor", lambda e, nm=nm, tt=tt, k=k: e.matmul(
                            out=PS[:, :], lhsT=wsb[nm][:, k*128:(k+1)*128],
                            rhs=xnt[:, k*T + tt*512:k*T + tt*512 + 512],
                            start=(k == 0), stop=False), reads=(f"w{nm}", "xnt"), writes=("PS",))
                    B.step("tensor", lambda e, nm=nm, bname=bname, hp=hp: e.matmul(
                        out=PS[:, :], lhsT=sb["brow_sb"][0:1, BROW[bname]*1024 + hp*128:BROW[bname]*1024 + (hp+1)*128],
                        rhs=sb["onesrow"][0:1, :], start=False, stop=True), writes=("PS",))
                    if nm == "v":
                        B.step("vector", lambda e, tt=tt: e.tensor_copy(out=Vb[:, tt*512:(tt+1)*512], in_=PS[:, :]),
                               reads=("PS",), writes=("Vb",))
                    else:
                        B.step("vector", lambda e, dst=dst, tt=tt: e.tensor_copy(out=dst[:, tt*512:(tt+1)*512], in_=PS[:, :]),
                               reads=("PS",), writes=(dstk,))
            for src, dst, srck, dstk in ((qf, Qb, "qf", "Qb"),):
                for h2 in range(2):
                    re0 = h2 * 64
                    im0 = h2 * 64 + 32
                    eng = "vector" if h2 == 0 else "gpsimd"
                    B.step(eng, lambda e, src=src, re0=re0: e.tensor_mul(
                        out=t1[re0:re0+32, :], in0=src[re0:re0+32, :], in1=ropec[re0:re0+32, :]),
                        reads=(srck, "ropec"), writes=("t1",))
                    B.step(eng, lambda e, src=src, re0=re0, im0=im0: e.tensor_mul(
                        out=t2[re0:re0+32, :], in0=src[im0:im0+32, :], in1=ropes[im0:im0+32, :]),
                        reads=(srck, "ropes"), writes=("t2",))
                    B.step(eng, lambda e, dst=dst, re0=re0: e.tensor_sub(
                        out=dst[re0:re0+32, :], in0=t1[re0:re0+32, :], in1=t2[re0:re0+32, :]),
                        reads=("t1", "t2"), writes=(dstk,))
                    B.step(eng, lambda e, src=src, re0=re0, im0=im0: e.tensor_mul(
                        out=t1[im0:im0+32, :], in0=src[re0:re0+32, :], in1=ropes[re0:re0+32, :]),
                        reads=(srck, "ropes", dstk), writes=("t1",))
                    B.step(eng, lambda e, src=src, re0=re0, im0=im0: e.tensor_mul(
                        out=t2[im0:im0+32, :], in0=src[im0:im0+32, :], in1=ropec[im0:im0+32, :]),
                        reads=(srck, "ropec"), writes=("t2",))
                    B.step(eng, lambda e, dst=dst, im0=im0: e.tensor_add(
                        out=dst[im0:im0+32, :], in0=t1[im0:im0+32, :], in1=t2[im0:im0+32, :]),
                        reads=("t1", "t2"), writes=(dstk,))
            for nm, bname in (("k", "bk1"),):
                for tt in range(8):
                    for k in range(KC):
                        B.step("tensor", lambda e, nm=nm, tt=tt, k=k: e.matmul(
                            out=PS[:, :], lhsT=wsb[nm][:, k*128:(k+1)*128],
                            rhs=xnt[:, k*T + tt*512:k*T + tt*512 + 512],
                            start=(k == 0), stop=False), reads=(f"w{nm}", "xnt"), writes=("PS",))
                    B.step("tensor", lambda e, nm=nm, bname=bname, hp=hp: e.matmul(
                        out=PS[:, :], lhsT=sb["brow_sb"][0:1, BROW[bname]*1024 + hp*128:BROW[bname]*1024 + (hp+1)*128],
                        rhs=sb["onesrow"][0:1, :], start=False, stop=True), writes=("PS",))
                    B.step("vector", lambda e, tt=tt: e.tensor_copy(out=kf[:, tt*512:(tt+1)*512], in_=PS[:, :]),
                           reads=("PS",), writes=("qf",))
            for src, dst, srck, dstk in ((kf, Kb, "qf", "Kb"),):
                for h2 in range(2):
                    re0 = h2 * 64
                    im0 = h2 * 64 + 32
                    eng = "vector" if h2 == 0 else "gpsimd"
                    B.step(eng, lambda e, src=src, re0=re0: e.tensor_mul(
                        out=t1[re0:re0+32, :], in0=src[re0:re0+32, :], in1=ropec[re0:re0+32, :]),
                        reads=(srck, "ropec"), writes=("t1",))
                    B.step(eng, lambda e, src=src, re0=re0, im0=im0: e.tensor_mul(
                        out=t2[re0:re0+32, :], in0=src[im0:im0+32, :], in1=ropes[im0:im0+32, :]),
                        reads=(srck, "ropes"), writes=("t2",))
                    B.step(eng, lambda e, dst=dst, re0=re0: e.tensor_sub(
                        out=dst[re0:re0+32, :], in0=t1[re0:re0+32, :], in1=t2[re0:re0+32, :]),
                        reads=("t1", "t2"), writes=(dstk,))
                    B.step(eng, lambda e, src=src, re0=re0, im0=im0: e.tensor_mul(
                        out=t1[im0:im0+32, :], in0=src[re0:re0+32, :], in1=ropes[re0:re0+32, :]),
                        reads=(srck, "ropes", dstk), writes=("t1",))
                    B.step(eng, lambda e, src=src, re0=re0, im0=im0: e.tensor_mul(
                        out=t2[im0:im0+32, :], in0=src[im0:im0+32, :], in1=ropec[im0:im0+32, :]),
                        reads=(srck, "ropec"), writes=("t2",))
                    B.step(eng, lambda e, dst=dst, im0=im0: e.tensor_add(
                        out=dst[im0:im0+32, :], in0=t1[im0:im0+32, :], in1=t2[im0:im0+32, :]),
                        reads=("t1", "t2"), writes=(dstk,))
            B.step("vector", lambda e: e.memset(vtm[:, :], 1.0), writes=("vtm",))
            for sc in range(32):
                for h2 in range(2):
                    pj = (sc * 2 + h2) % 2
                    if h2 == 1:
                        B.step("vector", lambda e, sc=sc: e.tensor_copy(out=v0[:, :], in_=Vb[64:128, sc*128:(sc+1)*128]),
                               reads=("Vb",), writes=("v0",))
                    B.step("tensor", lambda e, sc=sc, h2=h2: e.transpose(
                        out=PTv[:, :],
                        in_=(Vb[0:64, sc*128:(sc+1)*128] if h2 == 0 else v0[:, :]),
                        identity=sb["ident"][0:64, 0:64]),
                        reads=("Vb" if h2 == 0 else "v0",), writes=("PTv",))
                    B.step("vector", lambda e, sc=sc, h2=h2: e.tensor_copy(
                        out=vtm[:, (sc*2+h2)*65:(sc*2+h2)*65+64], in_=PTv[:, :]),
                        reads=("PTv",), writes=("vtm",))
            for h2 in range(2):
                hb = h2 * 64
                for qt in range(8):
                    ns = (qt + 1) * 4
                    for stt in range(ns):
                        B.step("tensor", lambda e, h2=h2, hb=hb, qt=qt, stt=stt: e.matmul(
                            out=PS[:, :], lhsT=Kb[hb:hb+64, stt*128:(stt+1)*128],
                            rhs=Qb[hb:hb+64, qt*512:(qt+1)*512], start=True, stop=True),
                            reads=("Kb", "Qb"), writes=("PS",))
                        pj = stt % 2
                        B.step("scalar", lambda e, pj=pj: e.activation(
                            out=Pb[pj][:, :], in_=PS[:, :], func=AF.Exp, scale=0.125,
                            bias=sb["fconst"][:, 1:2]), reads=("PS",), writes=(f"Pb{pj}",))
                        if stt >= qt * 4:
                            jm = stt - qt * 4
                            B.step("vector", lambda e, pj=pj, jm=jm: e.tensor_mul(
                                out=Pb[pj][:, :], in0=Pb[pj][:, :], in1=maskT4[:, jm*512:(jm+1)*512]),
                                reads=(f"Pb{pj}", "maskT4l"), writes=(f"Pb{pj}",))
                        B.step("tensor", lambda e, pj=pj, stt=stt, h2=h2, ns=ns: e.matmul(
                            out=PO[0:65, :], lhsT=vtm[:, (stt*2+h2)*65:(stt*2+h2)*65+65],
                            rhs=Pb[pj][:, :], start=(stt == 0), stop=(stt == ns - 1)),
                            reads=(f"Pb{pj}", "vtm"), writes=("PO",))
                    B.step("vector", lambda e: e.reciprocal(out=rn[:, :], in_=PO[64:65, :]), reads=("PO",), writes=("rn",))
                    B.step("vector", lambda e: e.tensor_copy(out=osb[:, :], in_=PO[0:64, :]), reads=("PO",), writes=("osb",))
                    B.step("tensor", lambda e: e.matmul(out=PB2[:, :], lhsT=sb["ones16f"][0:1, 0:64], rhs=rn[:, :],
                                                        start=True, stop=True), reads=("rn", "osb"), writes=("PB2",))
                    oj = qt % 2
                    B.step("vector", lambda e, oj=oj: e.tensor_mul(
                        out=ob2[oj][:, :], in0=osb[:, :], in1=PB2[:, :]),
                        reads=("osb", "PB2"), writes=(f"ob2{oj}",))
                    r0 = hp * 128 + h2 * 64
                    B.step("sync", lambda e, oj=oj, r0=r0, qt=qt: e.dma_start(
                        out=o_scr[r0:r0+64, qt*512:(qt+1)*512], in_=ob2[oj][:, :]),
                        reads=(f"ob2{oj}",), writes=("ost",))
        es.close()
        es2 = ExitStack()
        globals()
        ot = [es2.enter_context(nc.sbuf_tensor(f"pg_ot{j}", [128, 8 * 512], BF16)) for j in range(2)]
        s2t = [es2.enter_context(nc.sbuf_tensor(f"pg_s2{j}", [128, 8 * 512], BF16)) for j in range(2)]
        wps = es2.enter_context(nc.sbuf_tensor("pg_wps", [128, 8 * 128], BF16))
        for i in range(8):
            j = i % 2
            t0 = i * 512
            B.step("sync", lambda e, j=j, t0=t0: e.dma_start(
                out=ot[j].rearrange("p (k n) -> p k n", n=512),
                in_=o_scr.rearrange("(k p) t -> p k t", p=128)[:, :, t0:t0+512]),
                reads=("ost", "obf", "vtm", "Qb", "Kb", "Vb", "xnt"), writes=(f"ot{j}",))
            for m in range(KC):
                B.step("gpsimd", lambda e, m=m: e.dma_start(
                    out=wps.rearrange("p (k n) -> p k n", n=128),
                    in_=d["wpT"].rearrange("(k p) n -> p k n", p=128)[:, :, m*128:(m+1)*128]),
                    reads=("ost",), writes=("wq",))
                pj = m % 4
                for k in range(KC):
                    B.step("tensor", lambda e, j=j, k=k, pj=pj: e.matmul(
                        out=PP[pj][:, :], lhsT=wps[:, k*128:(k+1)*128],
                        rhs=ot[j][:, k*512:(k+1)*512], start=(k == 0), stop=False),
                        reads=("wq", f"ot{j}"), writes=(f"PPt{pj}",))
                B.step("tensor", lambda e, m=m, pj=pj: e.matmul(
                    out=PP[pj][:, :], lhsT=sb["brow_sb"][0:1, BROW["bp"]*1024 + m*128:BROW["bp"]*1024 + (m+1)*128],
                    rhs=sb["onesrow"][0:1, :], start=False, stop=True), writes=(f"PPt{pj}",))
                B.step("vector", lambda e, j=j, m=m, pj=pj: e.tensor_copy(
                    out=s2t[j][:, m*512:(m+1)*512], in_=PP[pj][:, :]), reads=(f"PPt{pj}",), writes=(f"s2{j}",))
            B.step("sync", lambda e, j=j, t0=t0: e.dma_start(
                out=d["s2_fm"].rearrange("(k p) t -> p k t", p=128)[:, :, t0:t0+512],
                in_=s2t[j].rearrange("p (k n) -> p k n", n=512)), reads=(f"s2{j}",), writes=(f"s2st{j}",))
        S.emit()
        es2.close()


def phase_H(ctx):
    nc, blk, d, sb = ctx.nc, ctx.blk, ctx.d, ctx.sb
    import os as _os
    _serh = _os.environ.get("KB_SER_H")
    S = Script(ctx, serial=(_serh if _serh is not None else not _phase_par("H")),
               serial_first=int(_os.environ.get("KB_SERF_H", "40")))
    _sec = _os.environ.get("KB_SERIAL_SEC", "")
    B = BD(S)
    es = ExitStack()
    sbt = lambda name, shape, dt: es.enter_context(nc.sbuf_tensor(name, shape, dt))
    pst = lambda name, shape, dt: es.enter_context(nc.psum_tensor(name, shape, dt))
    _xnt0 = sbt("ph_xnt0", [128, 8 * 1024], BF16)
    xnt = [_xnt0, _xnt0]
    wg = [sbt(f"ph_wg{j}", [128, 8 * 128], BF16) for j in range(2)]
    wu = [sbt(f"ph_wu{j}", [128, 8 * 128], BF16) for j in range(2)]
    wgi = [sbt(f"ph_wgi{j}", [128, 8 * 128], I32) for j in range(2)]
    wui = [sbt(f"ph_wui{j}", [128, 8 * 128], I32) for j in range(2)]
    wd = [sbt(f"ph_wd{j}", [128, 512], BF16) for j in range(4)]
    wdi = [sbt(f"ph_wdi{j}", [128, 512], I32) for j in range(4)]
    hb = [sbt(f"ph_hb{m}", [128, 1024], BF16) for m in range(24)]
    t1b = sbt("ph_t1", [128, 1024], BF16)
    t2b = sbt("ph_t2", [128, 1024], BF16)
    g2t = sbt("ph_g2", [128, 8 * 1024], BF16)
    s2l = sbt("ph_s2l", [128, 8 * 1024], BF16)
    prod = sbt("ph_prod", [128, 8 * 1024], BF16)
    x1l = [sbt(f"ph_x1{j}", [128, 1024], F32) for j in range(2)]
    yt = [sbt(f"ph_y{j}", [128, 1024], F32) for j in range(2)]
    PS = [pst(f"ph_PS{j}", [128, 512], F32) for j in range(6)]
    ptr = [pst(f"ph_ptr{j}", [128, 128], BF16) for j in range(2)]
    SCL = 1.0 / 127.0
    ident = sb["ident"]

    with es:
        for it in range(4):
            t0 = it * 1024
            jx = it % 2
            B.step("sync", lambda e, jx=jx, t0=t0: e.dma_start(
                out=xnt[jx].rearrange("p (k n) -> p k n", n=1024),
                in_=d["xn2_fm"].rearrange("(k p) t -> p k t", p=128)[:, :, t0:t0 + 1024]), writes=(f"xnt{jx}",))
            pass
            for m in range(24):
                j = m % 2
                B.step("gpsimd", lambda e, j=j, m=m: e.dma_start(
                    out=wgi[j].rearrange("p (k n) -> p k n", n=128),
                    in_=d["gateT"].rearrange("(k p) n -> p k n", p=128)[:, :, m*128:(m+1)*128]), writes=(f"wgi{j}",))
                B.step("gpsimd", lambda e, j=j, m=m: e.dma_start(
                    out=wui[j].rearrange("p (k n) -> p k n", n=128),
                    in_=d["upT"].rearrange("(k p) n -> p k n", p=128)[:, :, m*128:(m+1)*128]), writes=(f"wui{j}",))
                B.step("vector", lambda e, j=j: e.tensor_copy(out=wg[j][:, :], in_=wgi[j][:, :]),
                       reads=(f"wgi{j}",), writes=(f"wg{j}",))
                B.step("vector", lambda e, j=j: e.tensor_copy(out=wu[j][:, :], in_=wui[j][:, :]),
                       reads=(f"wui{j}",), writes=(f"wu{j}",))
                for half in range(2):
                    pa, pb = PS[half], PS[2 + half]
                    for k in range(KC):
                        B.step("tensor", lambda e, j=j, jx=jx, k=k, half=half, pa=pa: e.matmul(
                            out=pa[:, :], lhsT=wg[j][:, k*128:(k+1)*128],
                            rhs=xnt[jx][:, k*1024 + half*512:k*1024 + half*512 + 512],
                            start=(k == 0), stop=(k == KC - 1)), reads=(f"wg{j}", f"xnt{jx}"), writes=(f"bank{half}",))
                    for k in range(KC):
                        B.step("tensor", lambda e, j=j, jx=jx, k=k, half=half, pb=pb: e.matmul(
                            out=pb[:, :], lhsT=wu[j][:, k*128:(k+1)*128],
                            rhs=xnt[jx][:, k*1024 + half*512:k*1024 + half*512 + 512],
                            start=(k == 0), stop=(k == KC - 1)), reads=(f"wu{j}", f"xnt{jx}"), writes=(f"bank{2+half}",))
                    B.step("scalar", lambda e, half=half, pa=pa: e.activation(
                        out=t1b[:, half*512:half*512+512], in_=pa[:, :], func=AF.Silu, scale=SCL),
                        reads=(f"bank{half}",), writes=("t1b",))
                    B.step("vector", lambda e, half=half, pb=pb: e.tensor_scalar_mul(
                        out=t2b[:, half*512:half*512+512], in0=pb[:, :], scalar1=SCL),
                        reads=(f"bank{2+half}",), writes=("t2b",))
                    B.step("vector", lambda e, m=m, half=half: e.tensor_mul(
                        out=hb[m][:, half*512:half*512+512], in0=t1b[:, half*512:half*512+512],
                        in1=t2b[:, half*512:half*512+512]), reads=("t1b", "t2b", "downdone"), writes=(f"hb{m}",))
            pass
            for m0, m1 in ((0, 3), (3, 6), (6, 8)):
                nmi = m1 - m0
                for k in range(24):
                    j = k % 4
                    B.step("sync", lambda e, j=j, k=k, m0=m0, m1=m1: e.dma_start(
                        out=wdi[j][:, 0:(m1-m0)*128],
                        in_=d["downT"][k*128:(k+1)*128, m0*128:m1*128]), writes=(f"wdi{j}",))
                    B.step("vector", lambda e, j=j, m0=m0, m1=m1: e.tensor_copy(
                        out=wd[j][:, 0:(m1-m0)*128], in_=wdi[j][:, 0:(m1-m0)*128]),
                        reads=(f"wdi{j}",), writes=(f"wd{j}",))
                    for mi in range(nmi):
                        for half in range(2):
                            ps = PS[mi * 2 + half]
                            B.step("tensor", lambda e, j=j, k=k, mi=mi, half=half, ps=ps: e.matmul(
                                out=ps[:, :], lhsT=wd[j][:, mi*128:(mi+1)*128],
                                rhs=hb[k][:, half*512:half*512+512],
                                start=(k == 0), stop=(k == 23)),
                                reads=(f"wd{j}", f"hb{k}"), writes=(f"bank{mi*2+half}",))
                for mi in range(nmi):
                    m = m0 + mi
                    for half in range(2):
                        ps = PS[mi * 2 + half]
                        wmark = ("g2t", "downdone") if (m0 == 6 and mi == nmi - 1 and half == 1) else ("g2t",)
                        B.step("scalar", lambda e, ps=ps, m=m, half=half: e.activation(
                            out=g2t[:, m*1024 + half*512:m*1024 + half*512 + 512],
                            in_=ps[:, :], func=AF.Silu, scale=SCL),
                            reads=(f"bank{mi*2+half}",), writes=wmark)
            pass
            B.step("sync", lambda e, jx=jx, t0=t0: e.dma_start(
                out=s2l.rearrange("p (k n) -> p k n", n=1024),
                in_=d["s2_fm"].rearrange("(k p) t -> p k t", p=128)[:, :, t0:t0 + 1024]), writes=("s2l",))
            for k in range(KC):
                B.step("vector", lambda e, k=k: e.tensor_mul(
                    out=prod[:, k*1024:(k+1)*1024], in0=s2l[:, k*1024:(k+1)*1024],
                    in1=g2t[:, k*1024:(k+1)*1024]), reads=("s2l", "g2t"), writes=("prod",))
            for sub in range(8):
                j = sub % 2
                r0 = t0 + sub * 128
                B.step("sync", lambda e, j=j, r0=r0: e.dma_start(out=x1l[j][:, :], in_=d["x1_tm"][r0:r0+128, :]),
                       writes=(f"x1l{j}",))
                for k in range(KC):
                    pj = (sub * KC + k) % 2
                    B.step("tensor", lambda e, k=k, sub=sub, pj=pj: e.transpose(
                        out=ptr[pj][:, :], in_=prod[:, k*1024 + sub*128:k*1024 + sub*128 + 128],
                        identity=ident[:, :]), reads=("prod",), writes=(f"ptr{pj}",))
                    B.step("vector", lambda e, j=j, k=k, pj=pj: e.tensor_add(
                        out=yt[j][:, k*128:(k+1)*128], in0=ptr[pj][:, :], in1=x1l[j][:, k*128:(k+1)*128]),
                        reads=(f"ptr{pj}", f"x1l{j}"), writes=(f"yt{j}",))
                B.step("sync", lambda e, j=j, r0=r0: e.dma_start(out=d["y"][r0:r0+128, :], in_=yt[j][:, :]),
                       reads=(f"yt{j}",), writes=(f"yst{j}",))
        S.emit()


def _run(phase_groups=None):
    pass

def kernel(**inputs):
    import os
    import numpy as np
    from concourse.bass_utils import run_bass_kernel_spmd
    groups = os.environ.get("KB_GROUPS", "ABCDEFGH").split(",")
    consts = host_consts()
    vecs = host_vecs(inputs)
    weights = host_weights(inputs)
    x = np.asarray(inputs["L0_attn_x"], np.float32)
    ncores = x.shape[0]
    common = dict(consts)
    common["vecs"] = vecs
    common.update(weights)
    state = [dict() for _ in range(ncores)]
    if int(os.environ.get("KB_WARM_LAUNCH", "0")):
        import sys
        import concourse.bass as _bass
        def _build_warm():
            nc = _bass.Bass("TRN2", target_bir_lowering=False)
            xw = nc.dram_tensor("xw", [128, 512], F32, kind="ExternalInput").ap()
            yw = nc.dram_tensor("yw", [128, 512], F32, kind="ExternalOutput").ap()
            with (nc.sbuf_tensor("w_x", [128, 512], F32) as w_x,
                  nc.sbuf_tensor("w_b", [128, 512], BF16) as w_b,
                  nc.sbuf_tensor("w_y", [128, 512], F32) as w_y,
                  nc.psum_tensor("w_ps", [128, 512], F32) as w_ps,
                  nc.semaphore("w_ld") as w_ld,
                  nc.semaphore("w_mm") as w_mm,
                  nc.semaphore("w_cp") as w_cp,
                  nc.semaphore("w_st") as w_st,
                  nc.Block() as blkw):
                @blkw.sync
                def _(sync):
                    sync.dma_start(out=w_x[:, :], in_=xw).then_inc(w_ld, 16)
                    sync.wait_ge(w_cp, 3)
                    sync.dma_start(out=yw, in_=w_y[:, :]).then_inc(w_st, 16)
                    sync.wait_ge(w_st, 16)
                @blkw.vector
                def _(v):
                    v.wait_ge(w_ld, 16)
                    v.tensor_copy(out=w_b[:, :], in_=w_x[:, :])
                    v.tensor_scalar(out=w_b[:, :], in0=w_b[:, :], scalar1=w_x[:, 0:1],
                                    scalar2=w_x[:, 1:2], op0=ALU.mult, op1=ALU.add).then_inc(w_cp, 1)
                    v.wait_ge(w_mm, 1)
                    v.tensor_add(out=w_y[:, :], in0=w_ps[:, :], in1=w_x[:, :]).then_inc(w_cp, 1)
                @blkw.scalar
                def _(s):
                    s.wait_ge(w_ld, 16)
                    for f in (AF.Sigmoid, AF.Silu, AF.Exp, AF.Ln, AF.Gelu, AF.Sqrt, AF.Square):
                        s.activation(out=w_y[:, 0:4], in_=w_x[:, 0:4], func=f)
                    s.nop().then_inc(w_cp, 1)
                @blkw.tensor
                def _(t):
                    t.wait_ge(w_cp, 1)
                    t.matmul(out=w_ps[:, :], lhsT=w_b[:, 0:128], rhs=w_b[:, :], start=True, stop=True).then_inc(w_mm, 1)
                @blkw.gpsimd
                def _(g):
                    g.wait_ge(w_ld, 16)
                    g.tensor_copy(out=w_b[:, 0:64], in_=w_b[:, 64:128])
            return nc
        xw = np.zeros((128, 512), np.float32) + 1.0
        for attempt in range(int(os.environ.get("KB_WARM_RUNS", "3"))):
            try:
                ncw = _build_warm()
                run_bass_kernel_spmd(ncw, [{"xw": xw}] * ncores, core_ids=list(range(ncores)))
                print(f"[kernel] warm {attempt} ok", file=sys.stderr, flush=True)
            except Exception as e:
                print(f"[kernel] warm {attempt} raised {type(e).__name__}", file=sys.stderr, flush=True)
    import sys, time as _t
    import ml_dtypes as _mld
    _np_dt = {BF16: _mld.bfloat16, F32: np.float32, I32: np.int32}
    for g in groups:
        print(f"[kernel] build {g} {_t.strftime('%H:%M:%S')}", file=sys.stderr, flush=True)
        nc_g = build_nc(g, standalone=True)
        names_g = list(LAST_INPUTS)
        in_maps = []
        for b in range(ncores):
            feed = {}
            for name in names_g:
                if name == "x":
                    feed[name] = np.ascontiguousarray(x[b])
                elif name in common:
                    feed[name] = common[name]
                else:
                    feed[name] = state[b][name]
            in_maps.append(feed)
        print(f"[kernel] group {g} run {_t.strftime('%H:%M:%S')}", file=sys.stderr, flush=True)
        if len(groups) > 1:
            try:
                run_bass_kernel_spmd(nc_g, in_maps, core_ids=list(range(ncores)))
            except Exception as e:
                print(f"[kernel] group {g} first run raised {type(e).__name__}", file=sys.stderr, flush=True)
        res = run_bass_kernel_spmd(nc_g, in_maps, core_ids=list(range(ncores)))
        for b in range(ncores):
            state[b].update(res.results[b])
    return np.stack([state[b]["y"] for b in range(ncores)], axis=0).astype(np.float32)
